# version 12
# baseline (speedup 1.0000x reference)
import numpy as np
from contextlib import ExitStack

import concourse.bass as bass
import concourse.mybir as mybir
from concourse.ap import AP
from concourse.bass_utils import run_bass_kernel_spmd

F32 = mybir.dt.float32
BF16 = mybir.dt.bfloat16
I32 = mybir.dt.int32
AF = mybir.ActivationFunctionType
ALU = mybir.AluOpType

SEQ = 2048
DM = 1024
DKV = 256
NH, NKV, DH = 16, 4, 64
FH = 3072
KC = DM // 128
HC = FH // 128
NSC = SEQ // 128
EPS = 1e-5
ATT_SCALE = 0.125
DQS = 1.0 / 127.0

A_HEADS = [0, 1, 2, 3, 8, 9, 10, 11]
B_HEADS = [4, 5, 6, 7, 12, 13, 14, 15]
HEAD_POS = {}
for _c in range(8):
    HEAD_POS[A_HEADS[_c]] = (_c, 0)
    HEAD_POS[B_HEADS[_c]] = (_c, 1)
PERM_F = np.zeros(DM, np.int64)
for _h, (_c, _p) in HEAD_POS.items():
    PERM_F[_c * 128 + _p * 64: _c * 128 + _p * 64 + 64] = np.arange(64) + _h * 64

VT_W = 448
VT_STAT_OFF = {0: 128, 1: 0, 2: 320, 3: 192}
VT_KV_ORDER = [1, 0, 3, 2]


class Rsrc:
    __slots__ = ("name", "w", "r", "dirty_by")

    def __init__(self, name):
        self.name = name
        self.w = []
        self.r = []
        self.dirty_by = None


class Prog:

    serialize = False
    ser_phases = set()
    cur_phase = ""
    trace_log = None

    def __init__(self, nc, sems):
        self.nc = nc
        self.stream = {e: [] for e in ("t", "v", "s", "g", "q")}
        self.sems = sems
        self.count = {k: 0 for k in sems}
        self.floor = {e: {} for e in self.stream}
        self.pending = {e: ([], []) for e in self.stream}
        self.last_op_idx = {e: -1 for e in self.stream}
        self.evclock = {}
        self.eng_sem = {"t": "ts", "v": "vs", "s": "ss"}
        self.dma_sem = {"q": ["dq0", "dq1"], "g": ["dg0", "dg1"]}
        self.dma_rr = {"q": 0, "g": 0}

    def _wait(self, e, ev):
        sem, val = ev
        if self.floor[e].get(sem, 0) >= val:
            return
        assert val <= self.count[sem], (e, sem, val, self.count[sem])
        h = self.sems[sem]
        self.stream[e].append(lambda eng, h=h, v=val: eng.wait_ge(h, v))
        if Prog.trace_log is not None:
            Prog.trace_log.append(("W", e, sem, val, Prog.cur_phase))
        clk = self.evclock.get((sem, val), {})
        f = self.floor[e]
        for s2, v2 in clk.items():
            if f.get(s2, 0) < v2:
                f[s2] = v2
        if f.get(sem, 0) < val:
            f[sem] = val

    def _ser(self, e):
        if not (Prog.serialize or Prog.cur_phase in Prog.ser_phases):
            return
        for sem, cnt in self.count.items():
            if cnt:
                self._wait(e, (sem, cnt))

    def raw(self, e, fn):
        self.stream[e].append(fn)
        self.last_op_idx[e] = len(self.stream[e]) - 1

    def raw_inc(self, e):
        sem = self.eng_sem[e]
        self.count[sem] += 1
        val = self.count[sem]
        h = self.sems[sem]
        i = self.last_op_idx[e]
        assert i == len(self.stream[e]) - 1, (e, i, len(self.stream[e]))
        old_fn = self.stream[e][i]
        self.stream[e][i] = (
            lambda eng, f=old_fn, h=h: f(eng).then_inc(h, 1))
        if Prog.trace_log is not None:
            Prog.trace_log.append(("i", e, sem, val, Prog.cur_phase))
        f = dict(self.floor[e])
        f[sem] = val
        self.evclock[(sem, val)] = f
        return (sem, val)

    def op(self, e, fn, reads=(), writes=()):
        self._ser(e)
        for R in reads:
            assert R.dirty_by in (None, e), f"read of dirty {R.name} on {e}"
            for ev in R.w:
                self._wait(e, ev)
        for R in writes:
            assert R.dirty_by in (None, e), f"write of dirty {R.name} on {e}"
            for ev in R.w:
                self._wait(e, ev)
            for ev in R.r:
                self._wait(e, ev)
        self.stream[e].append(fn)
        self.last_op_idx[e] = len(self.stream[e]) - 1
        pr, pw = self.pending[e]
        pr.extend(reads)
        pw.extend(writes)
        for R in writes:
            R.dirty_by = e
        for R in reads:
            if R.dirty_by is None:
                R.dirty_by = e

    def inc(self, e, drain=None):
        if drain is None:
            drain = (e == "t")
        sem = self.eng_sem[e]
        self.count[sem] += 1
        val = self.count[sem]
        h = self.sems[sem]
        if drain:
            self.stream[e].append(
                lambda eng, h=h: eng.maybe_drain_then_inc((h, 1), fusable=True))
        else:
            i = self.last_op_idx[e]
            assert i == len(self.stream[e]) - 1, (e, i, len(self.stream[e]))
            old_fn = self.stream[e][i]
            self.stream[e][i] = (
                lambda eng, f=old_fn, h=h: f(eng).then_inc(h, 1))
        if Prog.trace_log is not None:
            Prog.trace_log.append(("I", e, sem, val, Prog.cur_phase))
        f = dict(self.floor[e])
        f[sem] = val
        self.evclock[(sem, val)] = f
        ev = (sem, val)
        pr, pw = self.pending[e]
        for R in pw:
            R.w = [ev]
            R.r = []
            R.dirty_by = None
        for R in pr:
            if R.dirty_by == e:
                R.dirty_by = None
            R.r.append(ev)
        self.pending[e] = ([], [])
        return ev

    def dma(self, e, out_ap, in_ap, reads=(), writes=()):
        self._ser(e)
        for R in reads:
            assert R.dirty_by in (None, e)
            for ev in R.w:
                self._wait(e, ev)
        for R in writes:
            assert R.dirty_by in (None, e)
            for ev in R.w + R.r:
                self._wait(e, ev)
        sem = self.dma_sem[e][self.dma_rr[e]]
        self.dma_rr[e] ^= 1
        h = self.sems[sem]
        self.count[sem] += 16
        val = self.count[sem]
        self.stream[e].append(
            lambda eng, h=h, o=out_ap, i=in_ap: eng.dma_start(o, i).then_inc(h, 16))
        if Prog.trace_log is not None:
            Prog.trace_log.append(("D", e, sem, val, Prog.cur_phase))
        f = dict(self.floor[e])
        f[sem] = val
        self.evclock[(sem, val)] = f
        ev = (sem, val)
        for R in writes:
            R.w = [ev]
            R.r = []
        for R in reads:
            R.r.append(ev)
        return ev


def build_program(taps=(), serialize=False, ser_phases=()):
    taps = set(taps)
    Prog.serialize = serialize
    Prog.ser_phases = set(ser_phases)
    Prog.cur_phase = "pre"
    nc = bass.Bass("TRN2", target_bir_lowering=False,
                   dynamic_dma_scratch_size=16384,
                   detect_race_conditions=False)

    def din(name, shape, dtype):
        return nc.dram_tensor(name, list(shape), dtype, kind="ExternalInput").ap()

    xT = din("xT", (DM, SEQ), F32)
    wq_a = din("wq_a", (128, KC * DM), F32)
    wk_a = din("wk_a", (128, KC * DKV), F32)
    wv_a = din("wv_a", (128, KC * DKV), F32)
    wo_a = din("wo_a", (128, KC * DM), F32)
    bq_r = din("bq_r", (1, DM), F32)
    bk_r = din("bk_r", (1, DKV), F32)
    bv_r = din("bv_r", (1, DKV), F32)
    bo_r = din("bo_r", (1, DM), F32)
    wg_a = din("wg_a", (HC, 128, DM), I32)
    wu_a = din("wu_a", (HC, 128, DM), I32)
    wd_a = din("wd_a", (KC, 128, FH), I32)
    yT = nc.dram_tensor("yT", [DM, SEQ], F32, kind="ExternalOutput").ap()

    tap_aps = {}
    for tname in taps:
        shp = {"h1": (DM, SEQ), "q": (DM, SEQ), "k": (DKV, SEQ),
               "vt": (128, NSC * VT_W), "ao": (DM, SEQ), "x1": (DM, SEQ),
               "h2": (DM, SEQ)}[tname]
        dt = F32 if tname == "x1" else BF16
        tap_aps[tname] = nc.dram_tensor("tap_" + tname, list(shp), dt,
                                        kind="ExternalOutput").ap()

    with ExitStack() as ctx:
        base = (nc.sbuf_base + 31) & ~31
        cur = [base]

        offs = {}

        def place(name, shape, dtype, at=None):
            if at is None:
                off = cur[0]
                per = int(np.prod(shape[1:])) * mybir.dt.size(dtype)
                cur[0] = (off + per + 31) & ~31
            else:
                off = at
            h = nc.alloc_sbuf_tensor_at(name, list(shape), dtype, offset=off)
            offs[name] = off
            assert cur[0] <= nc.sbuf_top, (name, cur[0], nc.sbuf_top)
            return h

        x_sb = place("x_sb", [128, KC * SEQ], F32)
        B0 = cur[0]
        h_sb = place("h_sb", [128, KC * SEQ], BF16)
        k_sb = place("k_sb", [128, 2 * SEQ], BF16)
        vt_sb = place("vt_sb", [128, NSC * VT_W], BF16)
        probs_sb = place("probs_sb", [128, 4 * 1024], BF16)
        hf_sb = place("hf_sb", [128, HC * 1024], BF16, at=B0)
        q_sb = place("q_sb", [128, KC * SEQ], BF16)
        W0 = cur[0]
        wq_bf = place("wq_bf", [128, KC * DM], BF16)
        wk_bf = place("wk_bf", [128, KC * DKV], BF16)
        wv_bf = place("wv_bf", [128, KC * DKV], BF16)
        wg_buf = place("wg_buf", [128, 2 * DM], BF16, at=W0)
        wu_buf = place("wu_buf", [128, 2 * DM], BF16, at=W0 + 4096)
        sg_buf = place("sg_buf", [128, 2 * 1024], BF16, at=W0 + 8192)
        wd_buf = place("wd_buf", [128, 2 * FH], BF16, at=W0 + 12288)
        stg32 = place("stg32", [128, 2048], F32)
        stgi = place("stgi", [128, 2048], I32, at=offs["stg32"])
        stgi2 = place("stgi2", [128, 2048], I32, at=W0 + 12288)
        ones_row = place("ones_row", [128, 128], BF16)
        ones_col = place("ones_col", [128, 1], BF16)
        ones_f32 = place("ones_f32", [1, 1], F32)
        eps_c = place("eps_c", [1, 1], F32)
        rowA = place("rowA", [1, 1024], F32)
        rowB = place("rowB", [1, 1024], F32)
        bv_row = place("bv_row", [1, DKV], F32)
        bv_rbf = place("bv_rbf", [1, DKV], BF16)
        bv_bs = place("bv_bs", [128, DKV], BF16)
        m_bf = place("m_bf", [1, SEQ], BF16, at=offs["k_sb"] + 4096)
        rstd_bf = place("rstd_bf", [1, SEQ], BF16, at=offs["vt_sb"] + 8192)
        av_f32 = place("av_f32", [128, 1024], F32, at=offs["rowA"])
        att_rbf = place("att_rbf", [128, 1024], BF16, at=offs["rowB"])
        rb_bf = place("rb_bf", [128, 1024], BF16, at=offs["rowB"] + 2048)
        bias_sb = place("bias_sb", [128, 18], F32)

        PA = ctx.enter_context(nc.psum_tensor("PA", [128, 1024], F32))
        PB = ctx.enter_context(nc.psum_tensor("PB", [128, 1024], F32))
        PC = ctx.enter_context(nc.psum_tensor("PC", [128, 1024], F32))
        PD = ctx.enter_context(nc.psum_tensor("PD", [128, 1024], F32))

        sems = {}
        for sname in ("ts", "vs", "ss", "dq0", "dq1", "dg0", "dg1"):
            sems[sname] = ctx.enter_context(nc.semaphore(sname))
        p = Prog(nc, sems)

        r_x = Rsrc("x")
        r_h = Rsrc("h")
        r_ao = Rsrc("ao")
        r_q = Rsrc("q")
        r_h2 = Rsrc("h2")
        r_k = Rsrc("k")
        r_vt = Rsrc("vt")
        r_probs = [Rsrc(f"probs{i}") for i in range(4)]
        r_hf = Rsrc("hf")
        r_h2h = [Rsrc("h2h0"), Rsrc("h2h1")]
        r_PA, r_PB = Rsrc("PA"), Rsrc("PB")
        r_PC, r_PD = Rsrc("PC"), Rsrc("PD")
        r_wq, r_wk, r_wv, r_wo = Rsrc("wq"), Rsrc("wk"), Rsrc("wv"), Rsrc("wo")
        r_wg = [Rsrc("wg0"), Rsrc("wg1")]
        r_wu = [Rsrc("wu0"), Rsrc("wu1")]
        r_sg = [Rsrc("sg0"), Rsrc("sg1")]
        r_wd = [Rsrc("wd0"), Rsrc("wd1")]
        r_stg = [Rsrc("stg0"), Rsrc("stg1")]
        r_stgi = r_stg
        r_stg4 = [r_stg[0], r_stg[1], Rsrc("stg2"), Rsrc("stg3")]
        r_rowA, r_rowB = Rsrc("rowA"), Rsrc("rowB")
        r_mrow, r_rrow = Rsrc("m_bf"), Rsrc("rstd_bf")
        r_bias, r_bv, r_bvbs, r_rb = (Rsrc("bias"), Rsrc("bv"),
                                      Rsrc("bv_bs"), Rsrc("rb"))
        r_const = Rsrc("const")
        r_xtmp, r_sqtmp, r_t1tmp = Rsrc("xtmp"), Rsrc("sqtmp"), Rsrc("t1tmp")
        PS = {id(PA): r_PA, id(PB): r_PB, id(PC): r_PC, id(PD): r_PD}
        RS_MBS = [r_probs[0], r_probs[1]]
        RS_RBS = [r_probs[2], r_probs[3]]
        m_bs = probs_sb[:, 0:2048]
        rs_bs = probs_sb[:, 2048:4096]

        p.op("v", lambda e: e.memset(ones_row[:], 1.0), writes=[r_const])
        p.op("v", lambda e: e.memset(ones_col[:], 1.0), writes=[r_const])
        p.op("v", lambda e: e.memset(ones_f32[:], 1.0), writes=[r_const])
        p.op("v", lambda e: e.memset(att_rbf[:], 0.0), writes=[r_rowB])
        p.op("v", lambda e: e.memset(eps_c[:], float(EPS)), writes=[r_const])
        p.inc("v")

        xevs = []
        for c in range(KC):
            xevs.append(p.dma("g", x_sb[:, c * SEQ:(c + 1) * SEQ],
                              xT[c * 128:(c + 1) * 128, :]))
        r_x.w = xevs

        def bias_block(row_src, n_chunks, col0):
            p.dma("q", rowA[0:1, 0:row_src.shape[1]], row_src, writes=[r_rowA])
            for j in range(n_chunks):
                p.op("t", lambda e, j=j, c0=col0: e.matmul(
                    PA[:, c0 + j:c0 + j + 1],
                    rowA[0:1, j * 128:(j + 1) * 128],
                    ones_f32[:], start=True, stop=True),
                    reads=[r_rowA, r_const], writes=[r_PA])
            p.inc("t")
        bias_block(bq_r, 8, 0)
        bias_block(bk_r, 2, 8)
        bias_block(bo_r, 8, 10)
        p.op("s", lambda e: e.copy(bias_sb[:], PA[:, 0:18]),
             reads=[r_PA], writes=[r_bias])
        p.inc("s")

        p.dma("q", bv_row[:], bv_r, writes=[r_bv])
        p.op("v", lambda e: e.tensor_copy(bv_rbf[:], bv_row[:]),
             reads=[r_bv], writes=[r_bv])
        p.inc("v")
        p.op("t", lambda e: e.matmul(PB[:, 0:DKV], ones_row[0:1, 0:128],
                                     bv_rbf[:], start=True, stop=True),
             reads=[r_bv, r_const], writes=[r_PB])
        p.inc("t")
        p.op("s", lambda e: e.copy(bv_bs[:], PB[:, 0:DKV]),
             reads=[r_PB], writes=[r_bvbs])
        p.inc("s")

        def stream32(src_ap, dst_ap, rsrcs, half):
            st = stg32[:, half * 1024:(half + 1) * 1024]
            p.dma("q", st, src_ap, writes=[r_stg[half]])
            p.op("v", lambda e, st=st, d=dst_ap: e.tensor_copy(d, st),
                 reads=[r_stg[half]], writes=list(rsrcs))
            p.inc("v")

        def streami(src_ap, dst_ap, rsrcs, slot, queue="q", extra_dma=()):
            t2 = (stgi, stgi, stgi2, stgi2)[slot]
            st = t2[:, (slot % 2) * 1024:(slot % 2 + 1) * 1024]
            p.dma(queue, st, src_ap,
                  writes=[r_stg4[slot]] + list(extra_dma))
            p.op("v", lambda e, st=st, d=dst_ap: e.tensor_copy(d, st),
                 reads=[r_stg4[slot]], writes=rsrcs)
            p.inc("v")

        for i in range(2):
            stream32(wk_a[:, i * 1024:(i + 1) * 1024],
                     wk_bf[:, i * 1024:(i + 1) * 1024], [r_wk], i % 2)
        for i in range(2):
            stream32(wv_a[:, i * 1024:(i + 1) * 1024],
                     wv_bf[:, i * 1024:(i + 1) * 1024], [r_wv], i % 2)
        for i in range(8):
            stream32(wq_a[:, i * 1024:(i + 1) * 1024],
                     wq_bf[:, i * 1024:(i + 1) * 1024], [r_wq], i % 2)

        def ln_stats(src_res, Hh, sum_ps, sq_ps, extra_scr=()):
            xb = k_sb[:, 0:1024]
            sq = vt_sb[:, 0:1024]
            t1 = vt_sb[:, 2048:3072]
            rsum, rsq = PS[id(sum_ps)], PS[id(sq_ps)]
            w_xb = [r_xtmp] + [r for r in extra_scr if r.name == "k"]
            w_sq = [r_sqtmp] + [r for r in extra_scr if r.name == "vt"]
            w_t1 = [r_t1tmp] + [r for r in extra_scr if r.name == "vt"]
            for c in range(KC):
                xs = x_sb[:, c * SEQ + Hh * 1024: c * SEQ + Hh * 1024 + 1024]
                p.op("v", lambda e, xs=xs: e.tensor_copy(xb, xs),
                     reads=[src_res], writes=w_xb)
                p.inc("v")
                p.op("s", lambda e, xs=xs: e.activation(sq, xs, AF.Square),
                     reads=[src_res], writes=w_sq)
                p.inc("s")
                for piece in range(2):
                    off = piece * 512
                    p.op("t", lambda e, off=off, c=c: e.matmul(
                        sum_ps[0:1, off:off + 512], ones_col[:],
                        xb[:, off:off + 512],
                        start=(c == 0), stop=(c == KC - 1)),
                        reads=[r_xtmp, r_const], writes=[rsum])
                for piece in range(2):
                    off = piece * 512
                    p.op("t", lambda e, off=off, c=c: e.matmul(
                        sq_ps[0:1, off:off + 512], ones_col[:],
                        sq[:, off:off + 512],
                        start=(c == 0), stop=(c == KC - 1)),
                        reads=[r_sqtmp, r_const], writes=[rsq])
                p.inc("t")

        def ln_finish(src_res, dst_sb, dst_out_res, Hh, sum_ps, sq_ps,
                      extra_scr=()):
            t1 = vt_sb[:, 2048:3072]
            rsum, rsq = PS[id(sum_ps)], PS[id(sq_ps)]
            w_t1 = [r_t1tmp] + [r for r in extra_scr if r.name == "vt"]
            p.op("v", lambda e: e.tensor_scalar(
                rowA[:], sum_ps[0:1, :], 1.0 / DM, None, ALU.mult),
                reads=[rsum], writes=[r_rowA])
            p.op("v", lambda e: e.tensor_mul(rowB[:], rowA[:], rowA[:]),
                 reads=[r_rowA], writes=[r_rowB])
            p.inc("v")
            p.op("s", lambda e, Hh=Hh: e.copy(
                m_bf[0:1, Hh * 1024:(Hh + 1) * 1024], rowA[:]),
                reads=[r_rowA], writes=[r_mrow, r_k])
            p.inc("s")
            p.op("v", lambda e: e.scalar_tensor_tensor(
                rowA[:], sq_ps[0:1, :], 1.0 / DM, rowB[:],
                ALU.mult, ALU.subtract),
                reads=[rsq, r_rowB], writes=[r_rowA])
            p.inc("v")
            p.op("s", lambda e: e.activation(rowB[:], rowA[:], AF.Ln,
                                             bias=eps_c[:]),
                 reads=[r_rowA, r_const], writes=[r_rowB])
            p.op("s", lambda e, Hh=Hh: e.activation(
                rstd_bf[0:1, Hh * 1024:(Hh + 1) * 1024], rowB[:],
                AF.Exp, scale=-0.5),
                reads=[r_rowB], writes=[r_rrow, r_vt])
            p.inc("s")
            for piece in range(2):
                off = piece * 512
                p.op("t", lambda e, off=off, Hh=Hh: e.matmul(
                    sum_ps[:, off:off + 512], ones_row[0:1, 0:128],
                    m_bf[0:1, Hh * 1024 + off: Hh * 1024 + off + 512],
                    start=True, stop=True),
                    reads=[r_mrow, r_k, r_const], writes=[rsum])
                p.op("t", lambda e, off=off, Hh=Hh: e.matmul(
                    sq_ps[:, off:off + 512], ones_row[0:1, 0:128],
                    rstd_bf[0:1, Hh * 1024 + off: Hh * 1024 + off + 512],
                    start=True, stop=True),
                    reads=[r_rrow, r_vt, r_const], writes=[rsq])
            p.inc("t")
            mb_s = probs_sb[:, 0:1024]
            rb_s = probs_sb[:, 2048:3072]
            p.op("s", lambda e: e.copy(mb_s[:], sum_ps[:]),
                 reads=[rsum], writes=[r_probs[0]])
            p.op("s", lambda e: e.copy(rb_s[:], sq_ps[:]),
                 reads=[rsq], writes=[r_probs[2]])
            p.inc("s")
            for c in range(KC):
                xs = x_sb[:, c * SEQ + Hh * 1024: c * SEQ + Hh * 1024 + 1024]
                p.op("v", lambda e, xs=xs: e.tensor_sub(t1, xs, mb_s),
                     reads=[src_res, r_probs[0]], writes=w_t1)
                p.op("v", lambda e, c=c, Hh=Hh: e.tensor_mul(
                    dst_sb[:, c * SEQ + Hh * 1024: c * SEQ + Hh * 1024 + 1024],
                    t1, rb_s),
                    reads=[r_t1tmp, r_probs[2]], writes=dst_out_res)
                p.inc("v")

        def layernorm(src_res, dst_sb, dst_out_res, extra_scr=()):
            ln_stats(src_res, 0, PA, PC, extra_scr)
            ln_finish(src_res, dst_sb, dst_out_res, 0, PA, PC, extra_scr)
            ln_stats(src_res, 1, PB, PD, extra_scr)
            ln_finish(src_res, dst_sb, dst_out_res, 1, PB, PD, extra_scr)

        layernorm(r_x, h_sb, [r_h])
        Prog.cur_phase = "qproj"

        for mc in range(KC):
            ps_pair = (PA, PB) if mc % 2 == 0 else (PC, PD)
            rs_pair = [PS[id(ps_pair[0])], PS[id(ps_pair[1])]]
            for kc in range(KC):
                for nf in range(4):
                    ps = ps_pair[nf // 2]
                    off = (nf % 2) * 512
                    p.op("t", lambda e, ps=ps, off=off, kc=kc, mc=mc, nf=nf:
                         e.matmul(
                             ps[:, off:off + 512],
                             wq_bf[:, kc * 1024 + mc * 128:
                                   kc * 1024 + mc * 128 + 128],
                             h_sb[:, kc * SEQ + nf * 512:
                                  kc * SEQ + nf * 512 + 512],
                             start=(kc == 0), stop=(kc == KC - 1)),
                         reads=[r_wq, r_h], writes=rs_pair)
            p.inc("t")
            for half in range(2):
                ps = ps_pair[half]
                p.op("s", lambda e, ps=ps, mc=mc, half=half: e.activation(
                    q_sb[:, mc * SEQ + half * 1024:
                         mc * SEQ + half * 1024 + 1024],
                    ps[:], AF.Identity, bias=bias_sb[:, mc:mc + 1]),
                    reads=[rs_pair[half], r_bias], writes=[r_q])
            p.inc("s")

        Prog.cur_phase = "kproj"
        for mc2 in range(2):
            ps_pair = (PA, PB) if mc2 % 2 == 0 else (PC, PD)
            rs_pair = [PS[id(ps_pair[0])], PS[id(ps_pair[1])]]
            for kc in range(KC):
                for nf in range(4):
                    ps = ps_pair[nf // 2]
                    off = (nf % 2) * 512
                    p.op("t", lambda e, ps=ps, off=off, kc=kc, mc2=mc2, nf=nf:
                         e.matmul(
                             ps[:, off:off + 512],
                             wk_bf[:, kc * 256 + mc2 * 128:
                                   kc * 256 + mc2 * 128 + 128],
                             h_sb[:, kc * SEQ + nf * 512:
                                  kc * SEQ + nf * 512 + 512],
                             start=(kc == 0), stop=(kc == KC - 1)),
                         reads=[r_wk, r_h], writes=rs_pair)
            p.inc("t")
            for half in range(2):
                ps = ps_pair[half]
                p.op("s", lambda e, ps=ps, mc2=mc2, half=half: e.activation(
                    k_sb[:, mc2 * SEQ + half * 1024:
                         mc2 * SEQ + half * 1024 + 1024],
                    ps[:], AF.Identity, bias=bias_sb[:, 8 + mc2:9 + mc2]),
                    reads=[rs_pair[half], r_bias], writes=[r_k])
            p.inc("s")

        Prog.cur_phase = "vtok"
        def z_init(e):
            e.memset(AP(vt_sb, 0, [[NSC * VT_W, 128], [VT_W, NSC],
                                   [192, 3], [1, 64]]), 0.0)
        p.op("v", z_init, writes=[r_vt, r_sqtmp, r_t1tmp])
        p.op("v", lambda e: e.memset(
            AP(vt_sb, 0, [[NSC * VT_W, 128], [VT_W, NSC], [192, 3]]), 1.0),
            writes=[r_vt])
        p.inc("v")

        for sc in range(NSC):
            ps = (PA, PB)[sc % 2]
            rp = PS[id(ps)]
            for kc in range(KC):
                p.op("t", lambda e, ps=ps, sc=sc, kc=kc: e.matmul(
                    ps[:, 0:DKV],
                    h_sb[:, kc * SEQ + sc * 128: kc * SEQ + sc * 128 + 128],
                    wv_bf[:, kc * 256:(kc + 1) * 256],
                    start=(kc == 0), stop=(kc == KC - 1)),
                    reads=[r_h, r_wv], writes=[rp])
            p.inc("t")
            p.op("v", lambda e, ps=ps, sc=sc: e.tensor_add(
                AP(vt_sb, sc * VT_W + 64, [[NSC * VT_W, 128], [192, 2], [1, 128]]),
                ps[:, 0:DKV].rearrange("p (s c) -> p s c", s=2),
                bv_bs[:].rearrange("p (s c) -> p s c", s=2)),
                reads=[rp, r_bvbs], writes=[r_vt])
            p.inc("v")

        if "h1" in tap_aps:
            for c in range(KC):
                p.dma("g", tap_aps["h1"][c * 128:(c + 1) * 128, :],
                      h_sb[:, c * SEQ:(c + 1) * SEQ], reads=[r_h])
        if "q" in tap_aps:
            for c in range(KC):
                p.dma("g", tap_aps["q"][c * 128:(c + 1) * 128, :],
                      q_sb[:, c * SEQ:(c + 1) * SEQ], reads=[r_q])
        if "k" in tap_aps:
            for c in range(2):
                p.dma("g", tap_aps["k"][c * 128:(c + 1) * 128, :],
                      k_sb[:, c * SEQ:(c + 1) * SEQ], reads=[r_k])
        if "vt" in tap_aps:
            p.dma("g", tap_aps["vt"], vt_sb[:], reads=[r_vt])

        for i in range(8):
            stream32(wo_a[:, i * 1024:(i + 1) * 1024],
                     wq_bf[:, i * 1024:(i + 1) * 1024],
                     [r_wo, r_wq] if i < 2 else [r_wo], i % 2)

        Prog.cur_phase = "att"
        ev_exp_tail = None
        ev_norm_prev = None
        ev_qk = {}
        for R in (r_PA, r_PB, r_PC, r_PD):
            for ev in R.w + R.r:
                p._wait("t", ev)
        for R in (r_k, r_q, r_vt):
            for ev in R.w:
                p._wait("t", ev)
        for ev in (r_rowA.w + r_rowA.r + r_rowB.w + r_rowB.r +
                   r_ao.w + r_ao.r + r_h.r + r_h.w):
            p._wait("v", ev)

        def ring_of(si):
            return (PA, PB, PD) if si % 2 == 0 else (PA, PB, PC)

        def avbuf_of(si):
            return PC if si % 2 == 0 else PD

        def t_drain_inc():
            val = p.count["ts"] + 1
            p.count["ts"] = val
            p.stream["t"].append(
                lambda eng, h2=p.sems["ts"]: eng.maybe_drain_then_inc(
                    (h2, 1), fusable=True))
            fcl = dict(p.floor["t"])
            fcl["ts"] = val
            p.evclock[("ts", val)] = fcl
            return ("ts", val)

        slots = [(h, qh) for h in range(NH) for qh in range(2)]

        def emit_qk(slot_idx, g):
            h, qh = slots[slot_idx]
            cq, par = HEAD_POS[h]
            kc2 = (h // 4) // 2
            ps = ring_of(slot_idx)[g % 3]
            for nf in range(2):
                p.raw("t", lambda e, ps=ps, sc=g, nf=nf, cq=cq, par=par,
                      kc2=kc2, qh=qh: e.matmul(
                          ps[:, nf * 512:(nf + 1) * 512],
                          k_sb[par * 64: par * 64 + 64,
                               kc2 * SEQ + sc * 128: kc2 * SEQ + sc * 128 + 128],
                          q_sb[par * 64: par * 64 + 64,
                               cq * SEQ + qh * 1024 + nf * 512:
                               cq * SEQ + qh * 1024 + nf * 512 + 512],
                          start=True, stop=True))
            ev_qk[(slot_idx, g)] = p.raw_inc("t")

        ev_ave_par = {0: None, 1: None}
        ev_norm_final = None
        for si, (h, qh) in enumerate(slots):
            cq, par = HEAD_POS[h]
            kv = h // 4
            so = VT_STAT_OFF[kv]
            AVB = avbuf_of(si)
            ev_exp = {}
            ev_av = {}
            if si == 0:
                for g0 in (0, 1, 2):
                    emit_qk(0, g0)
            for g in range(NSC):
                lag = ev_qk.get((si, g + 1)) or ev_av[g - 2]
                p._wait("s", lag)
                ps = ring_of(si)[g % 3]
                pb = probs_sb[:, (g % 4) * 1024:(g % 4 + 1) * 1024]
                p.raw("s", lambda e, ps=ps, pb=pb: e.activation(
                    pb, ps[:], AF.Exp, scale=ATT_SCALE))
                ev_exp[g] = p.raw_inc("s")
                if g + 3 < NSC:
                    p._wait("t", ev_exp[g])
                    emit_qk(si, g + 3)
                p._wait("t", ev_exp[g])
                if g == 0 and ev_ave_par[si % 2] is not None:
                    p._wait("t", ev_ave_par[si % 2])
                for nf in range(2):
                    p.raw("t", lambda e, g=g, nf=nf, so=so, AVB=AVB: e.matmul(
                        AVB[:, nf * 512:(nf + 1) * 512],
                        vt_sb[:, g * VT_W + so: g * VT_W + so + 128],
                        probs_sb[:, (g % 4) * 1024 + nf * 512:
                                 (g % 4) * 1024 + nf * 512 + 512],
                        start=(g == 0), stop=(g == NSC - 1)))
                ev_av[g] = p.raw_inc("t")
            if si + 1 < len(slots):
                p._wait("t", ev_exp[NSC - 1])
                emit_qk(si + 1, 0)
                emit_qk(si + 1, 1)
                ev_lag_tail = ev_qk[(si + 1, 1)]
            else:
                ev_lag_tail = t_drain_inc()
            srow = 64 * (1 - par)
            drow = par * 64
            p._wait("s", ev_lag_tail)
            if ev_norm_final is not None:
                p._wait("s", ev_norm_final)
            p.raw("s", lambda e, AVB=AVB: e.copy(av_f32[:], AVB[:]))
            ev_ave = p.raw_inc("s")
            ev_ave_par[si % 2] = ev_ave
            if si + 1 < len(slots):
                p._wait("t", ev_ave)
                emit_qk(si + 1, 2)
            p._wait("v", ev_ave)
            p.raw("v", lambda e, sr=srow: e.reciprocal(
                av_f32[sr:sr + 1, :], av_f32[sr:sr + 1, :]))
            p.raw("v", lambda e, sr=srow: e.tensor_copy(
                att_rbf[sr:sr + 1, :], av_f32[sr:sr + 1, :]))
            p.raw("v", lambda e, sr=srow, dr=drow: e.stream_shuffle(
                att_rbf[dr:dr + 64, :], att_rbf[sr:sr + 64, :], [0] * 32))
            p.raw("v", lambda e, dr=drow: e.stream_shuffle(
                att_rbf[dr + 32:dr + 64, :], att_rbf[dr:dr + 32, :], [0] * 32))
            p.raw("v", lambda e, dr=drow, cq=cq, qh=qh: e.tensor_mul(
                h_sb[dr:dr + 64,
                     cq * SEQ + qh * 1024: cq * SEQ + qh * 1024 + 1024],
                av_f32[dr:dr + 64, :], att_rbf[dr:dr + 64, :]))
            ev_norm_final = p.raw_inc("v")
            ev_exp_tail = ev_exp[NSC - 1]

        ev_fin = ev_norm_final
        p._wait("t", ev_fin)
        for R in (r_PA, r_PB, r_PC, r_PD, r_rowA, r_rowB, r_rb):
            R.w = [ev_fin]
            R.r = []
            R.dirty_by = None
        r_ao.w = [ev_fin]
        r_ao.r = []
        r_h.w = [ev_fin]
        r_h.r = []
        for R in (r_q, r_k, r_vt):
            R.r = [ev_fin]
        for R in r_probs:
            R.w = [ev_fin]
            R.r = []

        if "ao" in tap_aps:
            for c in range(KC):
                p.dma("g", tap_aps["ao"][c * 128:(c + 1) * 128, :],
                      h_sb[:, c * SEQ:(c + 1) * SEQ], reads=[r_ao])

        Prog.cur_phase = "wo"

        def wo_half(Hh):
            for mc in range(KC):
                ps = (PA, PB)[mc % 2]
                rp = PS[id(ps)]
                for kc in range(KC):
                    for nf in range(2):
                        p.op("t", lambda e, ps=ps, kc=kc, mc=mc, nf=nf,
                             Hh=Hh: e.matmul(
                                 ps[:, nf * 512:(nf + 1) * 512],
                                 wq_bf[:, kc * 1024 + mc * 128:
                                       kc * 1024 + mc * 128 + 128],
                                 h_sb[:, kc * SEQ + Hh * 1024 + nf * 512:
                                      kc * SEQ + Hh * 1024 + nf * 512 + 512],
                                 start=(kc == 0), stop=(kc == KC - 1)),
                             reads=[r_wo, r_ao], writes=[rp])
                p.inc("t")
                xs = x_sb[:, mc * SEQ + Hh * 1024: mc * SEQ + Hh * 1024 + 1024]
                p.op("v", lambda e, ps=ps, xs=xs, mc=mc: e.scalar_tensor_tensor(
                    xs, ps[:], bias_sb[:, 10 + mc:11 + mc], xs,
                    ALU.add, ALU.add),
                    reads=[rp, r_bias, r_x], writes=[r_x])
                p.inc("v")

        wo_half(0)
        Prog.cur_phase = "ln2"
        ln_stats(r_x, 0, PC, PD, extra_scr=[r_k, r_vt])
        Prog.cur_phase = "wo"
        wo_half(1)
        Prog.cur_phase = "ln2"
        ln_finish(r_x, q_sb, [r_h2, r_h2h[0], r_q], 0, PC, PD,
                  extra_scr=[r_k, r_vt])
        ln_stats(r_x, 1, PC, PD, extra_scr=[r_k, r_vt])
        ln_finish(r_x, q_sb, [r_h2, r_h2h[1], r_q], 1, PC, PD,
                  extra_scr=[r_k, r_vt])
        Prog.cur_phase = "ffn"
        Prog.cur_phase = "ln2"
        layernorm(r_x, q_sb, [r_h2, r_q], extra_scr=[r_k, r_vt])
        Prog.cur_phase = "ffn"

        if "h2" in tap_aps:
            for c in range(KC):
                p.dma("g", tap_aps["h2"][c * 128:(c + 1) * 128, :],
                      q_sb[:, c * SEQ:(c + 1) * SEQ], reads=[r_h2])

        first_ffn = [r_ao, r_h, r_k, r_vt]

        def stream_gu(th, hc):
            half = hc % 2
            fw = [r_wo] if (th == 0 and hc < 2) else []
            fd = (([r_wo, r_wk, r_wv] if th == 0 else
                   [r_wd[0], r_wd[1]]) if hc < 4 else [])
            streami(wg_a[hc], wg_buf[:, half * 1024:(half + 1) * 1024],
                    [r_wg[half]] + fw, (0, 2)[hc % 2], queue="q",
                    extra_dma=fd)
            streami(wu_a[hc], wu_buf[:, half * 1024:(half + 1) * 1024],
                    [r_wu[half]] + fw, (1, 3)[hc % 2], queue="g",
                    extra_dma=fd)

        def stream_wd(th, mc):
            half = mc % 2
            fwd = [r_wo, r_wk, r_wv] if (th == 0 and mc < 2) else []
            for j in range(3):
                ovl = []
                byte0 = half * 6144 + j * 2048
                if byte0 < 4096:
                    ovl.append(r_stg4[2])
                if byte0 + 2048 > 4096 and byte0 < 8192:
                    ovl.append(r_stg4[3])
                streami(wd_a[mc][:, j * 1024:(j + 1) * 1024],
                        wd_buf[:, half * FH + j * 1024:
                              half * FH + j * 1024 + 1024],
                        [r_wd[half]] + fwd + ovl, j % 2,
                        queue="q" if j % 2 == 0 else "g")

        for th in range(2):
            stream_gu(th, 0)
            for hc in range(HC):
                half = hc % 2
                extra = first_ffn if (th == 0 and hc < 2) else []
                gps = (PA, PB)[half]
                ups = (PC, PD)[half]
                for kc in range(KC):
                    for nf in range(2):
                        p.op("t", lambda e, gps=gps, kc=kc, nf=nf, th=th,
                             half=half: e.matmul(
                                 gps[:, nf * 512:(nf + 1) * 512],
                                 wg_buf[:, half * 1024 + kc * 128:
                                       half * 1024 + kc * 128 + 128],
                                 q_sb[:, kc * SEQ + th * 1024 + nf * 512:
                                      kc * SEQ + th * 1024 + nf * 512 + 512],
                                 start=(kc == 0), stop=(kc == KC - 1)),
                             reads=[r_wg[half], r_h2h[th]], writes=[PS[id(gps)]])
                for kc in range(KC):
                    for nf in range(2):
                        p.op("t", lambda e, ups=ups, kc=kc, nf=nf, th=th,
                             half=half: e.matmul(
                                 ups[:, nf * 512:(nf + 1) * 512],
                                 wu_buf[:, half * 1024 + kc * 128:
                                       half * 1024 + kc * 128 + 128],
                                 q_sb[:, kc * SEQ + th * 1024 + nf * 512:
                                      kc * SEQ + th * 1024 + nf * 512 + 512],
                                 start=(kc == 0), stop=(kc == KC - 1)),
                             reads=[r_wu[half], r_h2h[th]], writes=[PS[id(ups)]])
                p.inc("t")
                p.op("s", lambda e, gps=gps, half=half: e.activation(
                    sg_buf[:, half * 1024:(half + 1) * 1024], gps[:],
                    AF.Silu, scale=DQS),
                    reads=[PS[id(gps)]], writes=[r_sg[half]])
                p.inc("s")
                if hc + 1 < HC:
                    stream_gu(th, hc + 1)
                p.op("v", lambda e, ups=ups, hc=hc, half=half: e.tensor_mul(
                    hf_sb[:, hc * 1024:(hc + 1) * 1024], ups[:],
                    sg_buf[:, half * 1024:(half + 1) * 1024]),
                    reads=[PS[id(ups)], r_sg[half]],
                    writes=[r_hf] + extra)
                p.inc("v")
            stream_wd(th, 0)
            for mc in range(KC):
                half = mc % 2
                dps = (PA, PB)[half]
                for hcc in range(HC):
                    for nf in range(2):
                        p.op("t", lambda e, dps=dps, hcc=hcc, nf=nf,
                             half=half: e.matmul(
                                 dps[:, nf * 512:(nf + 1) * 512],
                                 wd_buf[:, half * FH + hcc * 128:
                                       half * FH + hcc * 128 + 128],
                                 hf_sb[:, hcc * 1024 + nf * 512:
                                      hcc * 1024 + nf * 512 + 512],
                                 start=(hcc == 0), stop=(hcc == HC - 1)),
                             reads=[r_wd[half], r_hf], writes=[PS[id(dps)]])
                p.inc("t")
                if mc + 1 < KC:
                    stream_wd(th, mc + 1)
                xs = x_sb[:, mc * SEQ + th * 1024: mc * SEQ + th * 1024 + 1024]
                p.op("v", lambda e, dps=dps, xs=xs: e.scalar_tensor_tensor(
                    xs, dps[:], DQS * DQS, xs, ALU.mult, ALU.add),
                    reads=[PS[id(dps)], r_x], writes=[r_x])
                p.inc("v")

        for c in range(KC):
            p.dma("g", yT[c * 128:(c + 1) * 128, :],
                  x_sb[:, c * SEQ:(c + 1) * SEQ], reads=[r_x])
        for s2 in ("dg0", "dg1", "dq0", "dq1"):
            p._wait("g", (s2, p.count[s2]))

        with nc.Block() as block:
            @block.tensor
            def _(eng):
                for fn in p.stream["t"]:
                    fn(eng)

            @block.vector
            def _(eng):
                for fn in p.stream["v"]:
                    fn(eng)

            @block.scalar
            def _(eng):
                for fn in p.stream["s"]:
                    fn(eng)

            @block.sync
            def _(eng):
                for fn in p.stream["q"]:
                    fn(eng)

            @block.gpsimd
            def _(eng):
                for fn in p.stream["g"]:
                    fn(eng)

    return nc


def prepare_inputs(inputs):
    x = np.asarray(inputs["L0_attn_x"])
    wq = np.asarray(inputs["L0_attn_wq"])
    bq = np.asarray(inputs["L0_attn_bq"])
    wk = np.asarray(inputs["L0_attn_wk"])
    bk = np.asarray(inputs["L0_attn_bk"])
    wv = np.asarray(inputs["L0_attn_wv"])
    bv = np.asarray(inputs["L0_attn_bv"])
    wo = np.asarray(inputs["L0_attn_wo"])
    bo = np.asarray(inputs["L0_attn_bo"])
    wg = np.asarray(inputs["L0_ffn_w_gate"])
    wu = np.asarray(inputs["L0_ffn_w_up"])
    wd = np.asarray(inputs["L0_ffn_w_down"])

    wq_c = wq[PERM_F]
    bq_c = bq[PERM_F]
    perm_kv = np.concatenate([np.arange(64) + k * 64 for k in VT_KV_ORDER])
    wv_c = wv[perm_kv]
    bv_c = bv[perm_kv]
    wo_c = wo[:, PERM_F]

    wq_al = np.ascontiguousarray(
        wq_c.reshape(KC, 128, KC, 128).transpose(3, 2, 0, 1)).reshape(128, KC * DM)
    wk_al = np.ascontiguousarray(
        wk.reshape(2, 128, KC, 128).transpose(3, 2, 0, 1)).reshape(128, KC * DKV)
    wv_al = np.ascontiguousarray(
        wv_c.reshape(DKV, KC, 128).transpose(2, 1, 0)).reshape(128, KC * DKV)
    wo_al = np.ascontiguousarray(
        wo_c.reshape(KC, 128, KC, 128).transpose(3, 2, 0, 1)).reshape(128, KC * DM)
    wg_al = np.ascontiguousarray(
        wg.reshape(HC, 128, KC, 128).transpose(0, 3, 2, 1)).reshape(HC, 128, DM)
    wu_al = np.ascontiguousarray(
        wu.reshape(HC, 128, KC, 128).transpose(0, 3, 2, 1)).reshape(HC, 128, DM)
    wd_al = np.ascontiguousarray(
        wd.reshape(KC, 128, HC, 128).transpose(0, 3, 2, 1)).reshape(KC, 128, FH)

    shared = {
        "wq_a": wq_al, "wk_a": wk_al, "wv_a": wv_al, "wo_a": wo_al,
        "bq_r": np.ascontiguousarray(bq_c.reshape(1, DM)),
        "bk_r": np.ascontiguousarray(bk.reshape(1, DKV)),
        "bv_r": np.ascontiguousarray(bv_c.reshape(1, DKV)),
        "bo_r": np.ascontiguousarray(bo.reshape(1, DM)),
        "wg_a": wg_al, "wu_a": wu_al, "wd_a": wd_al,
    }
    in_maps = []
    for i in range(8):
        m = dict(shared)
        m["xT"] = np.ascontiguousarray(x[i].T)
        in_maps.append(m)
    return in_maps


def kernel(**inputs):
    in_maps = prepare_inputs(inputs)
    nc = build_program()
    res = run_bass_kernel_spmd(nc, in_maps, core_ids=list(range(8)))
    out = np.stack([np.ascontiguousarray(np.asarray(r["yT"]).T)
                    for r in res.results], axis=0)
    return out.astype(np.float32, copy=False)


# version 13
# speedup vs baseline: 1.0403x; 1.0403x over previous
import numpy as np
from contextlib import ExitStack

import concourse.bass as bass
import concourse.mybir as mybir
from concourse.ap import AP
from concourse.bass_utils import run_bass_kernel_spmd

F32 = mybir.dt.float32
BF16 = mybir.dt.bfloat16
I32 = mybir.dt.int32
AF = mybir.ActivationFunctionType
ALU = mybir.AluOpType

SEQ = 2048
DM = 1024
DKV = 256
NH, NKV, DH = 16, 4, 64
FH = 3072
KC = DM // 128
HC = FH // 128
NSC = SEQ // 128
EPS = 1e-5
ATT_SCALE = 0.125
DQS = 1.0 / 127.0

A_HEADS = [0, 1, 2, 3, 8, 9, 10, 11]
B_HEADS = [4, 5, 6, 7, 12, 13, 14, 15]
HEAD_POS = {}
for _c in range(8):
    HEAD_POS[A_HEADS[_c]] = (_c, 0)
    HEAD_POS[B_HEADS[_c]] = (_c, 1)
PERM_F = np.zeros(DM, np.int64)
for _h, (_c, _p) in HEAD_POS.items():
    PERM_F[_c * 128 + _p * 64: _c * 128 + _p * 64 + 64] = np.arange(64) + _h * 64

VT_W = 448
VT_STAT_OFF = {0: 128, 1: 0, 2: 320, 3: 192}
VT_KV_ORDER = [1, 0, 3, 2]


class Rsrc:
    __slots__ = ("name", "w", "r", "dirty_by")

    def __init__(self, name):
        self.name = name
        self.w = []
        self.r = []
        self.dirty_by = None


class Prog:

    serialize = False
    ser_phases = set()
    cur_phase = ""
    trace_log = None

    def __init__(self, nc, sems):
        self.nc = nc
        self.stream = {e: [] for e in ("t", "v", "s", "g", "q")}
        self.sems = sems
        self.count = {k: 0 for k in sems}
        self.floor = {e: {} for e in self.stream}
        self.pending = {e: ([], []) for e in self.stream}
        self.last_op_idx = {e: -1 for e in self.stream}
        self.evclock = {}
        self.eng_sem = {"t": "ts", "v": "vs", "s": "ss"}
        self.dma_sem = {"q": ["dq0", "dq1"], "g": ["dg0", "dg1"]}
        self.dma_rr = {"q": 0, "g": 0}

    def _wait(self, e, ev):
        sem, val = ev
        if self.floor[e].get(sem, 0) >= val:
            return
        assert val <= self.count[sem], (e, sem, val, self.count[sem])
        h = self.sems[sem]
        self.stream[e].append(lambda eng, h=h, v=val: eng.wait_ge(h, v))
        if Prog.trace_log is not None:
            Prog.trace_log.append(("W", e, sem, val, Prog.cur_phase))
        clk = self.evclock.get((sem, val), {})
        f = self.floor[e]
        for s2, v2 in clk.items():
            if f.get(s2, 0) < v2:
                f[s2] = v2
        if f.get(sem, 0) < val:
            f[sem] = val

    def _ser(self, e):
        if not (Prog.serialize or Prog.cur_phase in Prog.ser_phases):
            return
        for sem, cnt in self.count.items():
            if cnt:
                self._wait(e, (sem, cnt))

    def raw(self, e, fn):
        self.stream[e].append(fn)
        self.last_op_idx[e] = len(self.stream[e]) - 1

    def raw_inc(self, e):
        sem = self.eng_sem[e]
        self.count[sem] += 1
        val = self.count[sem]
        h = self.sems[sem]
        i = self.last_op_idx[e]
        assert i == len(self.stream[e]) - 1, (e, i, len(self.stream[e]))
        old_fn = self.stream[e][i]
        self.stream[e][i] = (
            lambda eng, f=old_fn, h=h: f(eng).then_inc(h, 1))
        if Prog.trace_log is not None:
            Prog.trace_log.append(("i", e, sem, val, Prog.cur_phase))
        f = dict(self.floor[e])
        f[sem] = val
        self.evclock[(sem, val)] = f
        return (sem, val)

    def op(self, e, fn, reads=(), writes=()):
        self._ser(e)
        for R in reads:
            assert R.dirty_by in (None, e), f"read of dirty {R.name} on {e}"
            for ev in R.w:
                self._wait(e, ev)
        for R in writes:
            assert R.dirty_by in (None, e), f"write of dirty {R.name} on {e}"
            for ev in R.w:
                self._wait(e, ev)
            for ev in R.r:
                self._wait(e, ev)
        self.stream[e].append(fn)
        self.last_op_idx[e] = len(self.stream[e]) - 1
        pr, pw = self.pending[e]
        pr.extend(reads)
        pw.extend(writes)
        for R in writes:
            R.dirty_by = e
        for R in reads:
            if R.dirty_by is None:
                R.dirty_by = e

    def inc(self, e, drain=None):
        if drain is None:
            drain = (e == "t")
        sem = self.eng_sem[e]
        self.count[sem] += 1
        val = self.count[sem]
        h = self.sems[sem]
        if drain:
            self.stream[e].append(
                lambda eng, h=h: eng.maybe_drain_then_inc((h, 1), fusable=True))
        else:
            i = self.last_op_idx[e]
            assert i == len(self.stream[e]) - 1, (e, i, len(self.stream[e]))
            old_fn = self.stream[e][i]
            self.stream[e][i] = (
                lambda eng, f=old_fn, h=h: f(eng).then_inc(h, 1))
        if Prog.trace_log is not None:
            Prog.trace_log.append(("I", e, sem, val, Prog.cur_phase))
        f = dict(self.floor[e])
        f[sem] = val
        self.evclock[(sem, val)] = f
        ev = (sem, val)
        pr, pw = self.pending[e]
        for R in pw:
            R.w = [ev]
            R.r = []
            R.dirty_by = None
        for R in pr:
            if R.dirty_by == e:
                R.dirty_by = None
            R.r.append(ev)
        self.pending[e] = ([], [])
        return ev

    def dma(self, e, out_ap, in_ap, reads=(), writes=()):
        self._ser(e)
        for R in reads:
            assert R.dirty_by in (None, e)
            for ev in R.w:
                self._wait(e, ev)
        for R in writes:
            assert R.dirty_by in (None, e)
            for ev in R.w + R.r:
                self._wait(e, ev)
        sem = self.dma_sem[e][self.dma_rr[e]]
        self.dma_rr[e] ^= 1
        h = self.sems[sem]
        self.count[sem] += 16
        val = self.count[sem]
        self.stream[e].append(
            lambda eng, h=h, o=out_ap, i=in_ap: eng.dma_start(o, i).then_inc(h, 16))
        if Prog.trace_log is not None:
            Prog.trace_log.append(("D", e, sem, val, Prog.cur_phase))
        f = dict(self.floor[e])
        f[sem] = val
        self.evclock[(sem, val)] = f
        ev = (sem, val)
        for R in writes:
            R.w = [ev]
            R.r = []
        for R in reads:
            R.r.append(ev)
        return ev


def build_program(taps=(), serialize=False, ser_phases=()):
    taps = set(taps)
    Prog.serialize = serialize
    Prog.ser_phases = set(ser_phases)
    Prog.cur_phase = "pre"
    nc = bass.Bass("TRN2", target_bir_lowering=False,
                   dynamic_dma_scratch_size=16384,
                   detect_race_conditions=False)

    def din(name, shape, dtype):
        return nc.dram_tensor(name, list(shape), dtype, kind="ExternalInput").ap()

    xT = din("xT", (DM, SEQ), F32)
    wq_a = din("wq_a", (128, KC * DM), F32)
    wk_a = din("wk_a", (128, KC * DKV), F32)
    wv_a = din("wv_a", (128, KC * DKV), F32)
    wo_a = din("wo_a", (128, KC * DM), F32)
    bq_r = din("bq_r", (1, DM), F32)
    bk_r = din("bk_r", (1, DKV), F32)
    bv_r = din("bv_r", (1, DKV), F32)
    bo_r = din("bo_r", (1, DM), F32)
    wg_a = din("wg_a", (HC, 128, DM), I32)
    wu_a = din("wu_a", (HC, 128, DM), I32)
    wd_a = din("wd_a", (KC, 128, FH), I32)
    yT = nc.dram_tensor("yT", [DM, SEQ], F32, kind="ExternalOutput").ap()

    tap_aps = {}
    for tname in taps:
        shp = {"h1": (DM, SEQ), "q": (DM, SEQ), "k": (DKV, SEQ),
               "vt": (128, NSC * VT_W), "ao": (DM, SEQ), "x1": (DM, SEQ),
               "h2": (DM, SEQ)}[tname]
        dt = F32 if tname == "x1" else BF16
        tap_aps[tname] = nc.dram_tensor("tap_" + tname, list(shp), dt,
                                        kind="ExternalOutput").ap()

    with ExitStack() as ctx:
        base = (nc.sbuf_base + 31) & ~31
        cur = [base]

        offs = {}

        def place(name, shape, dtype, at=None):
            if at is None:
                off = cur[0]
                per = int(np.prod(shape[1:])) * mybir.dt.size(dtype)
                cur[0] = (off + per + 31) & ~31
            else:
                off = at
            h = nc.alloc_sbuf_tensor_at(name, list(shape), dtype, offset=off)
            offs[name] = off
            assert cur[0] <= nc.sbuf_top, (name, cur[0], nc.sbuf_top)
            return h

        x_sb = place("x_sb", [128, KC * SEQ], F32)
        B0 = cur[0]
        h_sb = place("h_sb", [128, KC * SEQ], BF16)
        k_sb = place("k_sb", [128, 2 * SEQ], BF16)
        vt_sb = place("vt_sb", [128, NSC * VT_W], BF16)
        probs_sb = place("probs_sb", [128, 4 * 1024], BF16)
        hf_sb = place("hf_sb", [128, HC * 1024], BF16, at=B0)
        q_sb = place("q_sb", [128, KC * SEQ], BF16)
        W0 = cur[0]
        wq_bf = place("wq_bf", [128, KC * DM], BF16)
        wk_bf = place("wk_bf", [128, KC * DKV], BF16)
        wv_bf = place("wv_bf", [128, KC * DKV], BF16)
        wg_buf = place("wg_buf", [128, 2 * DM], BF16, at=W0)
        wu_buf = place("wu_buf", [128, 2 * DM], BF16, at=W0 + 4096)
        sg_buf = place("sg_buf", [128, 2 * 1024], BF16, at=W0 + 8192)
        wd_buf = place("wd_buf", [128, 2 * FH], BF16, at=W0 + 12288)
        stg32 = place("stg32", [128, 2048], F32)
        stgi = place("stgi", [128, 2048], I32, at=offs["stg32"])
        stgi2 = place("stgi2", [128, 2048], I32, at=W0 + 12288)
        ones_row = place("ones_row", [128, 128], BF16)
        ones_col = place("ones_col", [128, 1], BF16)
        ones_f32 = place("ones_f32", [1, 1], F32)
        eps_c = place("eps_c", [1, 1], F32)
        rowA = place("rowA", [1, 1024], F32)
        rowB = place("rowB", [1, 1024], F32)
        bv_row = place("bv_row", [1, DKV], F32)
        bv_rbf = place("bv_rbf", [1, DKV], BF16)
        bv_bs = place("bv_bs", [128, DKV], BF16)
        m_bf = place("m_bf", [1, SEQ], BF16, at=offs["k_sb"] + 4096)
        rstd_bf = place("rstd_bf", [1, SEQ], BF16, at=offs["vt_sb"] + 8192)
        av_f32 = place("av_f32", [128, 1024], F32, at=offs["rowA"])
        att_rbf = place("att_rbf", [128, 1024], BF16, at=offs["rowB"])
        rb_bf = place("rb_bf", [128, 1024], BF16, at=offs["rowB"] + 2048)
        bias_sb = place("bias_sb", [128, 18], F32)

        PA = ctx.enter_context(nc.psum_tensor("PA", [128, 1024], F32))
        PB = ctx.enter_context(nc.psum_tensor("PB", [128, 1024], F32))
        PC = ctx.enter_context(nc.psum_tensor("PC", [128, 1024], F32))
        PD = ctx.enter_context(nc.psum_tensor("PD", [128, 1024], F32))

        sems = {}
        for sname in ("ts", "vs", "ss", "dq0", "dq1", "dg0", "dg1"):
            sems[sname] = ctx.enter_context(nc.semaphore(sname))
        p = Prog(nc, sems)

        r_x = Rsrc("x")
        r_xh = [Rsrc("xh0"), Rsrc("xh1")]
        r_h = Rsrc("h")
        r_ao = Rsrc("ao")
        r_q = Rsrc("q")
        r_h2 = Rsrc("h2")
        r_k = Rsrc("k")
        r_vt = Rsrc("vt")
        r_probs = [Rsrc(f"probs{i}") for i in range(4)]
        r_hf = Rsrc("hf")
        r_h2h = [Rsrc("h2h0"), Rsrc("h2h1")]
        r_PA, r_PB = Rsrc("PA"), Rsrc("PB")
        r_PC, r_PD = Rsrc("PC"), Rsrc("PD")
        r_wq, r_wk, r_wv, r_wo = Rsrc("wq"), Rsrc("wk"), Rsrc("wv"), Rsrc("wo")
        r_wg = [Rsrc("wg0"), Rsrc("wg1")]
        r_wu = [Rsrc("wu0"), Rsrc("wu1")]
        r_sg = [Rsrc("sg0"), Rsrc("sg1")]
        r_wd = [Rsrc("wd0"), Rsrc("wd1")]
        r_stg = [Rsrc("stg0"), Rsrc("stg1")]
        r_stgi = r_stg
        r_stg4 = [r_stg[0], r_stg[1], Rsrc("stg2"), Rsrc("stg3")]
        r_rowA, r_rowB = Rsrc("rowA"), Rsrc("rowB")
        r_mrow, r_rrow = Rsrc("m_bf"), Rsrc("rstd_bf")
        r_bias, r_bv, r_bvbs, r_rb = (Rsrc("bias"), Rsrc("bv"),
                                      Rsrc("bv_bs"), Rsrc("rb"))
        r_const = Rsrc("const")
        r_xtmp, r_sqtmp, r_t1tmp = Rsrc("xtmp"), Rsrc("sqtmp"), Rsrc("t1tmp")
        PS = {id(PA): r_PA, id(PB): r_PB, id(PC): r_PC, id(PD): r_PD}
        RS_MBS = [r_probs[0], r_probs[1]]
        RS_RBS = [r_probs[2], r_probs[3]]
        m_bs = probs_sb[:, 0:2048]
        rs_bs = probs_sb[:, 2048:4096]

        p.op("v", lambda e: e.memset(ones_row[:], 1.0), writes=[r_const])
        p.op("v", lambda e: e.memset(ones_col[:], 1.0), writes=[r_const])
        p.op("v", lambda e: e.memset(ones_f32[:], 1.0), writes=[r_const])
        p.op("v", lambda e: e.memset(att_rbf[:], 0.0), writes=[r_rowB])
        p.op("v", lambda e: e.memset(eps_c[:], float(EPS)), writes=[r_const])
        p.inc("v")

        xevs = []
        for c in range(KC):
            xevs.append(p.dma("g", x_sb[:, c * SEQ:(c + 1) * SEQ],
                              xT[c * 128:(c + 1) * 128, :]))
        r_x.w = xevs
        r_xh[0].w = list(xevs)
        r_xh[1].w = list(xevs)

        def bias_block(row_src, n_chunks, col0):
            p.dma("q", rowA[0:1, 0:row_src.shape[1]], row_src, writes=[r_rowA])
            for j in range(n_chunks):
                p.op("t", lambda e, j=j, c0=col0: e.matmul(
                    PA[:, c0 + j:c0 + j + 1],
                    rowA[0:1, j * 128:(j + 1) * 128],
                    ones_f32[:], start=True, stop=True),
                    reads=[r_rowA, r_const], writes=[r_PA])
            p.inc("t")
        bias_block(bq_r, 8, 0)
        bias_block(bk_r, 2, 8)
        bias_block(bo_r, 8, 10)
        p.op("s", lambda e: e.copy(bias_sb[:], PA[:, 0:18]),
             reads=[r_PA], writes=[r_bias])
        p.inc("s")

        p.dma("q", bv_row[:], bv_r, writes=[r_bv])
        p.op("v", lambda e: e.tensor_copy(bv_rbf[:], bv_row[:]),
             reads=[r_bv], writes=[r_bv])
        p.inc("v")
        p.op("t", lambda e: e.matmul(PB[:, 0:DKV], ones_row[0:1, 0:128],
                                     bv_rbf[:], start=True, stop=True),
             reads=[r_bv, r_const], writes=[r_PB])
        p.inc("t")
        p.op("s", lambda e: e.copy(bv_bs[:], PB[:, 0:DKV]),
             reads=[r_PB], writes=[r_bvbs])
        p.inc("s")

        def stream32(src_ap, dst_ap, rsrcs, half):
            st = stg32[:, half * 1024:(half + 1) * 1024]
            p.dma("q", st, src_ap, writes=[r_stg[half]])
            p.op("v", lambda e, st=st, d=dst_ap: e.tensor_copy(d, st),
                 reads=[r_stg[half]], writes=list(rsrcs))
            p.inc("v")

        def streami(src_ap, dst_ap, rsrcs, slot, queue="q", extra_dma=()):
            t2 = (stgi, stgi, stgi2, stgi2)[slot]
            st = t2[:, (slot % 2) * 1024:(slot % 2 + 1) * 1024]
            p.dma(queue, st, src_ap,
                  writes=[r_stg4[slot]] + list(extra_dma))
            p.op("v", lambda e, st=st, d=dst_ap: e.tensor_copy(d, st),
                 reads=[r_stg4[slot]], writes=rsrcs)
            p.inc("v")

        for i in range(2):
            stream32(wk_a[:, i * 1024:(i + 1) * 1024],
                     wk_bf[:, i * 1024:(i + 1) * 1024], [r_wk], i % 2)
        for i in range(2):
            stream32(wv_a[:, i * 1024:(i + 1) * 1024],
                     wv_bf[:, i * 1024:(i + 1) * 1024], [r_wv], i % 2)
        for i in range(8):
            stream32(wq_a[:, i * 1024:(i + 1) * 1024],
                     wq_bf[:, i * 1024:(i + 1) * 1024], [r_wq], i % 2)

        def ln_stats_chunk(src_res, Hh, sum_ps, sq_ps, c, extra_scr=()):
            xb = k_sb[:, 0:1024]
            sq = vt_sb[:, 0:1024]
            rsum, rsq = PS[id(sum_ps)], PS[id(sq_ps)]
            w_xb = [r_xtmp] + [r for r in extra_scr if r.name == "k"]
            w_sq = [r_sqtmp] + [r for r in extra_scr if r.name == "vt"]
            xs = x_sb[:, c * SEQ + Hh * 1024: c * SEQ + Hh * 1024 + 1024]
            p.op("v", lambda e, xs=xs: e.tensor_copy(xb, xs),
                 reads=[src_res], writes=w_xb)
            p.inc("v")
            p.op("s", lambda e, xs=xs: e.activation(sq, xs, AF.Square),
                 reads=[src_res], writes=w_sq)
            p.inc("s")
            for piece in range(2):
                off = piece * 512
                p.op("t", lambda e, off=off, c=c, sum_ps=sum_ps: e.matmul(
                    sum_ps[0:1, off:off + 512], ones_col[:],
                    xb[:, off:off + 512],
                    start=(c == 0), stop=(c == KC - 1)),
                    reads=[r_xtmp, r_const], writes=[rsum])
            for piece in range(2):
                off = piece * 512
                p.op("t", lambda e, off=off, c=c, sq_ps=sq_ps: e.matmul(
                    sq_ps[0:1, off:off + 512], ones_col[:],
                    sq[:, off:off + 512],
                    start=(c == 0), stop=(c == KC - 1)),
                    reads=[r_sqtmp, r_const], writes=[rsq])
            p.inc("t")

        def ln_stats(src_res, Hh, sum_ps, sq_ps, extra_scr=()):
            for c in range(KC):
                ln_stats_chunk(src_res, Hh, sum_ps, sq_ps, c, extra_scr)

        def ln_finish(src_res, dst_sb, dst_out_res, Hh, sum_ps, sq_ps,
                      extra_scr=()):
            t1 = vt_sb[:, 2048:3072]
            rsum, rsq = PS[id(sum_ps)], PS[id(sq_ps)]
            w_t1 = [r_t1tmp] + [r for r in extra_scr if r.name == "vt"]
            p.op("v", lambda e: e.tensor_scalar(
                rowA[:], sum_ps[0:1, :], 1.0 / DM, None, ALU.mult),
                reads=[rsum], writes=[r_rowA])
            p.op("v", lambda e: e.tensor_mul(rowB[:], rowA[:], rowA[:]),
                 reads=[r_rowA], writes=[r_rowB])
            p.inc("v")
            p.op("s", lambda e, Hh=Hh: e.copy(
                m_bf[0:1, Hh * 1024:(Hh + 1) * 1024], rowA[:]),
                reads=[r_rowA], writes=[r_mrow, r_k])
            p.inc("s")
            p.op("v", lambda e: e.scalar_tensor_tensor(
                rowA[:], sq_ps[0:1, :], 1.0 / DM, rowB[:],
                ALU.mult, ALU.subtract),
                reads=[rsq, r_rowB], writes=[r_rowA])
            p.inc("v")
            p.op("s", lambda e: e.activation(rowB[:], rowA[:], AF.Ln,
                                             bias=eps_c[:]),
                 reads=[r_rowA, r_const], writes=[r_rowB])
            p.op("s", lambda e, Hh=Hh: e.activation(
                rstd_bf[0:1, Hh * 1024:(Hh + 1) * 1024], rowB[:],
                AF.Exp, scale=-0.5),
                reads=[r_rowB], writes=[r_rrow, r_vt])
            p.inc("s")
            for piece in range(2):
                off = piece * 512
                p.op("t", lambda e, off=off, Hh=Hh: e.matmul(
                    sum_ps[:, off:off + 512], ones_row[0:1, 0:128],
                    m_bf[0:1, Hh * 1024 + off: Hh * 1024 + off + 512],
                    start=True, stop=True),
                    reads=[r_mrow, r_k, r_const], writes=[rsum])
                p.op("t", lambda e, off=off, Hh=Hh: e.matmul(
                    sq_ps[:, off:off + 512], ones_row[0:1, 0:128],
                    rstd_bf[0:1, Hh * 1024 + off: Hh * 1024 + off + 512],
                    start=True, stop=True),
                    reads=[r_rrow, r_vt, r_const], writes=[rsq])
            p.inc("t")
            mb_s = probs_sb[:, 0:1024]
            rb_s = probs_sb[:, 2048:3072]
            p.op("s", lambda e: e.copy(mb_s[:], sum_ps[:]),
                 reads=[rsum], writes=[r_probs[0]])
            p.op("s", lambda e: e.copy(rb_s[:], sq_ps[:]),
                 reads=[rsq], writes=[r_probs[2]])
            p.inc("s")
            for c in range(KC):
                xs = x_sb[:, c * SEQ + Hh * 1024: c * SEQ + Hh * 1024 + 1024]
                p.op("v", lambda e, xs=xs: e.tensor_sub(t1, xs, mb_s),
                     reads=[src_res, r_probs[0]], writes=w_t1)
                p.op("v", lambda e, c=c, Hh=Hh: e.tensor_mul(
                    dst_sb[:, c * SEQ + Hh * 1024: c * SEQ + Hh * 1024 + 1024],
                    t1, rb_s),
                    reads=[r_t1tmp, r_probs[2]], writes=dst_out_res)
                p.inc("v")

        def layernorm(src_hs, dst_sb, dst_out_res, extra_scr=()):
            ln_stats(src_hs[0], 0, PA, PC, extra_scr)
            ln_finish(src_hs[0], dst_sb, dst_out_res, 0, PA, PC, extra_scr)
            ln_stats(src_hs[1], 1, PB, PD, extra_scr)
            ln_finish(src_hs[1], dst_sb, dst_out_res, 1, PB, PD, extra_scr)

        layernorm(r_xh, h_sb, [r_h])
        Prog.cur_phase = "qproj"

        for mc in range(KC):
            ps_pair = (PA, PB) if mc % 2 == 0 else (PC, PD)
            rs_pair = [PS[id(ps_pair[0])], PS[id(ps_pair[1])]]
            for kc in range(KC):
                for nf in range(4):
                    ps = ps_pair[nf // 2]
                    off = (nf % 2) * 512
                    p.op("t", lambda e, ps=ps, off=off, kc=kc, mc=mc, nf=nf:
                         e.matmul(
                             ps[:, off:off + 512],
                             wq_bf[:, kc * 1024 + mc * 128:
                                   kc * 1024 + mc * 128 + 128],
                             h_sb[:, kc * SEQ + nf * 512:
                                  kc * SEQ + nf * 512 + 512],
                             start=(kc == 0), stop=(kc == KC - 1)),
                         reads=[r_wq, r_h], writes=rs_pair)
            p.inc("t")
            for half in range(2):
                ps = ps_pair[half]
                p.op("s", lambda e, ps=ps, mc=mc, half=half: e.activation(
                    q_sb[:, mc * SEQ + half * 1024:
                         mc * SEQ + half * 1024 + 1024],
                    ps[:], AF.Identity, bias=bias_sb[:, mc:mc + 1]),
                    reads=[rs_pair[half], r_bias], writes=[r_q])
            p.inc("s")

        Prog.cur_phase = "kproj"
        for mc2 in range(2):
            ps_pair = (PA, PB) if mc2 % 2 == 0 else (PC, PD)
            rs_pair = [PS[id(ps_pair[0])], PS[id(ps_pair[1])]]
            for kc in range(KC):
                for nf in range(4):
                    ps = ps_pair[nf // 2]
                    off = (nf % 2) * 512
                    p.op("t", lambda e, ps=ps, off=off, kc=kc, mc2=mc2, nf=nf:
                         e.matmul(
                             ps[:, off:off + 512],
                             wk_bf[:, kc * 256 + mc2 * 128:
                                   kc * 256 + mc2 * 128 + 128],
                             h_sb[:, kc * SEQ + nf * 512:
                                  kc * SEQ + nf * 512 + 512],
                             start=(kc == 0), stop=(kc == KC - 1)),
                         reads=[r_wk, r_h], writes=rs_pair)
            p.inc("t")
            for half in range(2):
                ps = ps_pair[half]
                p.op("s", lambda e, ps=ps, mc2=mc2, half=half: e.activation(
                    k_sb[:, mc2 * SEQ + half * 1024:
                         mc2 * SEQ + half * 1024 + 1024],
                    ps[:], AF.Identity, bias=bias_sb[:, 8 + mc2:9 + mc2]),
                    reads=[rs_pair[half], r_bias], writes=[r_k])
            p.inc("s")

        Prog.cur_phase = "vtok"
        def z_init(e):
            e.memset(AP(vt_sb, 0, [[NSC * VT_W, 128], [VT_W, NSC],
                                   [192, 3], [1, 64]]), 0.0)
        p.op("v", z_init, writes=[r_vt, r_sqtmp, r_t1tmp])
        p.op("v", lambda e: e.memset(
            AP(vt_sb, 0, [[NSC * VT_W, 128], [VT_W, NSC], [192, 3]]), 1.0),
            writes=[r_vt])
        p.inc("v")

        for sc in range(NSC):
            ps = (PA, PB)[sc % 2]
            rp = PS[id(ps)]
            for kc in range(KC):
                p.op("t", lambda e, ps=ps, sc=sc, kc=kc: e.matmul(
                    ps[:, 0:DKV],
                    h_sb[:, kc * SEQ + sc * 128: kc * SEQ + sc * 128 + 128],
                    wv_bf[:, kc * 256:(kc + 1) * 256],
                    start=(kc == 0), stop=(kc == KC - 1)),
                    reads=[r_h, r_wv], writes=[rp])
            p.inc("t")
            p.op("v", lambda e, ps=ps, sc=sc: e.tensor_add(
                AP(vt_sb, sc * VT_W + 64, [[NSC * VT_W, 128], [192, 2], [1, 128]]),
                ps[:, 0:DKV].rearrange("p (s c) -> p s c", s=2),
                bv_bs[:].rearrange("p (s c) -> p s c", s=2)),
                reads=[rp, r_bvbs], writes=[r_vt])
            p.inc("v")

        if "h1" in tap_aps:
            for c in range(KC):
                p.dma("g", tap_aps["h1"][c * 128:(c + 1) * 128, :],
                      h_sb[:, c * SEQ:(c + 1) * SEQ], reads=[r_h])
        if "q" in tap_aps:
            for c in range(KC):
                p.dma("g", tap_aps["q"][c * 128:(c + 1) * 128, :],
                      q_sb[:, c * SEQ:(c + 1) * SEQ], reads=[r_q])
        if "k" in tap_aps:
            for c in range(2):
                p.dma("g", tap_aps["k"][c * 128:(c + 1) * 128, :],
                      k_sb[:, c * SEQ:(c + 1) * SEQ], reads=[r_k])
        if "vt" in tap_aps:
            p.dma("g", tap_aps["vt"], vt_sb[:], reads=[r_vt])

        for i in range(8):
            stream32(wo_a[:, i * 1024:(i + 1) * 1024],
                     wq_bf[:, i * 1024:(i + 1) * 1024],
                     [r_wo, r_wq] if i < 2 else [r_wo], i % 2)

        Prog.cur_phase = "att"
        ev_exp_tail = None
        ev_norm_prev = None
        ev_qk = {}
        for R in (r_PA, r_PB, r_PC, r_PD):
            for ev in R.w + R.r:
                p._wait("t", ev)
        for R in (r_k, r_q, r_vt):
            for ev in R.w:
                p._wait("t", ev)
        for ev in (r_rowA.w + r_rowA.r + r_rowB.w + r_rowB.r +
                   r_ao.w + r_ao.r + r_h.r + r_h.w):
            p._wait("v", ev)

        def ring_of(si):
            return (PA, PB, PD) if si % 2 == 0 else (PA, PB, PC)

        def avbuf_of(si):
            return PC if si % 2 == 0 else PD

        def t_drain_inc():
            val = p.count["ts"] + 1
            p.count["ts"] = val
            p.stream["t"].append(
                lambda eng, h2=p.sems["ts"]: eng.maybe_drain_then_inc(
                    (h2, 1), fusable=True))
            fcl = dict(p.floor["t"])
            fcl["ts"] = val
            p.evclock[("ts", val)] = fcl
            return ("ts", val)

        slots = [(h, qh) for h in range(NH) for qh in range(2)]

        def emit_qk(slot_idx, g):
            h, qh = slots[slot_idx]
            cq, par = HEAD_POS[h]
            kc2 = (h // 4) // 2
            ps = ring_of(slot_idx)[g % 3]
            for nf in range(2):
                p.raw("t", lambda e, ps=ps, sc=g, nf=nf, cq=cq, par=par,
                      kc2=kc2, qh=qh: e.matmul(
                          ps[:, nf * 512:(nf + 1) * 512],
                          k_sb[par * 64: par * 64 + 64,
                               kc2 * SEQ + sc * 128: kc2 * SEQ + sc * 128 + 128],
                          q_sb[par * 64: par * 64 + 64,
                               cq * SEQ + qh * 1024 + nf * 512:
                               cq * SEQ + qh * 1024 + nf * 512 + 512],
                          start=True, stop=True))
            ev_qk[(slot_idx, g)] = p.raw_inc("t")

        ev_ave_par = {0: None, 1: None}
        ev_norm_final = None
        for si, (h, qh) in enumerate(slots):
            cq, par = HEAD_POS[h]
            kv = h // 4
            so = VT_STAT_OFF[kv]
            AVB = avbuf_of(si)
            ev_exp = {}
            ev_av = {}
            if si == 0:
                for g0 in (0, 1, 2):
                    emit_qk(0, g0)
            for g in range(NSC):
                lag = ev_qk.get((si, g + 1)) or ev_av[g - 2]
                p._wait("s", lag)
                ps = ring_of(si)[g % 3]
                pb = probs_sb[:, (g % 4) * 1024:(g % 4 + 1) * 1024]
                p.raw("s", lambda e, ps=ps, pb=pb: e.activation(
                    pb, ps[:], AF.Exp, scale=ATT_SCALE))
                ev_exp[g] = p.raw_inc("s")
                if g + 3 < NSC:
                    p._wait("t", ev_exp[g])
                    emit_qk(si, g + 3)
                p._wait("t", ev_exp[g])
                if g == 0 and ev_ave_par[si % 2] is not None:
                    p._wait("t", ev_ave_par[si % 2])
                for nf in range(2):
                    p.raw("t", lambda e, g=g, nf=nf, so=so, AVB=AVB: e.matmul(
                        AVB[:, nf * 512:(nf + 1) * 512],
                        vt_sb[:, g * VT_W + so: g * VT_W + so + 128],
                        probs_sb[:, (g % 4) * 1024 + nf * 512:
                                 (g % 4) * 1024 + nf * 512 + 512],
                        start=(g == 0), stop=(g == NSC - 1)))
                ev_av[g] = p.raw_inc("t")
            if si + 1 < len(slots):
                p._wait("t", ev_exp[NSC - 1])
                emit_qk(si + 1, 0)
                emit_qk(si + 1, 1)
                ev_lag_tail = ev_qk[(si + 1, 1)]
            else:
                ev_lag_tail = t_drain_inc()
            srow = 64 * (1 - par)
            drow = par * 64
            p._wait("s", ev_lag_tail)
            if ev_norm_final is not None:
                p._wait("s", ev_norm_final)
            p.raw("s", lambda e, AVB=AVB: e.copy(av_f32[:], AVB[:]))
            ev_ave = p.raw_inc("s")
            ev_ave_par[si % 2] = ev_ave
            if si + 1 < len(slots):
                p._wait("t", ev_ave)
                emit_qk(si + 1, 2)
            p._wait("v", ev_ave)
            p.raw("v", lambda e, sr=srow: e.reciprocal(
                av_f32[sr:sr + 1, :], av_f32[sr:sr + 1, :]))
            p.raw("v", lambda e, sr=srow: e.tensor_copy(
                att_rbf[sr:sr + 1, :], av_f32[sr:sr + 1, :]))
            p.raw("v", lambda e, sr=srow, dr=drow: e.stream_shuffle(
                att_rbf[dr:dr + 64, :], att_rbf[sr:sr + 64, :], [0] * 32))
            p.raw("v", lambda e, dr=drow: e.stream_shuffle(
                att_rbf[dr + 32:dr + 64, :], att_rbf[dr:dr + 32, :], [0] * 32))
            p.raw("v", lambda e, dr=drow, cq=cq, qh=qh: e.tensor_mul(
                h_sb[dr:dr + 64,
                     cq * SEQ + qh * 1024: cq * SEQ + qh * 1024 + 1024],
                av_f32[dr:dr + 64, :], att_rbf[dr:dr + 64, :]))
            ev_norm_final = p.raw_inc("v")
            ev_exp_tail = ev_exp[NSC - 1]

        ev_fin = ev_norm_final
        p._wait("t", ev_fin)
        for R in (r_PA, r_PB, r_PC, r_PD, r_rowA, r_rowB, r_rb):
            R.w = [ev_fin]
            R.r = []
            R.dirty_by = None
        r_ao.w = [ev_fin]
        r_ao.r = []
        r_h.w = [ev_fin]
        r_h.r = []
        for R in (r_q, r_k, r_vt):
            R.r = [ev_fin]
        for R in r_probs:
            R.w = [ev_fin]
            R.r = []

        if "ao" in tap_aps:
            for c in range(KC):
                p.dma("g", tap_aps["ao"][c * 128:(c + 1) * 128, :],
                      h_sb[:, c * SEQ:(c + 1) * SEQ], reads=[r_ao])

        Prog.cur_phase = "wo"

        def wo_half(Hh):
            for mc in range(KC):
                ps = (PA, PB)[mc % 2]
                rp = PS[id(ps)]
                for kc in range(KC):
                    for nf in range(2):
                        p.op("t", lambda e, ps=ps, kc=kc, mc=mc, nf=nf,
                             Hh=Hh: e.matmul(
                                 ps[:, nf * 512:(nf + 1) * 512],
                                 wq_bf[:, kc * 1024 + mc * 128:
                                       kc * 1024 + mc * 128 + 128],
                                 h_sb[:, kc * SEQ + Hh * 1024 + nf * 512:
                                      kc * SEQ + Hh * 1024 + nf * 512 + 512],
                                 start=(kc == 0), stop=(kc == KC - 1)),
                             reads=[r_wo, r_ao], writes=[rp])
                p.inc("t")
                xs = x_sb[:, mc * SEQ + Hh * 1024: mc * SEQ + Hh * 1024 + 1024]
                p.op("v", lambda e, ps=ps, xs=xs, mc=mc: e.scalar_tensor_tensor(
                    xs, ps[:], bias_sb[:, 10 + mc:11 + mc], xs,
                    ALU.add, ALU.add),
                    reads=[rp, r_bias, r_xh[Hh]], writes=[r_xh[Hh]])
                p.inc("v")

        wo_half(0)
        Prog.cur_phase = "ln2"
        def wo_mc(Hh, mc):
            ps = (PA, PB)[mc % 2]
            rp = PS[id(ps)]
            for kc in range(KC):
                for nf in range(2):
                    p.op("t", lambda e, ps=ps, kc=kc, mc=mc, nf=nf,
                         Hh=Hh: e.matmul(
                             ps[:, nf * 512:(nf + 1) * 512],
                             wq_bf[:, kc * 1024 + mc * 128:
                                   kc * 1024 + mc * 128 + 128],
                             h_sb[:, kc * SEQ + Hh * 1024 + nf * 512:
                                  kc * SEQ + Hh * 1024 + nf * 512 + 512],
                             start=(kc == 0), stop=(kc == KC - 1)),
                         reads=[r_wo, r_ao], writes=[rp])
            p.inc("t")
            xs = x_sb[:, mc * SEQ + Hh * 1024: mc * SEQ + Hh * 1024 + 1024]
            p.op("v", lambda e, ps=ps, xs=xs, mc=mc: e.scalar_tensor_tensor(
                xs, ps[:], bias_sb[:, 10 + mc:11 + mc], xs,
                ALU.add, ALU.add),
                reads=[rp, r_bias, r_xh[Hh]], writes=[r_xh[Hh]])
            p.inc("v")

        for c in range(KC):
            wo_mc(1, c)
            ln_stats_chunk(r_xh[0], 0, PC, PD, c, extra_scr=[r_k, r_vt])
        ln_finish(r_xh[0], q_sb, [r_h2, r_h2h[0], r_q], 0, PC, PD,
                  extra_scr=[r_k, r_vt])
        ln_stats(r_xh[1], 1, PC, PD, extra_scr=[r_k, r_vt])
        ln_finish(r_xh[1], q_sb, [r_h2, r_h2h[1], r_q], 1, PC, PD,
                  extra_scr=[r_k, r_vt])
        Prog.cur_phase = "ffn"

        if "h2" in tap_aps:
            for c in range(KC):
                p.dma("g", tap_aps["h2"][c * 128:(c + 1) * 128, :],
                      q_sb[:, c * SEQ:(c + 1) * SEQ], reads=[r_h2])

        first_ffn = [r_ao, r_h, r_k, r_vt]

        def stream_gu(th, hc):
            half = hc % 2
            fw = [r_wo] if (th == 0 and hc < 2) else []
            fd = (([r_wo, r_wk, r_wv] if th == 0 else
                   [r_wd[0], r_wd[1]]) if hc < 4 else [])
            streami(wg_a[hc], wg_buf[:, half * 1024:(half + 1) * 1024],
                    [r_wg[half]] + fw, (0, 2)[hc % 2], queue="q",
                    extra_dma=fd)
            streami(wu_a[hc], wu_buf[:, half * 1024:(half + 1) * 1024],
                    [r_wu[half]] + fw, (1, 3)[hc % 2], queue="g",
                    extra_dma=fd)

        def stream_wd(th, mc):
            half = mc % 2
            fwd = [r_wo, r_wk, r_wv] if (th == 0 and mc < 2) else []
            for j in range(3):
                ovl = []
                byte0 = half * 6144 + j * 2048
                if byte0 < 4096:
                    ovl.append(r_stg4[2])
                if byte0 + 2048 > 4096 and byte0 < 8192:
                    ovl.append(r_stg4[3])
                streami(wd_a[mc][:, j * 1024:(j + 1) * 1024],
                        wd_buf[:, half * FH + j * 1024:
                              half * FH + j * 1024 + 1024],
                        [r_wd[half]] + fwd + ovl, j % 2,
                        queue="q" if j % 2 == 0 else "g")

        for th in range(2):
            stream_gu(th, 0)
            for hc in range(HC):
                half = hc % 2
                extra = first_ffn if (th == 0 and hc < 2) else []
                gps = (PA, PB)[half]
                ups = (PC, PD)[half]
                for kc in range(KC):
                    for nf in range(2):
                        p.op("t", lambda e, gps=gps, kc=kc, nf=nf, th=th,
                             half=half: e.matmul(
                                 gps[:, nf * 512:(nf + 1) * 512],
                                 wg_buf[:, half * 1024 + kc * 128:
                                       half * 1024 + kc * 128 + 128],
                                 q_sb[:, kc * SEQ + th * 1024 + nf * 512:
                                      kc * SEQ + th * 1024 + nf * 512 + 512],
                                 start=(kc == 0), stop=(kc == KC - 1)),
                             reads=[r_wg[half], r_h2h[th]], writes=[PS[id(gps)]])
                for kc in range(KC):
                    for nf in range(2):
                        p.op("t", lambda e, ups=ups, kc=kc, nf=nf, th=th,
                             half=half: e.matmul(
                                 ups[:, nf * 512:(nf + 1) * 512],
                                 wu_buf[:, half * 1024 + kc * 128:
                                       half * 1024 + kc * 128 + 128],
                                 q_sb[:, kc * SEQ + th * 1024 + nf * 512:
                                      kc * SEQ + th * 1024 + nf * 512 + 512],
                                 start=(kc == 0), stop=(kc == KC - 1)),
                             reads=[r_wu[half], r_h2h[th]], writes=[PS[id(ups)]])
                p.inc("t")
                p.op("s", lambda e, gps=gps, half=half: e.activation(
                    sg_buf[:, half * 1024:(half + 1) * 1024], gps[:],
                    AF.Silu, scale=DQS),
                    reads=[PS[id(gps)]], writes=[r_sg[half]])
                p.inc("s")
                if hc + 1 < HC:
                    stream_gu(th, hc + 1)
                p.op("v", lambda e, ups=ups, hc=hc, half=half: e.tensor_mul(
                    hf_sb[:, hc * 1024:(hc + 1) * 1024], ups[:],
                    sg_buf[:, half * 1024:(half + 1) * 1024]),
                    reads=[PS[id(ups)], r_sg[half]],
                    writes=[r_hf] + extra)
                p.inc("v")
            stream_wd(th, 0)
            for mc in range(KC):
                half = mc % 2
                dps = (PA, PB)[half]
                for hcc in range(HC):
                    for nf in range(2):
                        p.op("t", lambda e, dps=dps, hcc=hcc, nf=nf,
                             half=half: e.matmul(
                                 dps[:, nf * 512:(nf + 1) * 512],
                                 wd_buf[:, half * FH + hcc * 128:
                                       half * FH + hcc * 128 + 128],
                                 hf_sb[:, hcc * 1024 + nf * 512:
                                      hcc * 1024 + nf * 512 + 512],
                                 start=(hcc == 0), stop=(hcc == HC - 1)),
                             reads=[r_wd[half], r_hf], writes=[PS[id(dps)]])
                p.inc("t")
                if mc + 1 < KC:
                    stream_wd(th, mc + 1)
                xs = x_sb[:, mc * SEQ + th * 1024: mc * SEQ + th * 1024 + 1024]
                p.op("v", lambda e, dps=dps, xs=xs: e.scalar_tensor_tensor(
                    xs, dps[:], DQS * DQS, xs, ALU.mult, ALU.add),
                    reads=[PS[id(dps)], r_xh[th]], writes=[r_xh[th]])
                p.inc("v")

        for c in range(KC):
            p.dma("g", yT[c * 128:(c + 1) * 128, :],
                  x_sb[:, c * SEQ:(c + 1) * SEQ],
                  reads=[r_x, r_xh[0], r_xh[1]])
        for s2 in ("dg0", "dg1", "dq0", "dq1"):
            p._wait("g", (s2, p.count[s2]))

        with nc.Block() as block:
            @block.tensor
            def _(eng):
                for fn in p.stream["t"]:
                    fn(eng)

            @block.vector
            def _(eng):
                for fn in p.stream["v"]:
                    fn(eng)

            @block.scalar
            def _(eng):
                for fn in p.stream["s"]:
                    fn(eng)

            @block.sync
            def _(eng):
                for fn in p.stream["q"]:
                    fn(eng)

            @block.gpsimd
            def _(eng):
                for fn in p.stream["g"]:
                    fn(eng)

    return nc


def prepare_inputs(inputs):
    x = np.asarray(inputs["L0_attn_x"])
    wq = np.asarray(inputs["L0_attn_wq"])
    bq = np.asarray(inputs["L0_attn_bq"])
    wk = np.asarray(inputs["L0_attn_wk"])
    bk = np.asarray(inputs["L0_attn_bk"])
    wv = np.asarray(inputs["L0_attn_wv"])
    bv = np.asarray(inputs["L0_attn_bv"])
    wo = np.asarray(inputs["L0_attn_wo"])
    bo = np.asarray(inputs["L0_attn_bo"])
    wg = np.asarray(inputs["L0_ffn_w_gate"])
    wu = np.asarray(inputs["L0_ffn_w_up"])
    wd = np.asarray(inputs["L0_ffn_w_down"])

    wq_c = wq[PERM_F]
    bq_c = bq[PERM_F]
    perm_kv = np.concatenate([np.arange(64) + k * 64 for k in VT_KV_ORDER])
    wv_c = wv[perm_kv]
    bv_c = bv[perm_kv]
    wo_c = wo[:, PERM_F]

    wq_al = np.ascontiguousarray(
        wq_c.reshape(KC, 128, KC, 128).transpose(3, 2, 0, 1)).reshape(128, KC * DM)
    wk_al = np.ascontiguousarray(
        wk.reshape(2, 128, KC, 128).transpose(3, 2, 0, 1)).reshape(128, KC * DKV)
    wv_al = np.ascontiguousarray(
        wv_c.reshape(DKV, KC, 128).transpose(2, 1, 0)).reshape(128, KC * DKV)
    wo_al = np.ascontiguousarray(
        wo_c.reshape(KC, 128, KC, 128).transpose(3, 2, 0, 1)).reshape(128, KC * DM)
    wg_al = np.ascontiguousarray(
        wg.reshape(HC, 128, KC, 128).transpose(0, 3, 2, 1)).reshape(HC, 128, DM)
    wu_al = np.ascontiguousarray(
        wu.reshape(HC, 128, KC, 128).transpose(0, 3, 2, 1)).reshape(HC, 128, DM)
    wd_al = np.ascontiguousarray(
        wd.reshape(KC, 128, HC, 128).transpose(0, 3, 2, 1)).reshape(KC, 128, FH)

    shared = {
        "wq_a": wq_al, "wk_a": wk_al, "wv_a": wv_al, "wo_a": wo_al,
        "bq_r": np.ascontiguousarray(bq_c.reshape(1, DM)),
        "bk_r": np.ascontiguousarray(bk.reshape(1, DKV)),
        "bv_r": np.ascontiguousarray(bv_c.reshape(1, DKV)),
        "bo_r": np.ascontiguousarray(bo.reshape(1, DM)),
        "wg_a": wg_al, "wu_a": wu_al, "wd_a": wd_al,
    }
    in_maps = []
    for i in range(8):
        m = dict(shared)
        m["xT"] = np.ascontiguousarray(x[i].T)
        in_maps.append(m)
    return in_maps


def kernel(**inputs):
    in_maps = prepare_inputs(inputs)
    nc = build_program()
    res = run_bass_kernel_spmd(nc, in_maps, core_ids=list(range(8)))
    out = np.stack([np.ascontiguousarray(np.asarray(r["yT"]).T)
                    for r in res.results], axis=0)
    return out.astype(np.float32, copy=False)


# version 14
# speedup vs baseline: 1.0559x; 1.0150x over previous
import numpy as np
from contextlib import ExitStack

import concourse.bass as bass
import concourse.mybir as mybir
from concourse.ap import AP
from concourse.bass_utils import run_bass_kernel_spmd

F32 = mybir.dt.float32
BF16 = mybir.dt.bfloat16
I32 = mybir.dt.int32
AF = mybir.ActivationFunctionType
ALU = mybir.AluOpType

SEQ = 2048
DM = 1024
DKV = 256
NH, NKV, DH = 16, 4, 64
FH = 3072
KC = DM // 128
HC = FH // 128
NSC = SEQ // 128
EPS = 1e-5
ATT_SCALE = 0.125
DQS = 1.0 / 127.0

A_HEADS = [0, 1, 2, 3, 8, 9, 10, 11]
B_HEADS = [4, 5, 6, 7, 12, 13, 14, 15]
HEAD_POS = {}
for _c in range(8):
    HEAD_POS[A_HEADS[_c]] = (_c, 0)
    HEAD_POS[B_HEADS[_c]] = (_c, 1)
PERM_F = np.zeros(DM, np.int64)
for _h, (_c, _p) in HEAD_POS.items():
    PERM_F[_c * 128 + _p * 64: _c * 128 + _p * 64 + 64] = np.arange(64) + _h * 64

VT_W = 448
VT_STAT_OFF = {0: 128, 1: 0, 2: 320, 3: 192}
VT_KV_ORDER = [1, 0, 3, 2]


class Rsrc:
    __slots__ = ("name", "w", "r", "dirty_by")

    def __init__(self, name):
        self.name = name
        self.w = []
        self.r = []
        self.dirty_by = None


class Prog:

    serialize = False
    ser_phases = set()
    cur_phase = ""
    trace_log = None

    def __init__(self, nc, sems):
        self.nc = nc
        self.stream = {e: [] for e in ("t", "v", "s", "g", "q")}
        self.sems = sems
        self.count = {k: 0 for k in sems}
        self.floor = {e: {} for e in self.stream}
        self.pending = {e: ([], []) for e in self.stream}
        self.last_op_idx = {e: -1 for e in self.stream}
        self.evclock = {}
        self.eng_sem = {"t": "ts", "v": "vs", "s": "ss"}
        self.dma_sem = {"q": ["dq0", "dq1"], "g": ["dg0", "dg1"]}
        self.dma_rr = {"q": 0, "g": 0}

    def _wait(self, e, ev):
        sem, val = ev
        if self.floor[e].get(sem, 0) >= val:
            return
        assert val <= self.count[sem], (e, sem, val, self.count[sem])
        h = self.sems[sem]
        self.stream[e].append(lambda eng, h=h, v=val: eng.wait_ge(h, v))
        if Prog.trace_log is not None:
            Prog.trace_log.append(("W", e, sem, val, Prog.cur_phase))
        clk = self.evclock.get((sem, val), {})
        f = self.floor[e]
        for s2, v2 in clk.items():
            if f.get(s2, 0) < v2:
                f[s2] = v2
        if f.get(sem, 0) < val:
            f[sem] = val

    def _ser(self, e):
        if not (Prog.serialize or Prog.cur_phase in Prog.ser_phases):
            return
        for sem, cnt in self.count.items():
            if cnt:
                self._wait(e, (sem, cnt))

    def raw(self, e, fn):
        self.stream[e].append(fn)
        self.last_op_idx[e] = len(self.stream[e]) - 1

    def raw_inc(self, e):
        sem = self.eng_sem[e]
        self.count[sem] += 1
        val = self.count[sem]
        h = self.sems[sem]
        i = self.last_op_idx[e]
        assert i == len(self.stream[e]) - 1, (e, i, len(self.stream[e]))
        old_fn = self.stream[e][i]
        self.stream[e][i] = (
            lambda eng, f=old_fn, h=h: f(eng).then_inc(h, 1))
        if Prog.trace_log is not None:
            Prog.trace_log.append(("i", e, sem, val, Prog.cur_phase))
        f = dict(self.floor[e])
        f[sem] = val
        self.evclock[(sem, val)] = f
        return (sem, val)

    def op(self, e, fn, reads=(), writes=()):
        self._ser(e)
        for R in reads:
            assert R.dirty_by in (None, e), f"read of dirty {R.name} on {e}"
            for ev in R.w:
                self._wait(e, ev)
        for R in writes:
            assert R.dirty_by in (None, e), f"write of dirty {R.name} on {e}"
            for ev in R.w:
                self._wait(e, ev)
            for ev in R.r:
                self._wait(e, ev)
        self.stream[e].append(fn)
        self.last_op_idx[e] = len(self.stream[e]) - 1
        pr, pw = self.pending[e]
        pr.extend(reads)
        pw.extend(writes)
        for R in writes:
            R.dirty_by = e
        for R in reads:
            if R.dirty_by is None:
                R.dirty_by = e

    def inc(self, e, drain=None):
        if drain is None:
            drain = (e == "t")
        sem = self.eng_sem[e]
        self.count[sem] += 1
        val = self.count[sem]
        h = self.sems[sem]
        if drain:
            self.stream[e].append(
                lambda eng, h=h: eng.maybe_drain_then_inc((h, 1), fusable=True))
        else:
            i = self.last_op_idx[e]
            assert i == len(self.stream[e]) - 1, (e, i, len(self.stream[e]))
            old_fn = self.stream[e][i]
            self.stream[e][i] = (
                lambda eng, f=old_fn, h=h: f(eng).then_inc(h, 1))
        if Prog.trace_log is not None:
            Prog.trace_log.append(("I", e, sem, val, Prog.cur_phase))
        f = dict(self.floor[e])
        f[sem] = val
        self.evclock[(sem, val)] = f
        ev = (sem, val)
        pr, pw = self.pending[e]
        for R in pw:
            R.w = [ev]
            R.r = []
            R.dirty_by = None
        for R in pr:
            if R.dirty_by == e:
                R.dirty_by = None
            R.r.append(ev)
        self.pending[e] = ([], [])
        return ev

    def dma(self, e, out_ap, in_ap, reads=(), writes=()):
        self._ser(e)
        for R in reads:
            assert R.dirty_by in (None, e)
            for ev in R.w:
                self._wait(e, ev)
        for R in writes:
            assert R.dirty_by in (None, e)
            for ev in R.w + R.r:
                self._wait(e, ev)
        sem = self.dma_sem[e][self.dma_rr[e]]
        self.dma_rr[e] ^= 1
        h = self.sems[sem]
        self.count[sem] += 16
        val = self.count[sem]
        self.stream[e].append(
            lambda eng, h=h, o=out_ap, i=in_ap: eng.dma_start(o, i).then_inc(h, 16))
        if Prog.trace_log is not None:
            Prog.trace_log.append(("D", e, sem, val, Prog.cur_phase))
        f = dict(self.floor[e])
        f[sem] = val
        self.evclock[(sem, val)] = f
        ev = (sem, val)
        for R in writes:
            R.w = [ev]
            R.r = []
        for R in reads:
            R.r.append(ev)
        return ev


def build_program(taps=(), serialize=False, ser_phases=()):
    taps = set(taps)
    Prog.serialize = serialize
    Prog.ser_phases = set(ser_phases)
    Prog.cur_phase = "pre"
    nc = bass.Bass("TRN2", target_bir_lowering=False,
                   dynamic_dma_scratch_size=16384,
                   detect_race_conditions=False)

    def din(name, shape, dtype):
        return nc.dram_tensor(name, list(shape), dtype, kind="ExternalInput").ap()

    xT = din("xT", (DM, SEQ), F32)
    wq_a = din("wq_a", (128, KC * DM), F32)
    wk_a = din("wk_a", (128, KC * DKV), F32)
    wv_a = din("wv_a", (128, KC * DKV), F32)
    wo_a = din("wo_a", (128, KC * DM), F32)
    bq_r = din("bq_r", (1, DM), F32)
    bk_r = din("bk_r", (1, DKV), F32)
    bv_r = din("bv_r", (1, DKV), F32)
    bo_r = din("bo_r", (1, DM), F32)
    wg_a = din("wg_a", (HC, 128, DM), I32)
    wu_a = din("wu_a", (HC, 128, DM), I32)
    wd_a = din("wd_a", (KC, 128, FH), I32)
    yT = nc.dram_tensor("yT", [DM, SEQ], F32, kind="ExternalOutput").ap()

    tap_aps = {}
    for tname in taps:
        shp = {"h1": (DM, SEQ), "q": (DM, SEQ), "k": (DKV, SEQ),
               "vt": (128, NSC * VT_W), "ao": (DM, SEQ), "x1": (DM, SEQ),
               "h2": (DM, SEQ)}[tname]
        dt = F32 if tname == "x1" else BF16
        tap_aps[tname] = nc.dram_tensor("tap_" + tname, list(shp), dt,
                                        kind="ExternalOutput").ap()

    with ExitStack() as ctx:
        base = (nc.sbuf_base + 31) & ~31
        cur = [base]

        offs = {}

        def place(name, shape, dtype, at=None):
            if at is None:
                off = cur[0]
                per = int(np.prod(shape[1:])) * mybir.dt.size(dtype)
                cur[0] = (off + per + 31) & ~31
            else:
                off = at
            h = nc.alloc_sbuf_tensor_at(name, list(shape), dtype, offset=off)
            offs[name] = off
            assert cur[0] <= nc.sbuf_top, (name, cur[0], nc.sbuf_top)
            return h

        x_sb = place("x_sb", [128, KC * SEQ], F32)
        B0 = cur[0]
        h_sb = place("h_sb", [128, KC * SEQ], BF16)
        k_sb = place("k_sb", [128, 2 * SEQ], BF16)
        vt_sb = place("vt_sb", [128, NSC * VT_W], BF16)
        probs_sb = place("probs_sb", [128, 4 * 1024], BF16)
        hf_sb = place("hf_sb", [128, HC * 1024], BF16, at=B0)
        q_sb = place("q_sb", [128, KC * SEQ], BF16)
        W0 = cur[0]
        wq_bf = place("wq_bf", [128, KC * DM], BF16)
        wk_bf = place("wk_bf", [128, KC * DKV], BF16)
        wv_bf = place("wv_bf", [128, KC * DKV], BF16)
        wg_buf = place("wg_buf", [128, 2 * DM], BF16, at=W0)
        wu_buf = place("wu_buf", [128, 2 * DM], BF16, at=W0 + 4096)
        sg_buf = place("sg_buf", [128, 2 * 1024], BF16, at=W0 + 8192)
        wd_buf = place("wd_buf", [128, 2 * FH], BF16, at=W0 + 12288)
        stg32 = place("stg32", [128, 2048], F32)
        stgi = place("stgi", [128, 2048], I32, at=offs["stg32"])
        stgi2 = place("stgi2", [128, 2048], I32, at=W0 + 12288)
        ones_row = place("ones_row", [128, 128], BF16)
        ones_col = place("ones_col", [128, 1], BF16)
        ones_f32 = place("ones_f32", [1, 1], F32)
        eps_c = place("eps_c", [1, 1], F32)
        rowA = place("rowA", [1, 1024], F32)
        rowB = place("rowB", [1, 1024], F32)
        bv_row = place("bv_row", [1, DKV], F32)
        bv_rbf = place("bv_rbf", [1, DKV], BF16)
        bv_bs = place("bv_bs", [128, DKV], BF16)
        m_bf = place("m_bf", [1, SEQ], BF16, at=offs["k_sb"] + 4096)
        rstd_bf = place("rstd_bf", [1, SEQ], BF16, at=offs["vt_sb"] + 8192)
        av_f32 = place("av_f32", [128, 1024], F32, at=offs["rowA"])
        att_rbf = place("att_rbf", [128, 1024], BF16, at=offs["rowB"])
        rb_bf = place("rb_bf", [128, 1024], BF16, at=offs["rowB"] + 2048)
        bias_sb = place("bias_sb", [128, 18], F32)

        PA = ctx.enter_context(nc.psum_tensor("PA", [128, 1024], F32))
        PB = ctx.enter_context(nc.psum_tensor("PB", [128, 1024], F32))
        PC = ctx.enter_context(nc.psum_tensor("PC", [128, 1024], F32))
        PD = ctx.enter_context(nc.psum_tensor("PD", [128, 1024], F32))

        sems = {}
        for sname in ("ts", "vs", "ss", "dq0", "dq1", "dg0", "dg1"):
            sems[sname] = ctx.enter_context(nc.semaphore(sname))
        p = Prog(nc, sems)

        r_x = Rsrc("x")
        r_xh = [Rsrc("xh0"), Rsrc("xh1")]
        r_h = Rsrc("h")
        r_hh = [Rsrc("hh0"), Rsrc("hh1")]
        r_ao = Rsrc("ao")
        r_q = Rsrc("q")
        r_h2 = Rsrc("h2")
        r_k = Rsrc("k")
        r_vt = Rsrc("vt")
        r_probs = [Rsrc(f"probs{i}") for i in range(4)]
        r_hf = Rsrc("hf")
        r_h2h = [Rsrc("h2h0"), Rsrc("h2h1")]
        r_PA, r_PB = Rsrc("PA"), Rsrc("PB")
        r_PC, r_PD = Rsrc("PC"), Rsrc("PD")
        r_wq, r_wk, r_wv, r_wo = Rsrc("wq"), Rsrc("wk"), Rsrc("wv"), Rsrc("wo")
        r_wg = [Rsrc("wg0"), Rsrc("wg1")]
        r_wu = [Rsrc("wu0"), Rsrc("wu1")]
        r_sg = [Rsrc("sg0"), Rsrc("sg1")]
        r_wd = [Rsrc("wd0"), Rsrc("wd1")]
        r_stg = [Rsrc("stg0"), Rsrc("stg1")]
        r_stgi = r_stg
        r_stg4 = [r_stg[0], r_stg[1], Rsrc("stg2"), Rsrc("stg3")]
        r_rowA, r_rowB = Rsrc("rowA"), Rsrc("rowB")
        r_mrow, r_rrow = Rsrc("m_bf"), Rsrc("rstd_bf")
        r_bias, r_bv, r_bvbs, r_rb = (Rsrc("bias"), Rsrc("bv"),
                                      Rsrc("bv_bs"), Rsrc("rb"))
        r_const = Rsrc("const")
        r_xtmp, r_sqtmp, r_t1tmp = Rsrc("xtmp"), Rsrc("sqtmp"), Rsrc("t1tmp")
        PS = {id(PA): r_PA, id(PB): r_PB, id(PC): r_PC, id(PD): r_PD}
        RS_MBS = [r_probs[0], r_probs[1]]
        RS_RBS = [r_probs[2], r_probs[3]]
        m_bs = probs_sb[:, 0:2048]
        rs_bs = probs_sb[:, 2048:4096]

        p.op("v", lambda e: e.memset(ones_row[:], 1.0), writes=[r_const])
        p.op("v", lambda e: e.memset(ones_col[:], 1.0), writes=[r_const])
        p.op("v", lambda e: e.memset(ones_f32[:], 1.0), writes=[r_const])
        p.op("v", lambda e: e.memset(att_rbf[:], 0.0), writes=[r_rowB])
        p.op("v", lambda e: e.memset(eps_c[:], float(EPS)), writes=[r_const])
        p.inc("v")

        xevs = []
        for c in range(KC):
            xevs.append(p.dma("g", x_sb[:, c * SEQ:(c + 1) * SEQ],
                              xT[c * 128:(c + 1) * 128, :]))
        r_x.w = xevs
        r_xh[0].w = list(xevs)
        r_xh[1].w = list(xevs)

        def bias_block(row_src, n_chunks, col0):
            p.dma("q", rowA[0:1, 0:row_src.shape[1]], row_src, writes=[r_rowA])
            for j in range(n_chunks):
                p.op("t", lambda e, j=j, c0=col0: e.matmul(
                    PA[:, c0 + j:c0 + j + 1],
                    rowA[0:1, j * 128:(j + 1) * 128],
                    ones_f32[:], start=True, stop=True),
                    reads=[r_rowA, r_const], writes=[r_PA])
            p.inc("t")
        bias_block(bq_r, 8, 0)
        bias_block(bk_r, 2, 8)
        bias_block(bo_r, 8, 10)
        p.op("s", lambda e: e.copy(bias_sb[:], PA[:, 0:18]),
             reads=[r_PA], writes=[r_bias])
        p.inc("s")

        p.dma("q", bv_row[:], bv_r, writes=[r_bv])
        p.op("v", lambda e: e.tensor_copy(bv_rbf[:], bv_row[:]),
             reads=[r_bv], writes=[r_bv])
        p.inc("v")
        p.op("t", lambda e: e.matmul(PB[:, 0:DKV], ones_row[0:1, 0:128],
                                     bv_rbf[:], start=True, stop=True),
             reads=[r_bv, r_const], writes=[r_PB])
        p.inc("t")
        p.op("s", lambda e: e.copy(bv_bs[:], PB[:, 0:DKV]),
             reads=[r_PB], writes=[r_bvbs])
        p.inc("s")

        def stream32(src_ap, dst_ap, rsrcs, half):
            st = stg32[:, half * 1024:(half + 1) * 1024]
            p.dma("q", st, src_ap, writes=[r_stg[half]])
            p.op("v", lambda e, st=st, d=dst_ap: e.tensor_copy(d, st),
                 reads=[r_stg[half]], writes=list(rsrcs))
            p.inc("v")

        def streami(src_ap, dst_ap, rsrcs, slot, queue="q", extra_dma=()):
            t2 = (stgi, stgi, stgi2, stgi2)[slot]
            st = t2[:, (slot % 2) * 1024:(slot % 2 + 1) * 1024]
            p.dma(queue, st, src_ap,
                  writes=[r_stg4[slot]] + list(extra_dma))
            p.op("v", lambda e, st=st, d=dst_ap: e.tensor_copy(d, st),
                 reads=[r_stg4[slot]], writes=rsrcs)
            p.inc("v")

        for i in range(2):
            stream32(wk_a[:, i * 1024:(i + 1) * 1024],
                     wk_bf[:, i * 1024:(i + 1) * 1024], [r_wk], i % 2)
        for i in range(2):
            stream32(wv_a[:, i * 1024:(i + 1) * 1024],
                     wv_bf[:, i * 1024:(i + 1) * 1024], [r_wv], i % 2)
        for i in range(8):
            stream32(wq_a[:, i * 1024:(i + 1) * 1024],
                     wq_bf[:, i * 1024:(i + 1) * 1024], [r_wq], i % 2)

        def ln_stats_chunk(src_res, Hh, sum_ps, sq_ps, c, extra_scr=()):
            xb = k_sb[:, 0:1024]
            sq = vt_sb[:, 0:1024]
            rsum, rsq = PS[id(sum_ps)], PS[id(sq_ps)]
            w_xb = [r_xtmp] + [r for r in extra_scr if r.name == "k"]
            w_sq = [r_sqtmp] + [r for r in extra_scr if r.name == "vt"]
            xs = x_sb[:, c * SEQ + Hh * 1024: c * SEQ + Hh * 1024 + 1024]
            p.op("v", lambda e, xs=xs: e.tensor_copy(xb, xs),
                 reads=[src_res], writes=w_xb)
            p.inc("v")
            p.op("s", lambda e, xs=xs: e.activation(sq, xs, AF.Square),
                 reads=[src_res], writes=w_sq)
            p.inc("s")
            for piece in range(2):
                off = piece * 512
                p.op("t", lambda e, off=off, c=c, sum_ps=sum_ps: e.matmul(
                    sum_ps[0:1, off:off + 512], ones_col[:],
                    xb[:, off:off + 512],
                    start=(c == 0), stop=(c == KC - 1)),
                    reads=[r_xtmp, r_const], writes=[rsum])
            for piece in range(2):
                off = piece * 512
                p.op("t", lambda e, off=off, c=c, sq_ps=sq_ps: e.matmul(
                    sq_ps[0:1, off:off + 512], ones_col[:],
                    sq[:, off:off + 512],
                    start=(c == 0), stop=(c == KC - 1)),
                    reads=[r_sqtmp, r_const], writes=[rsq])
            p.inc("t")

        def ln_stats(src_res, Hh, sum_ps, sq_ps, extra_scr=()):
            for c in range(KC):
                ln_stats_chunk(src_res, Hh, sum_ps, sq_ps, c, extra_scr)

        def ln_finish(src_res, dst_sb, dst_out_res, Hh, sum_ps, sq_ps,
                      extra_scr=()):
            t1 = vt_sb[:, 2048:3072]
            rsum, rsq = PS[id(sum_ps)], PS[id(sq_ps)]
            w_t1 = [r_t1tmp] + [r for r in extra_scr if r.name == "vt"]
            p.op("v", lambda e: e.tensor_scalar(
                rowA[:], sum_ps[0:1, :], 1.0 / DM, None, ALU.mult),
                reads=[rsum], writes=[r_rowA])
            p.op("v", lambda e: e.tensor_mul(rowB[:], rowA[:], rowA[:]),
                 reads=[r_rowA], writes=[r_rowB])
            p.inc("v")
            p.op("s", lambda e, Hh=Hh: e.copy(
                m_bf[0:1, Hh * 1024:(Hh + 1) * 1024], rowA[:]),
                reads=[r_rowA], writes=[r_mrow, r_k])
            p.inc("s")
            p.op("v", lambda e: e.scalar_tensor_tensor(
                rowA[:], sq_ps[0:1, :], 1.0 / DM, rowB[:],
                ALU.mult, ALU.subtract),
                reads=[rsq, r_rowB], writes=[r_rowA])
            p.inc("v")
            p.op("s", lambda e: e.activation(rowB[:], rowA[:], AF.Ln,
                                             bias=eps_c[:]),
                 reads=[r_rowA, r_const], writes=[r_rowB])
            p.op("s", lambda e, Hh=Hh: e.activation(
                rstd_bf[0:1, Hh * 1024:(Hh + 1) * 1024], rowB[:],
                AF.Exp, scale=-0.5),
                reads=[r_rowB], writes=[r_rrow, r_vt])
            p.inc("s")
            for piece in range(2):
                off = piece * 512
                p.op("t", lambda e, off=off, Hh=Hh: e.matmul(
                    sum_ps[:, off:off + 512], ones_row[0:1, 0:128],
                    m_bf[0:1, Hh * 1024 + off: Hh * 1024 + off + 512],
                    start=True, stop=True),
                    reads=[r_mrow, r_k, r_const], writes=[rsum])
                p.op("t", lambda e, off=off, Hh=Hh: e.matmul(
                    sq_ps[:, off:off + 512], ones_row[0:1, 0:128],
                    rstd_bf[0:1, Hh * 1024 + off: Hh * 1024 + off + 512],
                    start=True, stop=True),
                    reads=[r_rrow, r_vt, r_const], writes=[rsq])
            p.inc("t")
            mb_s = probs_sb[:, 0:1024]
            rb_s = probs_sb[:, 2048:3072]
            p.op("s", lambda e: e.copy(mb_s[:], sum_ps[:]),
                 reads=[rsum], writes=[r_probs[0]])
            p.op("s", lambda e: e.copy(rb_s[:], sq_ps[:]),
                 reads=[rsq], writes=[r_probs[2]])
            p.inc("s")
            for c in range(KC):
                xs = x_sb[:, c * SEQ + Hh * 1024: c * SEQ + Hh * 1024 + 1024]
                p.op("v", lambda e, xs=xs: e.tensor_sub(t1, xs, mb_s),
                     reads=[src_res, r_probs[0]], writes=w_t1)
                p.op("v", lambda e, c=c, Hh=Hh: e.tensor_mul(
                    dst_sb[:, c * SEQ + Hh * 1024: c * SEQ + Hh * 1024 + 1024],
                    t1, rb_s),
                    reads=[r_t1tmp, r_probs[2]], writes=dst_out_res)
                p.inc("v")

        def layernorm(src_hs, dst_sb, dst_out_res, extra_scr=()):
            ln_stats(src_hs[0], 0, PA, PC, extra_scr)
            ln_finish(src_hs[0], dst_sb, dst_out_res, 0, PA, PC, extra_scr)
            ln_stats(src_hs[1], 1, PB, PD, extra_scr)
            ln_finish(src_hs[1], dst_sb, dst_out_res, 1, PB, PD, extra_scr)

        ln_stats(r_xh[0], 0, PA, PC)
        ln_finish(r_xh[0], h_sb, [r_h, r_hh[0]], 0, PA, PC)

        def qproj_mc_half(mc, Hh):
            ps = ((PA, PC) if Hh == 0 else (PB, PD))[mc % 2]
            rp = PS[id(ps)]
            for kc in range(KC):
                for nf in range(2):
                    p.op("t", lambda e, ps=ps, kc=kc, mc=mc, nf=nf, Hh=Hh:
                         e.matmul(
                             ps[:, nf * 512:(nf + 1) * 512],
                             wq_bf[:, kc * 1024 + mc * 128:
                                   kc * 1024 + mc * 128 + 128],
                             h_sb[:, kc * SEQ + Hh * 1024 + nf * 512:
                                  kc * SEQ + Hh * 1024 + nf * 512 + 512],
                             start=(kc == 0), stop=(kc == KC - 1)),
                         reads=[r_wq, r_hh[Hh]], writes=[rp])
            p.inc("t")
            p.op("s", lambda e, ps=ps, mc=mc, Hh=Hh: e.activation(
                q_sb[:, mc * SEQ + Hh * 1024: mc * SEQ + Hh * 1024 + 1024],
                ps[:], AF.Identity, bias=bias_sb[:, mc:mc + 1]),
                reads=[rp, r_bias], writes=[r_q])
            p.inc("s")

        for c in range(KC):
            qproj_mc_half(c, 0)
            ln_stats_chunk(r_xh[1], 1, PB, PD, c)
        ln_finish(r_xh[1], h_sb, [r_h, r_hh[1]], 1, PB, PD)
        for c in range(KC):
            qproj_mc_half(c, 1)
        Prog.cur_phase = "qproj"

        for mc2 in range(2):
            ps_pair = (PA, PB) if mc2 % 2 == 0 else (PC, PD)
            rs_pair = [PS[id(ps_pair[0])], PS[id(ps_pair[1])]]
            for kc in range(KC):
                for nf in range(4):
                    ps = ps_pair[nf // 2]
                    off = (nf % 2) * 512
                    p.op("t", lambda e, ps=ps, off=off, kc=kc, mc2=mc2, nf=nf:
                         e.matmul(
                             ps[:, off:off + 512],
                             wk_bf[:, kc * 256 + mc2 * 128:
                                   kc * 256 + mc2 * 128 + 128],
                             h_sb[:, kc * SEQ + nf * 512:
                                  kc * SEQ + nf * 512 + 512],
                             start=(kc == 0), stop=(kc == KC - 1)),
                         reads=[r_wk, r_h], writes=rs_pair)
            p.inc("t")
            for half in range(2):
                ps = ps_pair[half]
                p.op("s", lambda e, ps=ps, mc2=mc2, half=half: e.activation(
                    k_sb[:, mc2 * SEQ + half * 1024:
                         mc2 * SEQ + half * 1024 + 1024],
                    ps[:], AF.Identity, bias=bias_sb[:, 8 + mc2:9 + mc2]),
                    reads=[rs_pair[half], r_bias], writes=[r_k])
            p.inc("s")

        Prog.cur_phase = "vtok"
        def z_init(e):
            e.memset(AP(vt_sb, 0, [[NSC * VT_W, 128], [VT_W, NSC],
                                   [192, 3], [1, 64]]), 0.0)
        p.op("v", z_init, writes=[r_vt, r_sqtmp, r_t1tmp])
        p.op("v", lambda e: e.memset(
            AP(vt_sb, 0, [[NSC * VT_W, 128], [VT_W, NSC], [192, 3]]), 1.0),
            writes=[r_vt])
        p.inc("v")

        for sc in range(NSC):
            ps = (PA, PB)[sc % 2]
            rp = PS[id(ps)]
            for kc in range(KC):
                p.op("t", lambda e, ps=ps, sc=sc, kc=kc: e.matmul(
                    ps[:, 0:DKV],
                    h_sb[:, kc * SEQ + sc * 128: kc * SEQ + sc * 128 + 128],
                    wv_bf[:, kc * 256:(kc + 1) * 256],
                    start=(kc == 0), stop=(kc == KC - 1)),
                    reads=[r_h, r_wv], writes=[rp])
            p.inc("t")
            p.op("v", lambda e, ps=ps, sc=sc: e.tensor_add(
                AP(vt_sb, sc * VT_W + 64, [[NSC * VT_W, 128], [192, 2], [1, 128]]),
                ps[:, 0:DKV].rearrange("p (s c) -> p s c", s=2),
                bv_bs[:].rearrange("p (s c) -> p s c", s=2)),
                reads=[rp, r_bvbs], writes=[r_vt])
            p.inc("v")

        if "h1" in tap_aps:
            for c in range(KC):
                p.dma("g", tap_aps["h1"][c * 128:(c + 1) * 128, :],
                      h_sb[:, c * SEQ:(c + 1) * SEQ], reads=[r_h])
        if "q" in tap_aps:
            for c in range(KC):
                p.dma("g", tap_aps["q"][c * 128:(c + 1) * 128, :],
                      q_sb[:, c * SEQ:(c + 1) * SEQ], reads=[r_q])
        if "k" in tap_aps:
            for c in range(2):
                p.dma("g", tap_aps["k"][c * 128:(c + 1) * 128, :],
                      k_sb[:, c * SEQ:(c + 1) * SEQ], reads=[r_k])
        if "vt" in tap_aps:
            p.dma("g", tap_aps["vt"], vt_sb[:], reads=[r_vt])

        for i in range(8):
            stream32(wo_a[:, i * 1024:(i + 1) * 1024],
                     wq_bf[:, i * 1024:(i + 1) * 1024],
                     [r_wo, r_wq] if i < 2 else [r_wo], i % 2)

        Prog.cur_phase = "att"
        ev_exp_tail = None
        ev_norm_prev = None
        ev_qk = {}
        for R in (r_PA, r_PB, r_PC, r_PD):
            for ev in R.w + R.r:
                p._wait("t", ev)
        for R in (r_k, r_q, r_vt):
            for ev in R.w:
                p._wait("t", ev)
        for ev in (r_rowA.w + r_rowA.r + r_rowB.w + r_rowB.r +
                   r_ao.w + r_ao.r + r_h.r + r_h.w):
            p._wait("v", ev)

        def ring_of(si):
            return (PA, PB, PD) if si % 2 == 0 else (PA, PB, PC)

        def avbuf_of(si):
            return PC if si % 2 == 0 else PD

        def t_drain_inc():
            val = p.count["ts"] + 1
            p.count["ts"] = val
            p.stream["t"].append(
                lambda eng, h2=p.sems["ts"]: eng.maybe_drain_then_inc(
                    (h2, 1), fusable=True))
            fcl = dict(p.floor["t"])
            fcl["ts"] = val
            p.evclock[("ts", val)] = fcl
            return ("ts", val)

        slots = [(h, qh) for h in range(NH) for qh in range(2)]

        def emit_qk(slot_idx, g):
            h, qh = slots[slot_idx]
            cq, par = HEAD_POS[h]
            kc2 = (h // 4) // 2
            ps = ring_of(slot_idx)[g % 3]
            for nf in range(2):
                p.raw("t", lambda e, ps=ps, sc=g, nf=nf, cq=cq, par=par,
                      kc2=kc2, qh=qh: e.matmul(
                          ps[:, nf * 512:(nf + 1) * 512],
                          k_sb[par * 64: par * 64 + 64,
                               kc2 * SEQ + sc * 128: kc2 * SEQ + sc * 128 + 128],
                          q_sb[par * 64: par * 64 + 64,
                               cq * SEQ + qh * 1024 + nf * 512:
                               cq * SEQ + qh * 1024 + nf * 512 + 512],
                          start=True, stop=True))
            ev_qk[(slot_idx, g)] = p.raw_inc("t")

        ev_ave_par = {0: None, 1: None}
        ev_norm_final = None
        for si, (h, qh) in enumerate(slots):
            cq, par = HEAD_POS[h]
            kv = h // 4
            so = VT_STAT_OFF[kv]
            AVB = avbuf_of(si)
            ev_exp = {}
            ev_av = {}
            if si == 0:
                for g0 in (0, 1, 2):
                    emit_qk(0, g0)
            for g in range(NSC):
                lag = ev_qk.get((si, g + 1)) or ev_av[g - 2]
                p._wait("s", lag)
                ps = ring_of(si)[g % 3]
                pb = probs_sb[:, (g % 4) * 1024:(g % 4 + 1) * 1024]
                p.raw("s", lambda e, ps=ps, pb=pb: e.activation(
                    pb, ps[:], AF.Exp, scale=ATT_SCALE))
                ev_exp[g] = p.raw_inc("s")
                if g + 3 < NSC:
                    p._wait("t", ev_exp[g])
                    emit_qk(si, g + 3)
                p._wait("t", ev_exp[g])
                if g == 0 and ev_ave_par[si % 2] is not None:
                    p._wait("t", ev_ave_par[si % 2])
                for nf in range(2):
                    p.raw("t", lambda e, g=g, nf=nf, so=so, AVB=AVB: e.matmul(
                        AVB[:, nf * 512:(nf + 1) * 512],
                        vt_sb[:, g * VT_W + so: g * VT_W + so + 128],
                        probs_sb[:, (g % 4) * 1024 + nf * 512:
                                 (g % 4) * 1024 + nf * 512 + 512],
                        start=(g == 0), stop=(g == NSC - 1)))
                ev_av[g] = p.raw_inc("t")
            if si + 1 < len(slots):
                p._wait("t", ev_exp[NSC - 1])
                emit_qk(si + 1, 0)
                emit_qk(si + 1, 1)
                ev_lag_tail = ev_qk[(si + 1, 1)]
            else:
                ev_lag_tail = t_drain_inc()
            srow = 64 * (1 - par)
            drow = par * 64
            p._wait("s", ev_lag_tail)
            if ev_norm_final is not None:
                p._wait("s", ev_norm_final)
            p.raw("s", lambda e, AVB=AVB: e.copy(av_f32[:], AVB[:]))
            ev_ave = p.raw_inc("s")
            ev_ave_par[si % 2] = ev_ave
            if si + 1 < len(slots):
                p._wait("t", ev_ave)
                emit_qk(si + 1, 2)
            p._wait("v", ev_ave)
            p.raw("v", lambda e, sr=srow: e.reciprocal(
                av_f32[sr:sr + 1, :], av_f32[sr:sr + 1, :]))
            p.raw("v", lambda e, sr=srow: e.tensor_copy(
                att_rbf[sr:sr + 1, :], av_f32[sr:sr + 1, :]))
            p.raw("v", lambda e, sr=srow, dr=drow: e.stream_shuffle(
                att_rbf[dr:dr + 64, :], att_rbf[sr:sr + 64, :], [0] * 32))
            p.raw("v", lambda e, dr=drow: e.stream_shuffle(
                att_rbf[dr + 32:dr + 64, :], att_rbf[dr:dr + 32, :], [0] * 32))
            p.raw("v", lambda e, dr=drow, cq=cq, qh=qh: e.tensor_mul(
                h_sb[dr:dr + 64,
                     cq * SEQ + qh * 1024: cq * SEQ + qh * 1024 + 1024],
                av_f32[dr:dr + 64, :], att_rbf[dr:dr + 64, :]))
            ev_norm_final = p.raw_inc("v")
            ev_exp_tail = ev_exp[NSC - 1]

        ev_fin = ev_norm_final
        p._wait("t", ev_fin)
        for R in (r_PA, r_PB, r_PC, r_PD, r_rowA, r_rowB, r_rb):
            R.w = [ev_fin]
            R.r = []
            R.dirty_by = None
        r_ao.w = [ev_fin]
        r_ao.r = []
        r_h.w = [ev_fin]
        r_h.r = []
        for R in (r_q, r_k, r_vt):
            R.r = [ev_fin]
        for R in r_probs:
            R.w = [ev_fin]
            R.r = []

        if "ao" in tap_aps:
            for c in range(KC):
                p.dma("g", tap_aps["ao"][c * 128:(c + 1) * 128, :],
                      h_sb[:, c * SEQ:(c + 1) * SEQ], reads=[r_ao])

        Prog.cur_phase = "wo"

        def wo_half(Hh):
            for mc in range(KC):
                ps = (PA, PB)[mc % 2]
                rp = PS[id(ps)]
                for kc in range(KC):
                    for nf in range(2):
                        p.op("t", lambda e, ps=ps, kc=kc, mc=mc, nf=nf,
                             Hh=Hh: e.matmul(
                                 ps[:, nf * 512:(nf + 1) * 512],
                                 wq_bf[:, kc * 1024 + mc * 128:
                                       kc * 1024 + mc * 128 + 128],
                                 h_sb[:, kc * SEQ + Hh * 1024 + nf * 512:
                                      kc * SEQ + Hh * 1024 + nf * 512 + 512],
                                 start=(kc == 0), stop=(kc == KC - 1)),
                             reads=[r_wo, r_ao], writes=[rp])
                p.inc("t")
                xs = x_sb[:, mc * SEQ + Hh * 1024: mc * SEQ + Hh * 1024 + 1024]
                p.op("v", lambda e, ps=ps, xs=xs, mc=mc: e.scalar_tensor_tensor(
                    xs, ps[:], bias_sb[:, 10 + mc:11 + mc], xs,
                    ALU.add, ALU.add),
                    reads=[rp, r_bias, r_xh[Hh]], writes=[r_xh[Hh]])
                p.inc("v")

        wo_half(0)
        Prog.cur_phase = "ln2"
        def wo_mc(Hh, mc):
            ps = (PA, PB)[mc % 2]
            rp = PS[id(ps)]
            for kc in range(KC):
                for nf in range(2):
                    p.op("t", lambda e, ps=ps, kc=kc, mc=mc, nf=nf,
                         Hh=Hh: e.matmul(
                             ps[:, nf * 512:(nf + 1) * 512],
                             wq_bf[:, kc * 1024 + mc * 128:
                                   kc * 1024 + mc * 128 + 128],
                             h_sb[:, kc * SEQ + Hh * 1024 + nf * 512:
                                  kc * SEQ + Hh * 1024 + nf * 512 + 512],
                             start=(kc == 0), stop=(kc == KC - 1)),
                         reads=[r_wo, r_ao], writes=[rp])
            p.inc("t")
            xs = x_sb[:, mc * SEQ + Hh * 1024: mc * SEQ + Hh * 1024 + 1024]
            p.op("v", lambda e, ps=ps, xs=xs, mc=mc: e.scalar_tensor_tensor(
                xs, ps[:], bias_sb[:, 10 + mc:11 + mc], xs,
                ALU.add, ALU.add),
                reads=[rp, r_bias, r_xh[Hh]], writes=[r_xh[Hh]])
            p.inc("v")

        for c in range(KC):
            wo_mc(1, c)
            ln_stats_chunk(r_xh[0], 0, PC, PD, c, extra_scr=[r_k, r_vt])
        ln_finish(r_xh[0], q_sb, [r_h2, r_h2h[0], r_q], 0, PC, PD,
                  extra_scr=[r_k, r_vt])
        ln_stats(r_xh[1], 1, PC, PD, extra_scr=[r_k, r_vt])
        ln_finish(r_xh[1], q_sb, [r_h2, r_h2h[1], r_q], 1, PC, PD,
                  extra_scr=[r_k, r_vt])
        Prog.cur_phase = "ffn"

        if "h2" in tap_aps:
            for c in range(KC):
                p.dma("g", tap_aps["h2"][c * 128:(c + 1) * 128, :],
                      q_sb[:, c * SEQ:(c + 1) * SEQ], reads=[r_h2])

        first_ffn = [r_ao, r_h, r_k, r_vt]

        def stream_gu(th, hc):
            half = hc % 2
            fw = [r_wo] if (th == 0 and hc < 2) else []
            fd = (([r_wo, r_wk, r_wv] if th == 0 else
                   [r_wd[0], r_wd[1]]) if hc < 4 else [])
            streami(wg_a[hc], wg_buf[:, half * 1024:(half + 1) * 1024],
                    [r_wg[half]] + fw, (0, 2)[hc % 2], queue="q",
                    extra_dma=fd)
            streami(wu_a[hc], wu_buf[:, half * 1024:(half + 1) * 1024],
                    [r_wu[half]] + fw, (1, 3)[hc % 2], queue="g",
                    extra_dma=fd)

        def stream_wd(th, mc):
            half = mc % 2
            fwd = [r_wo, r_wk, r_wv] if (th == 0 and mc < 2) else []
            for j in range(3):
                ovl = []
                byte0 = half * 6144 + j * 2048
                if byte0 < 4096:
                    ovl.append(r_stg4[2])
                if byte0 + 2048 > 4096 and byte0 < 8192:
                    ovl.append(r_stg4[3])
                streami(wd_a[mc][:, j * 1024:(j + 1) * 1024],
                        wd_buf[:, half * FH + j * 1024:
                              half * FH + j * 1024 + 1024],
                        [r_wd[half]] + fwd + ovl, j % 2,
                        queue="q" if j % 2 == 0 else "g")

        for th in range(2):
            stream_gu(th, 0)
            for hc in range(HC):
                half = hc % 2
                extra = first_ffn if (th == 0 and hc < 2) else []
                gps = (PA, PB)[half]
                ups = (PC, PD)[half]
                for kc in range(KC):
                    for nf in range(2):
                        p.op("t", lambda e, gps=gps, kc=kc, nf=nf, th=th,
                             half=half: e.matmul(
                                 gps[:, nf * 512:(nf + 1) * 512],
                                 wg_buf[:, half * 1024 + kc * 128:
                                       half * 1024 + kc * 128 + 128],
                                 q_sb[:, kc * SEQ + th * 1024 + nf * 512:
                                      kc * SEQ + th * 1024 + nf * 512 + 512],
                                 start=(kc == 0), stop=(kc == KC - 1)),
                             reads=[r_wg[half], r_h2h[th]], writes=[PS[id(gps)]])
                for kc in range(KC):
                    for nf in range(2):
                        p.op("t", lambda e, ups=ups, kc=kc, nf=nf, th=th,
                             half=half: e.matmul(
                                 ups[:, nf * 512:(nf + 1) * 512],
                                 wu_buf[:, half * 1024 + kc * 128:
                                       half * 1024 + kc * 128 + 128],
                                 q_sb[:, kc * SEQ + th * 1024 + nf * 512:
                                      kc * SEQ + th * 1024 + nf * 512 + 512],
                                 start=(kc == 0), stop=(kc == KC - 1)),
                             reads=[r_wu[half], r_h2h[th]], writes=[PS[id(ups)]])
                p.inc("t")
                p.op("s", lambda e, gps=gps, half=half: e.activation(
                    sg_buf[:, half * 1024:(half + 1) * 1024], gps[:],
                    AF.Silu, scale=DQS),
                    reads=[PS[id(gps)]], writes=[r_sg[half]])
                p.inc("s")
                if hc + 1 < HC:
                    stream_gu(th, hc + 1)
                p.op("v", lambda e, ups=ups, hc=hc, half=half: e.tensor_mul(
                    hf_sb[:, hc * 1024:(hc + 1) * 1024], ups[:],
                    sg_buf[:, half * 1024:(half + 1) * 1024]),
                    reads=[PS[id(ups)], r_sg[half]],
                    writes=[r_hf] + extra)
                p.inc("v")
            stream_wd(th, 0)
            for mc in range(KC):
                half = mc % 2
                dps = (PA, PB)[half]
                for hcc in range(HC):
                    for nf in range(2):
                        p.op("t", lambda e, dps=dps, hcc=hcc, nf=nf,
                             half=half: e.matmul(
                                 dps[:, nf * 512:(nf + 1) * 512],
                                 wd_buf[:, half * FH + hcc * 128:
                                       half * FH + hcc * 128 + 128],
                                 hf_sb[:, hcc * 1024 + nf * 512:
                                      hcc * 1024 + nf * 512 + 512],
                                 start=(hcc == 0), stop=(hcc == HC - 1)),
                             reads=[r_wd[half], r_hf], writes=[PS[id(dps)]])
                p.inc("t")
                if mc + 1 < KC:
                    stream_wd(th, mc + 1)
                xs = x_sb[:, mc * SEQ + th * 1024: mc * SEQ + th * 1024 + 1024]
                p.op("v", lambda e, dps=dps, xs=xs: e.scalar_tensor_tensor(
                    xs, dps[:], DQS * DQS, xs, ALU.mult, ALU.add),
                    reads=[PS[id(dps)], r_xh[th]], writes=[r_xh[th]])
                p.inc("v")

        for c in range(KC):
            p.dma("g", yT[c * 128:(c + 1) * 128, :],
                  x_sb[:, c * SEQ:(c + 1) * SEQ],
                  reads=[r_x, r_xh[0], r_xh[1]])
        for s2 in ("dg0", "dg1", "dq0", "dq1"):
            p._wait("g", (s2, p.count[s2]))

        with nc.Block() as block:
            @block.tensor
            def _(eng):
                for fn in p.stream["t"]:
                    fn(eng)

            @block.vector
            def _(eng):
                for fn in p.stream["v"]:
                    fn(eng)

            @block.scalar
            def _(eng):
                for fn in p.stream["s"]:
                    fn(eng)

            @block.sync
            def _(eng):
                for fn in p.stream["q"]:
                    fn(eng)

            @block.gpsimd
            def _(eng):
                for fn in p.stream["g"]:
                    fn(eng)

    return nc


def prepare_inputs(inputs):
    x = np.asarray(inputs["L0_attn_x"])
    wq = np.asarray(inputs["L0_attn_wq"])
    bq = np.asarray(inputs["L0_attn_bq"])
    wk = np.asarray(inputs["L0_attn_wk"])
    bk = np.asarray(inputs["L0_attn_bk"])
    wv = np.asarray(inputs["L0_attn_wv"])
    bv = np.asarray(inputs["L0_attn_bv"])
    wo = np.asarray(inputs["L0_attn_wo"])
    bo = np.asarray(inputs["L0_attn_bo"])
    wg = np.asarray(inputs["L0_ffn_w_gate"])
    wu = np.asarray(inputs["L0_ffn_w_up"])
    wd = np.asarray(inputs["L0_ffn_w_down"])

    wq_c = wq[PERM_F]
    bq_c = bq[PERM_F]
    perm_kv = np.concatenate([np.arange(64) + k * 64 for k in VT_KV_ORDER])
    wv_c = wv[perm_kv]
    bv_c = bv[perm_kv]
    wo_c = wo[:, PERM_F]

    wq_al = np.ascontiguousarray(
        wq_c.reshape(KC, 128, KC, 128).transpose(3, 2, 0, 1)).reshape(128, KC * DM)
    wk_al = np.ascontiguousarray(
        wk.reshape(2, 128, KC, 128).transpose(3, 2, 0, 1)).reshape(128, KC * DKV)
    wv_al = np.ascontiguousarray(
        wv_c.reshape(DKV, KC, 128).transpose(2, 1, 0)).reshape(128, KC * DKV)
    wo_al = np.ascontiguousarray(
        wo_c.reshape(KC, 128, KC, 128).transpose(3, 2, 0, 1)).reshape(128, KC * DM)
    wg_al = np.ascontiguousarray(
        wg.reshape(HC, 128, KC, 128).transpose(0, 3, 2, 1)).reshape(HC, 128, DM)
    wu_al = np.ascontiguousarray(
        wu.reshape(HC, 128, KC, 128).transpose(0, 3, 2, 1)).reshape(HC, 128, DM)
    wd_al = np.ascontiguousarray(
        wd.reshape(KC, 128, HC, 128).transpose(0, 3, 2, 1)).reshape(KC, 128, FH)

    shared = {
        "wq_a": wq_al, "wk_a": wk_al, "wv_a": wv_al, "wo_a": wo_al,
        "bq_r": np.ascontiguousarray(bq_c.reshape(1, DM)),
        "bk_r": np.ascontiguousarray(bk.reshape(1, DKV)),
        "bv_r": np.ascontiguousarray(bv_c.reshape(1, DKV)),
        "bo_r": np.ascontiguousarray(bo.reshape(1, DM)),
        "wg_a": wg_al, "wu_a": wu_al, "wd_a": wd_al,
    }
    in_maps = []
    for i in range(8):
        m = dict(shared)
        m["xT"] = np.ascontiguousarray(x[i].T)
        in_maps.append(m)
    return in_maps


def kernel(**inputs):
    in_maps = prepare_inputs(inputs)
    nc = build_program()
    res = run_bass_kernel_spmd(nc, in_maps, core_ids=list(range(8)))
    out = np.stack([np.ascontiguousarray(np.asarray(r["yT"]).T)
                    for r in res.results], axis=0)
    return out.astype(np.float32, copy=False)


# version 16
# speedup vs baseline: 1.3363x; 1.2656x over previous
import numpy as np
from contextlib import ExitStack

import concourse.bass as bass
import concourse.mybir as mybir
from concourse.ap import AP
from concourse.bass_utils import run_bass_kernel_spmd

F32 = mybir.dt.float32
BF16 = mybir.dt.bfloat16
I32 = mybir.dt.int32
AF = mybir.ActivationFunctionType
ALU = mybir.AluOpType

SEQ = 2048
DM = 1024
DKV = 256
NH, NKV, DH = 16, 4, 64
FH = 3072
KC = DM // 128
HC = FH // 128
NSC = SEQ // 128
EPS = 1e-5
ATT_SCALE = 0.125
DQS = 1.0 / 127.0

A_HEADS = [0, 1, 2, 3, 8, 9, 10, 11]
B_HEADS = [4, 5, 6, 7, 12, 13, 14, 15]
HEAD_POS = {}
for _c in range(8):
    HEAD_POS[A_HEADS[_c]] = (_c, 0)
    HEAD_POS[B_HEADS[_c]] = (_c, 1)
PERM_F = np.zeros(DM, np.int64)
for _h, (_c, _p) in HEAD_POS.items():
    PERM_F[_c * 128 + _p * 64: _c * 128 + _p * 64 + 64] = np.arange(64) + _h * 64

VT_W = 448
VT_STAT_OFF = {0: 128, 1: 0, 2: 320, 3: 192}
VT_KV_ORDER = [1, 0, 3, 2]


class Rsrc:
    __slots__ = ("name", "w", "r", "dirty_by")

    def __init__(self, name):
        self.name = name
        self.w = []
        self.r = []
        self.dirty_by = None


class Prog:

    serialize = False
    ser_phases = set()
    cur_phase = ""
    trace_log = None

    def __init__(self, nc, sems):
        self.nc = nc
        self.stream = {e: [] for e in ("t", "v", "s", "g", "q")}
        self.sems = sems
        self.count = {k: 0 for k in sems}
        self.floor = {e: {} for e in self.stream}
        self.pending = {e: ([], []) for e in self.stream}
        self.last_op_idx = {e: -1 for e in self.stream}
        self.evclock = {}
        self.eng_sem = {"t": "ts", "v": "vs", "s": "ss"}
        self.dma_sem = {"q": ["dq0", "dq1"], "g": ["dg0", "dg1"]}
        self.dma_rr = {"q": 0, "g": 0}

    def _wait(self, e, ev):
        sem, val = ev
        if self.floor[e].get(sem, 0) >= val:
            return
        assert val <= self.count[sem], (e, sem, val, self.count[sem])
        h = self.sems[sem]
        self.stream[e].append(lambda eng, h=h, v=val: eng.wait_ge(h, v))
        if Prog.trace_log is not None:
            Prog.trace_log.append(("W", e, sem, val, Prog.cur_phase))
        clk = self.evclock.get((sem, val), {})
        f = self.floor[e]
        for s2, v2 in clk.items():
            if f.get(s2, 0) < v2:
                f[s2] = v2
        if f.get(sem, 0) < val:
            f[sem] = val

    def _ser(self, e):
        if not (Prog.serialize or Prog.cur_phase in Prog.ser_phases):
            return
        for sem, cnt in self.count.items():
            if cnt:
                self._wait(e, (sem, cnt))

    def raw(self, e, fn):
        self.stream[e].append(fn)
        self.last_op_idx[e] = len(self.stream[e]) - 1

    def raw_inc(self, e):
        sem = self.eng_sem[e]
        self.count[sem] += 1
        val = self.count[sem]
        h = self.sems[sem]
        i = self.last_op_idx[e]
        assert i == len(self.stream[e]) - 1, (e, i, len(self.stream[e]))
        old_fn = self.stream[e][i]
        self.stream[e][i] = (
            lambda eng, f=old_fn, h=h: f(eng).then_inc(h, 1))
        if Prog.trace_log is not None:
            Prog.trace_log.append(("i", e, sem, val, Prog.cur_phase))
        f = dict(self.floor[e])
        f[sem] = val
        self.evclock[(sem, val)] = f
        return (sem, val)

    def op(self, e, fn, reads=(), writes=()):
        self._ser(e)
        for R in reads:
            assert R.dirty_by in (None, e), f"read of dirty {R.name} on {e}"
            for ev in R.w:
                self._wait(e, ev)
        for R in writes:
            assert R.dirty_by in (None, e), f"write of dirty {R.name} on {e}"
            for ev in R.w:
                self._wait(e, ev)
            for ev in R.r:
                self._wait(e, ev)
        self.stream[e].append(fn)
        self.last_op_idx[e] = len(self.stream[e]) - 1
        pr, pw = self.pending[e]
        pr.extend(reads)
        pw.extend(writes)
        for R in writes:
            R.dirty_by = e
        for R in reads:
            if R.dirty_by is None:
                R.dirty_by = e

    def inc(self, e, drain=None):
        if drain is None:
            drain = (e == "t")
        sem = self.eng_sem[e]
        self.count[sem] += 1
        val = self.count[sem]
        h = self.sems[sem]
        if drain:
            self.stream[e].append(
                lambda eng, h=h: eng.maybe_drain_then_inc((h, 1), fusable=True))
        else:
            i = self.last_op_idx[e]
            assert i == len(self.stream[e]) - 1, (e, i, len(self.stream[e]))
            old_fn = self.stream[e][i]
            self.stream[e][i] = (
                lambda eng, f=old_fn, h=h: f(eng).then_inc(h, 1))
        if Prog.trace_log is not None:
            Prog.trace_log.append(("I", e, sem, val, Prog.cur_phase))
        f = dict(self.floor[e])
        f[sem] = val
        self.evclock[(sem, val)] = f
        ev = (sem, val)
        pr, pw = self.pending[e]
        for R in pw:
            R.w = [ev]
            R.r = []
            R.dirty_by = None
        for R in pr:
            if R.dirty_by == e:
                R.dirty_by = None
            R.r.append(ev)
        self.pending[e] = ([], [])
        return ev

    def dma(self, e, out_ap, in_ap, reads=(), writes=()):
        self._ser(e)
        for R in reads:
            assert R.dirty_by in (None, e)
            for ev in R.w:
                self._wait(e, ev)
        for R in writes:
            assert R.dirty_by in (None, e)
            for ev in R.w + R.r:
                self._wait(e, ev)
        sem = self.dma_sem[e][self.dma_rr[e]]
        self.dma_rr[e] ^= 1
        h = self.sems[sem]
        self.count[sem] += 16
        val = self.count[sem]
        self.stream[e].append(
            lambda eng, h=h, o=out_ap, i=in_ap: eng.dma_start(o, i).then_inc(h, 16))
        if Prog.trace_log is not None:
            Prog.trace_log.append(("D", e, sem, val, Prog.cur_phase))
        f = dict(self.floor[e])
        f[sem] = val
        self.evclock[(sem, val)] = f
        ev = (sem, val)
        for R in writes:
            R.w = [ev]
            R.r = []
        for R in reads:
            R.r.append(ev)
        return ev


def build_program(taps=(), serialize=False, ser_phases=()):
    taps = set(taps)
    Prog.serialize = serialize
    Prog.ser_phases = set(ser_phases)
    Prog.cur_phase = "pre"
    nc = bass.Bass("TRN2", target_bir_lowering=False,
                   dynamic_dma_scratch_size=16384,
                   detect_race_conditions=False)

    def din(name, shape, dtype):
        return nc.dram_tensor(name, list(shape), dtype, kind="ExternalInput").ap()

    xT = din("xT", (DM, SEQ), F32)
    wq_a = din("wq_a", (128, KC * DM), F32)
    wk_a = din("wk_a", (128, KC * DKV), F32)
    wv_a = din("wv_a", (128, KC * DKV), F32)
    wo_a = din("wo_a", (128, KC * DM), F32)
    bq_r = din("bq_r", (1, DM), F32)
    bk_r = din("bk_r", (1, DKV), F32)
    bv_r = din("bv_r", (1, DKV), F32)
    bo_r = din("bo_r", (1, DM), F32)
    wg_a = din("wg_a", (HC, 128, DM), I32)
    wu_a = din("wu_a", (HC, 128, DM), I32)
    wd_a = din("wd_a", (KC, 128, FH), I32)
    yT = nc.dram_tensor("yT", [DM, SEQ], F32, kind="ExternalOutput").ap()

    tap_aps = {}
    for tname in taps:
        shp = {"h1": (DM, SEQ), "q": (DM, SEQ), "k": (DKV, SEQ),
               "vt": (128, NSC * VT_W), "ao": (DM, SEQ), "x1": (DM, SEQ),
               "h2": (DM, SEQ)}[tname]
        dt = F32 if tname == "x1" else BF16
        tap_aps[tname] = nc.dram_tensor("tap_" + tname, list(shp), dt,
                                        kind="ExternalOutput").ap()

    with ExitStack() as ctx:
        base = (nc.sbuf_base + 31) & ~31
        cur = [base]

        offs = {}

        def place(name, shape, dtype, at=None):
            if at is None:
                off = cur[0]
                per = int(np.prod(shape[1:])) * mybir.dt.size(dtype)
                cur[0] = (off + per + 31) & ~31
            else:
                off = at
            h = nc.alloc_sbuf_tensor_at(name, list(shape), dtype, offset=off)
            offs[name] = off
            assert cur[0] <= nc.sbuf_top, (name, cur[0], nc.sbuf_top)
            return h

        x_sb = place("x_sb", [128, KC * SEQ], F32)
        B0 = cur[0]
        h_sb = place("h_sb", [128, KC * SEQ], BF16)
        k_sb = place("k_sb", [128, 2 * SEQ], BF16)
        vt_sb = place("vt_sb", [128, NSC * VT_W], BF16)
        probs_sb = place("probs_sb", [128, 4 * 1024], BF16)
        hf_sb = place("hf_sb", [128, HC * 1024], BF16, at=B0)
        q_sb = place("q_sb", [128, KC * SEQ], BF16)
        W0 = cur[0]
        wq_bf = place("wq_bf", [128, KC * DM], BF16)
        wk_bf = place("wk_bf", [128, KC * DKV], BF16)
        wv_bf = place("wv_bf", [128, KC * DKV], BF16)
        wg_buf = place("wg_buf", [128, 2 * DM], BF16, at=W0)
        wu_buf = place("wu_buf", [128, 2 * DM], BF16, at=W0 + 4096)
        sg_buf = place("sg_buf", [128, 2 * 1024], BF16, at=W0 + 8192)
        wd_buf = place("wd_buf", [128, 2 * FH], BF16, at=W0 + 12288)
        stg32 = place("stg32", [128, 2048], F32)
        stgi = place("stgi", [128, 2048], I32, at=offs["stg32"])
        stgi2 = place("stgi2", [128, 2048], I32, at=W0 + 12288)
        ones_row = place("ones_row", [128, 128], BF16)
        ones_col = place("ones_col", [128, 1], BF16)
        ones_f32 = place("ones_f32", [1, 1], F32)
        eps_c = place("eps_c", [1, 1], F32)
        rowA = place("rowA", [1, 1024], F32)
        rowB = place("rowB", [1, 1024], F32)
        bv_row = place("bv_row", [1, DKV], F32)
        bv_rbf = place("bv_rbf", [1, DKV], BF16)
        bv_bs = place("bv_bs", [128, DKV], BF16)
        m_bf = place("m_bf", [1, SEQ], BF16, at=offs["k_sb"] + 4096)
        rstd_bf = place("rstd_bf", [1, SEQ], BF16, at=offs["vt_sb"] + 8192)
        av_f32 = place("av_f32", [128, 1024], F32, at=offs["rowA"])
        att_rbf = place("att_rbf", [128, 1024], BF16, at=offs["rowB"])
        rb_bf = place("rb_bf", [128, 1024], BF16, at=offs["rowB"] + 2048)
        bias_sb = place("bias_sb", [128, 18], F32)

        PA = ctx.enter_context(nc.psum_tensor("PA", [128, 1024], F32))
        PB = ctx.enter_context(nc.psum_tensor("PB", [128, 1024], F32))
        PC = ctx.enter_context(nc.psum_tensor("PC", [128, 1024], F32))
        PD = ctx.enter_context(nc.psum_tensor("PD", [128, 1024], F32))

        sems = {}
        for sname in ("ts", "vs", "ss", "dq0", "dq1", "dg0", "dg1"):
            sems[sname] = ctx.enter_context(nc.semaphore(sname))
        p = Prog(nc, sems)

        r_x = Rsrc("x")
        r_h = Rsrc("h")
        r_ao = Rsrc("ao")
        r_q = Rsrc("q")
        r_h2 = Rsrc("h2")
        r_k = Rsrc("k")
        r_vt = Rsrc("vt")
        r_probs = [Rsrc(f"probs{i}") for i in range(4)]
        r_hf = Rsrc("hf")
        r_PA, r_PB = Rsrc("PA"), Rsrc("PB")
        r_PC, r_PD = Rsrc("PC"), Rsrc("PD")
        r_wq, r_wk, r_wv, r_wo = Rsrc("wq"), Rsrc("wk"), Rsrc("wv"), Rsrc("wo")
        r_wg = [Rsrc("wg0"), Rsrc("wg1")]
        r_wu = [Rsrc("wu0"), Rsrc("wu1")]
        r_sg = [Rsrc("sg0"), Rsrc("sg1")]
        r_wd = [Rsrc("wd0"), Rsrc("wd1")]
        r_stg = [Rsrc("stg0"), Rsrc("stg1")]
        r_stgi = r_stg
        r_stg4 = [r_stg[0], r_stg[1], Rsrc("stg2"), Rsrc("stg3")]
        r_rowA, r_rowB = Rsrc("rowA"), Rsrc("rowB")
        r_mrow, r_rrow = Rsrc("m_bf"), Rsrc("rstd_bf")
        r_bias, r_bv, r_bvbs, r_rb = (Rsrc("bias"), Rsrc("bv"),
                                      Rsrc("bv_bs"), Rsrc("rb"))
        r_const = Rsrc("const")
        r_xtmp, r_sqtmp, r_t1tmp = Rsrc("xtmp"), Rsrc("sqtmp"), Rsrc("t1tmp")
        PS = {id(PA): r_PA, id(PB): r_PB, id(PC): r_PC, id(PD): r_PD}
        RS_MBS = [r_probs[0], r_probs[1]]
        RS_RBS = [r_probs[2], r_probs[3]]
        m_bs = probs_sb[:, 0:2048]
        rs_bs = probs_sb[:, 2048:4096]

        p.op("v", lambda e: e.memset(ones_row[:], 1.0), writes=[r_const])
        p.op("v", lambda e: e.memset(ones_col[:], 1.0), writes=[r_const])
        p.op("v", lambda e: e.memset(ones_f32[:], 1.0), writes=[r_const])
        p.op("v", lambda e: e.memset(att_rbf[:], 0.0), writes=[r_rowB])
        p.op("v", lambda e: e.memset(eps_c[:], float(EPS)), writes=[r_const])
        p.inc("v")

        xevs = []
        for c in range(KC):
            xevs.append(p.dma("g", x_sb[:, c * SEQ:(c + 1) * SEQ],
                              xT[c * 128:(c + 1) * 128, :]))
        r_x.w = xevs

        def bias_block(row_src, n_chunks, col0):
            p.dma("q", rowA[0:1, 0:row_src.shape[1]], row_src, writes=[r_rowA])
            for j in range(n_chunks):
                p.op("t", lambda e, j=j, c0=col0: e.matmul(
                    PA[:, c0 + j:c0 + j + 1],
                    rowA[0:1, j * 128:(j + 1) * 128],
                    ones_f32[:], start=True, stop=True),
                    reads=[r_rowA, r_const], writes=[r_PA])
            p.inc("t")
        bias_block(bq_r, 8, 0)
        bias_block(bk_r, 2, 8)
        bias_block(bo_r, 8, 10)
        p.op("s", lambda e: e.copy(bias_sb[:], PA[:, 0:18]),
             reads=[r_PA], writes=[r_bias])
        p.inc("s")

        p.dma("q", bv_row[:], bv_r, writes=[r_bv])
        p.op("v", lambda e: e.tensor_copy(bv_rbf[:], bv_row[:]),
             reads=[r_bv], writes=[r_bv])
        p.inc("v")
        p.op("t", lambda e: e.matmul(PB[:, 0:DKV], ones_row[0:1, 0:128],
                                     bv_rbf[:], start=True, stop=True),
             reads=[r_bv, r_const], writes=[r_PB])
        p.inc("t")
        p.op("s", lambda e: e.copy(bv_bs[:], PB[:, 0:DKV]),
             reads=[r_PB], writes=[r_bvbs])
        p.inc("s")

        def stream32(src_ap, dst_ap, rsrcs, half):
            st = stg32[:, half * 1024:(half + 1) * 1024]
            p.dma("q", st, src_ap, writes=[r_stg[half]])
            p.op("v", lambda e, st=st, d=dst_ap: e.tensor_copy(d, st),
                 reads=[r_stg[half]], writes=list(rsrcs))
            p.inc("v")

        def streami(src_ap, dst_ap, rsrcs, slot, queue="q", extra_dma=()):
            t2 = (stgi, stgi, stgi2, stgi2)[slot]
            st = t2[:, (slot % 2) * 1024:(slot % 2 + 1) * 1024]
            p.dma(queue, st, src_ap,
                  writes=[r_stg4[slot]] + list(extra_dma))
            p.op("v", lambda e, st=st, d=dst_ap: e.tensor_copy(d, st),
                 reads=[r_stg4[slot]], writes=rsrcs)
            p.inc("v")

        for i in range(2):
            stream32(wk_a[:, i * 1024:(i + 1) * 1024],
                     wk_bf[:, i * 1024:(i + 1) * 1024], [r_wk], i % 2)
        for i in range(2):
            stream32(wv_a[:, i * 1024:(i + 1) * 1024],
                     wv_bf[:, i * 1024:(i + 1) * 1024], [r_wv], i % 2)
        for i in range(8):
            stream32(wq_a[:, i * 1024:(i + 1) * 1024],
                     wq_bf[:, i * 1024:(i + 1) * 1024], [r_wq], i % 2)

        def layernorm(src_res, dst_sb, dst_out_res, extra_scr=()):
            xb = k_sb[:, 0:2048]
            sq = vt_sb[:, 0:2048]
            t1 = vt_sb[:, 2048:4096]
            w_xb = [r_xtmp] + [r for r in extra_scr if r.name == "k"]
            w_sq = [r_sqtmp] + [r for r in extra_scr if r.name == "vt"]
            w_t1 = [r_t1tmp] + [r for r in extra_scr if r.name == "vt"]
            for c in range(KC):
                xs = x_sb[:, c * SEQ:(c + 1) * SEQ]
                p.op("v", lambda e, xs=xs: e.tensor_copy(xb, xs),
                     reads=[src_res], writes=w_xb)
                p.inc("v")
                p.op("s", lambda e, xs=xs: e.activation(sq, xs, AF.Square),
                     reads=[src_res], writes=w_sq)
                p.inc("s")
                for piece in range(4):
                    ps = (PA, PA, PB, PB)[piece]
                    off = (piece % 2) * 512
                    p.op("t", lambda e, ps=ps, off=off, piece=piece, c=c: e.matmul(
                        ps[0:1, off:off + 512], ones_col[:],
                        xb[:, piece * 512:(piece + 1) * 512],
                        start=(c == 0), stop=(c == KC - 1)),
                        reads=[r_xtmp, r_const], writes=[PS[id(ps)]])
                for piece in range(4):
                    ps = (PC, PC, PD, PD)[piece]
                    off = (piece % 2) * 512
                    p.op("t", lambda e, ps=ps, off=off, piece=piece, c=c: e.matmul(
                        ps[0:1, off:off + 512], ones_col[:],
                        sq[:, piece * 512:(piece + 1) * 512],
                        start=(c == 0), stop=(c == KC - 1)),
                        reads=[r_sqtmp, r_const], writes=[PS[id(ps)]])
                p.inc("t")
            for Hh in range(2):
                sum_ps = (PA, PB)[Hh]
                sq_ps = (PC, PD)[Hh]
                p.op("v", lambda e, ps=sum_ps: e.tensor_scalar(
                    rowA[:], ps[0:1, :], 1.0 / DM, None, ALU.mult),
                    reads=[PS[id(sum_ps)]], writes=[r_rowA])
                p.op("v", lambda e: e.tensor_mul(rowB[:], rowA[:], rowA[:]),
                     reads=[r_rowA], writes=[r_rowB])
                p.inc("v")
                p.op("s", lambda e, Hh=Hh: e.copy(
                    m_bf[0:1, Hh * 1024:(Hh + 1) * 1024], rowA[:]),
                    reads=[r_rowA], writes=[r_mrow, r_k])
                p.inc("s")
                p.op("v", lambda e, ps=sq_ps: e.scalar_tensor_tensor(
                    rowA[:], ps[0:1, :], 1.0 / DM, rowB[:],
                    ALU.mult, ALU.subtract),
                    reads=[PS[id(sq_ps)], r_rowB], writes=[r_rowA])
                p.inc("v")
                p.op("s", lambda e: e.activation(rowB[:], rowA[:], AF.Ln,
                                                 bias=eps_c[:]),
                     reads=[r_rowA, r_const], writes=[r_rowB])
                p.op("s", lambda e, Hh=Hh: e.activation(
                    rstd_bf[0:1, Hh * 1024:(Hh + 1) * 1024], rowB[:],
                    AF.Exp, scale=-0.5),
                    reads=[r_rowB], writes=[r_rrow, r_vt])
                p.inc("s")
            for piece in range(4):
                ps = (PA, PA, PB, PB)[piece]
                off = (piece % 2) * 512
                p.op("t", lambda e, ps=ps, off=off, piece=piece: e.matmul(
                    ps[:, off:off + 512], ones_row[0:1, 0:128],
                    m_bf[0:1, piece * 512:(piece + 1) * 512],
                    start=True, stop=True),
                    reads=[r_mrow, r_k, r_const], writes=[PS[id(ps)]])
                ps2 = (PC, PC, PD, PD)[piece]
                p.op("t", lambda e, ps=ps2, off=off, piece=piece: e.matmul(
                    ps[:, off:off + 512], ones_row[0:1, 0:128],
                    rstd_bf[0:1, piece * 512:(piece + 1) * 512],
                    start=True, stop=True),
                    reads=[r_rrow, r_vt, r_const], writes=[PS[id(ps2)]])
            p.inc("t")
            p.op("s", lambda e: e.copy(m_bs[:, 0:1024], PA[:]),
                 reads=[r_PA], writes=RS_MBS)
            p.op("s", lambda e: e.copy(m_bs[:, 1024:2048], PB[:]),
                 reads=[r_PB], writes=RS_MBS)
            p.op("s", lambda e: e.copy(rs_bs[:, 0:1024], PC[:]),
                 reads=[r_PC], writes=RS_RBS)
            p.op("s", lambda e: e.copy(rs_bs[:, 1024:2048], PD[:]),
                 reads=[r_PD], writes=RS_RBS)
            p.inc("s")
            for c in range(KC):
                xs = x_sb[:, c * SEQ:(c + 1) * SEQ]
                p.op("v", lambda e, xs=xs: e.tensor_sub(t1, xs, m_bs),
                     reads=[src_res] + RS_MBS, writes=w_t1)
                p.op("v", lambda e, c=c: e.tensor_mul(
                    dst_sb[:, c * SEQ:(c + 1) * SEQ], t1, rs_bs),
                    reads=[r_t1tmp] + RS_RBS, writes=dst_out_res)
                p.inc("v")

        Prog.cur_phase = "ln1"
        layernorm(r_x, h_sb, [r_h])
        Prog.cur_phase = "qproj"

        for mc in range(KC):
            ps_pair = (PA, PB) if mc % 2 == 0 else (PC, PD)
            rs_pair = [PS[id(ps_pair[0])], PS[id(ps_pair[1])]]
            for kc in range(KC):
                for nf in range(4):
                    ps = ps_pair[nf // 2]
                    off = (nf % 2) * 512
                    p.op("t", lambda e, ps=ps, off=off, kc=kc, mc=mc, nf=nf:
                         e.matmul(
                             ps[:, off:off + 512],
                             wq_bf[:, kc * 1024 + mc * 128:
                                   kc * 1024 + mc * 128 + 128],
                             h_sb[:, kc * SEQ + nf * 512:
                                  kc * SEQ + nf * 512 + 512],
                             start=(kc == 0), stop=(kc == KC - 1)),
                         reads=[r_wq, r_h], writes=rs_pair)
            p.inc("t")
            for half in range(2):
                ps = ps_pair[half]
                p.op("s", lambda e, ps=ps, mc=mc, half=half: e.activation(
                    q_sb[:, mc * SEQ + half * 1024:
                         mc * SEQ + half * 1024 + 1024],
                    ps[:], AF.Identity, bias=bias_sb[:, mc:mc + 1]),
                    reads=[rs_pair[half], r_bias], writes=[r_q])
            p.inc("s")

        Prog.cur_phase = "kproj"
        for mc2 in range(2):
            ps_pair = (PA, PB) if mc2 % 2 == 0 else (PC, PD)
            rs_pair = [PS[id(ps_pair[0])], PS[id(ps_pair[1])]]
            for kc in range(KC):
                for nf in range(4):
                    ps = ps_pair[nf // 2]
                    off = (nf % 2) * 512
                    p.op("t", lambda e, ps=ps, off=off, kc=kc, mc2=mc2, nf=nf:
                         e.matmul(
                             ps[:, off:off + 512],
                             wk_bf[:, kc * 256 + mc2 * 128:
                                   kc * 256 + mc2 * 128 + 128],
                             h_sb[:, kc * SEQ + nf * 512:
                                  kc * SEQ + nf * 512 + 512],
                             start=(kc == 0), stop=(kc == KC - 1)),
                         reads=[r_wk, r_h], writes=rs_pair)
            p.inc("t")
            for half in range(2):
                ps = ps_pair[half]
                p.op("s", lambda e, ps=ps, mc2=mc2, half=half: e.activation(
                    k_sb[:, mc2 * SEQ + half * 1024:
                         mc2 * SEQ + half * 1024 + 1024],
                    ps[:], AF.Identity, bias=bias_sb[:, 8 + mc2:9 + mc2]),
                    reads=[rs_pair[half], r_bias], writes=[r_k])
            p.inc("s")

        Prog.cur_phase = "vtok"
        def z_init(e):
            e.memset(AP(vt_sb, 0, [[NSC * VT_W, 128], [VT_W, NSC],
                                   [192, 3], [1, 64]]), 0.0)
        p.op("v", z_init, writes=[r_vt, r_sqtmp, r_t1tmp])
        p.op("v", lambda e: e.memset(
            AP(vt_sb, 0, [[NSC * VT_W, 128], [VT_W, NSC], [192, 3]]), 1.0),
            writes=[r_vt])
        p.inc("v")

        for sc in range(NSC):
            ps = (PA, PB)[sc % 2]
            rp = PS[id(ps)]
            for kc in range(KC):
                p.op("t", lambda e, ps=ps, sc=sc, kc=kc: e.matmul(
                    ps[:, 0:DKV],
                    h_sb[:, kc * SEQ + sc * 128: kc * SEQ + sc * 128 + 128],
                    wv_bf[:, kc * 256:(kc + 1) * 256],
                    start=(kc == 0), stop=(kc == KC - 1)),
                    reads=[r_h, r_wv], writes=[rp])
            p.inc("t")
            p.op("v", lambda e, ps=ps, sc=sc: e.tensor_add(
                AP(vt_sb, sc * VT_W + 64, [[NSC * VT_W, 128], [192, 2], [1, 128]]),
                ps[:, 0:DKV].rearrange("p (s c) -> p s c", s=2),
                bv_bs[:].rearrange("p (s c) -> p s c", s=2)),
                reads=[rp, r_bvbs], writes=[r_vt])
            p.inc("v")

        if "h1" in tap_aps:
            for c in range(KC):
                p.dma("g", tap_aps["h1"][c * 128:(c + 1) * 128, :],
                      h_sb[:, c * SEQ:(c + 1) * SEQ], reads=[r_h])
        if "q" in tap_aps:
            for c in range(KC):
                p.dma("g", tap_aps["q"][c * 128:(c + 1) * 128, :],
                      q_sb[:, c * SEQ:(c + 1) * SEQ], reads=[r_q])
        if "k" in tap_aps:
            for c in range(2):
                p.dma("g", tap_aps["k"][c * 128:(c + 1) * 128, :],
                      k_sb[:, c * SEQ:(c + 1) * SEQ], reads=[r_k])
        if "vt" in tap_aps:
            p.dma("g", tap_aps["vt"], vt_sb[:], reads=[r_vt])

        for i in range(8):
            stream32(wo_a[:, i * 1024:(i + 1) * 1024],
                     wq_bf[:, i * 1024:(i + 1) * 1024],
                     [r_wo, r_wq] if i < 2 else [r_wo], i % 2)

        Prog.cur_phase = "att"
        ev_exp_tail = None
        ev_qk = {}
        for R in (r_PA, r_PB, r_PC, r_PD):
            for ev in R.w + R.r:
                p._wait("t", ev)
        for R in (r_k, r_q, r_vt):
            for ev in R.w:
                p._wait("t", ev)
        for ev in (r_rowA.w + r_rowA.r + r_rowB.w + r_rowB.r +
                   r_ao.w + r_ao.r + r_h.r + r_h.w):
            p._wait("v", ev)

        def ring_of(si):
            return (PA, PB, PD) if si % 2 == 0 else (PA, PB, PC)

        def avbuf_of(si):
            return PC if si % 2 == 0 else PD

        def t_drain_inc():
            val = p.count["ts"] + 1
            p.count["ts"] = val
            p.stream["t"].append(
                lambda eng, h2=p.sems["ts"]: eng.maybe_drain_then_inc(
                    (h2, 1), fusable=True))
            fcl = dict(p.floor["t"])
            fcl["ts"] = val
            p.evclock[("ts", val)] = fcl
            return ("ts", val)

        slots = [(c, qb) for c in range(8) for qb in range(4)]

        def emit_qk(slot_idx, g):
            c, qb = slots[slot_idx]
            ps = ring_of(slot_idx)[g % 3]
            for side, hh in ((0, A_HEADS[c]), (1, B_HEADS[c])):
                kc2 = (hh // 4) // 2
                p.raw("t", lambda e, ps=ps, sc=g, side=side, c=c, qb=qb,
                      kc2=kc2: e.matmul(
                          ps[:, side * 512:(side + 1) * 512],
                          k_sb[side * 64: side * 64 + 64,
                               kc2 * SEQ + sc * 128: kc2 * SEQ + sc * 128 + 128],
                          q_sb[side * 64: side * 64 + 64,
                               c * SEQ + qb * 512: c * SEQ + qb * 512 + 512],
                          start=True, stop=True))
            ev_qk[(slot_idx, g)] = p.raw_inc("t")

        ev_ave_par = {0: None, 1: None}
        ev_norm_final = None
        for si, (c, qb) in enumerate(slots):
            AVB = avbuf_of(si)
            soA = VT_STAT_OFF[A_HEADS[c] // 4]
            soB = VT_STAT_OFF[B_HEADS[c] // 4]
            ev_exp = {}
            ev_av = {}
            if si == 0:
                for g0 in (0, 1, 2):
                    emit_qk(0, g0)
            for g in range(NSC):
                lag = ev_qk.get((si, g + 1)) or ev_av[g - 2]
                p._wait("s", lag)
                ps = ring_of(si)[g % 3]
                pb = probs_sb[:, (g % 4) * 1024:(g % 4 + 1) * 1024]
                p.raw("s", lambda e, ps=ps, pb=pb: e.activation(
                    pb, ps[:], AF.Exp, scale=ATT_SCALE))
                ev_exp[g] = p.raw_inc("s")
                if g + 3 < NSC:
                    p._wait("t", ev_exp[g])
                    emit_qk(si, g + 3)
                p._wait("t", ev_exp[g])
                if g == 0 and ev_ave_par[si % 2] is not None:
                    p._wait("t", ev_ave_par[si % 2])
                for side, so in ((0, soA), (1, soB)):
                    p.raw("t", lambda e, g=g, side=side, so=so, AVB=AVB:
                          e.matmul(
                              AVB[:, side * 512:(side + 1) * 512],
                              vt_sb[:, g * VT_W + so: g * VT_W + so + 128],
                              probs_sb[:, (g % 4) * 1024 + side * 512:
                                       (g % 4) * 1024 + side * 512 + 512],
                              start=(g == 0), stop=(g == NSC - 1)))
                ev_av[g] = p.raw_inc("t")
            if si + 1 < len(slots):
                p._wait("t", ev_exp[NSC - 1])
                emit_qk(si + 1, 0)
                emit_qk(si + 1, 1)
                ev_lag_tail = ev_qk[(si + 1, 1)]
            else:
                ev_lag_tail = t_drain_inc()
            p._wait("v", ev_lag_tail)
            p.raw("v", lambda e, AVB=AVB: e.tensor_copy(av_f32[:], AVB[:]))
            ev_ave = p.raw_inc("v")
            ev_ave_par[si % 2] = ev_ave
            if si + 1 < len(slots):
                p._wait("t", ev_ave)
                emit_qk(si + 1, 2)
            p._wait("v", ev_ave)
            for side, hh in ((0, A_HEADS[c]), (1, B_HEADS[c])):
                cs = side * 512
                srow = 64 * (1 - side)
                drow = side * 64
                p.raw("v", lambda e, sr=srow, cs=cs: e.reciprocal(
                    av_f32[sr:sr + 1, cs:cs + 512],
                    av_f32[sr:sr + 1, cs:cs + 512]))
                p.raw("v", lambda e, sr=srow, cs=cs: e.tensor_copy(
                    att_rbf[sr:sr + 1, cs:cs + 512],
                    av_f32[sr:sr + 1, cs:cs + 512]))
                sq = 64 * (1 - side)
                p.raw("v", lambda e, sq=sq, dr=drow, cs=cs: e.stream_shuffle(
                    att_rbf[dr:dr + 64, cs:cs + 512],
                    att_rbf[sq:sq + 64, cs:cs + 512], [0] * 32))
                p.raw("v", lambda e, dr=drow, cs=cs: e.stream_shuffle(
                    att_rbf[dr + 32:dr + 64, cs:cs + 512],
                    att_rbf[dr:dr + 32, cs:cs + 512], [0] * 32))
                p.raw("v", lambda e, dr=drow, cs=cs, c=c, qb=qb: e.tensor_mul(
                    h_sb[dr:dr + 64,
                         c * SEQ + qb * 512: c * SEQ + qb * 512 + 512],
                    av_f32[dr:dr + 64, cs:cs + 512],
                    att_rbf[dr:dr + 64, cs:cs + 512]))
            ev_norm_final = p.raw_inc("v")
            ev_exp_tail = ev_exp[NSC - 1]

        ev_fin = ev_norm_final
        p._wait("t", ev_fin)
        for R in (r_PA, r_PB, r_PC, r_PD, r_rowA, r_rowB, r_rb):
            R.w = [ev_fin]
            R.r = []
            R.dirty_by = None
        r_ao.w = [ev_fin]
        r_ao.r = []
        r_h.w = [ev_fin]
        r_h.r = []
        for R in (r_q, r_k, r_vt):
            R.r = [ev_fin]
        for R in r_probs:
            R.w = [ev_fin]
            R.r = []

        if "ao" in tap_aps:
            for c in range(KC):
                p.dma("g", tap_aps["ao"][c * 128:(c + 1) * 128, :],
                      h_sb[:, c * SEQ:(c + 1) * SEQ], reads=[r_ao])

        Prog.cur_phase = "wo"
        for mc in range(KC):
            ps_pair = (PA, PB) if mc % 2 == 0 else (PC, PD)
            rs_pair = [PS[id(ps_pair[0])], PS[id(ps_pair[1])]]
            for kc in range(KC):
                for nf in range(4):
                    ps = ps_pair[nf // 2]
                    off = (nf % 2) * 512
                    p.op("t", lambda e, ps=ps, off=off, kc=kc, mc=mc, nf=nf:
                         e.matmul(
                             ps[:, off:off + 512],
                             wq_bf[:, kc * 1024 + mc * 128:
                                   kc * 1024 + mc * 128 + 128],
                             h_sb[:, kc * SEQ + nf * 512:
                                  kc * SEQ + nf * 512 + 512],
                             start=(kc == 0), stop=(kc == KC - 1)),
                         reads=[r_wo, r_ao], writes=rs_pair)
            p.inc("t")
            for half in range(2):
                ps = ps_pair[half]
                xs = x_sb[:, mc * SEQ + half * 1024: mc * SEQ + half * 1024 + 1024]
                p.op("v", lambda e, ps=ps, xs=xs, mc=mc: e.scalar_tensor_tensor(
                    xs, ps[:], bias_sb[:, 10 + mc:11 + mc], xs,
                    ALU.add, ALU.add),
                    reads=[rs_pair[half], r_bias, r_x], writes=[r_x])
            p.inc("v")

        if "x1" in tap_aps:
            for c in range(KC):
                p.dma("g", tap_aps["x1"][c * 128:(c + 1) * 128, :],
                      x_sb[:, c * SEQ:(c + 1) * SEQ], reads=[r_x])

        Prog.cur_phase = "ln2"
        layernorm(r_x, q_sb, [r_h2, r_q], extra_scr=[r_k, r_vt])
        Prog.cur_phase = "ffn"

        if "h2" in tap_aps:
            for c in range(KC):
                p.dma("g", tap_aps["h2"][c * 128:(c + 1) * 128, :],
                      q_sb[:, c * SEQ:(c + 1) * SEQ], reads=[r_h2])

        first_ffn = [r_ao, r_h, r_k, r_vt]

        def stream_gu(th, hc):
            half = hc % 2
            fw = [r_wo] if (th == 0 and hc < 2) else []
            fd = (([r_wo, r_wk, r_wv] if th == 0 else
                   [r_wd[0], r_wd[1]]) if hc < 4 else [])
            streami(wg_a[hc], wg_buf[:, half * 1024:(half + 1) * 1024],
                    [r_wg[half]] + fw, (0, 2)[hc % 2], queue="q",
                    extra_dma=fd)
            streami(wu_a[hc], wu_buf[:, half * 1024:(half + 1) * 1024],
                    [r_wu[half]] + fw, (1, 3)[hc % 2], queue="g",
                    extra_dma=fd)

        def stream_wd(th, mc):
            half = mc % 2
            fwd = [r_wo, r_wk, r_wv] if (th == 0 and mc < 2) else []
            for j in range(3):
                ovl = []
                byte0 = half * 6144 + j * 2048
                if byte0 < 4096:
                    ovl.append(r_stg4[2])
                if byte0 + 2048 > 4096 and byte0 < 8192:
                    ovl.append(r_stg4[3])
                streami(wd_a[mc][:, j * 1024:(j + 1) * 1024],
                        wd_buf[:, half * FH + j * 1024:
                              half * FH + j * 1024 + 1024],
                        [r_wd[half]] + fwd + ovl, j % 2,
                        queue="q" if j % 2 == 0 else "g")

        for th in range(2):
            stream_gu(th, 0)
            for hc in range(HC):
                half = hc % 2
                extra = first_ffn if (th == 0 and hc < 2) else []
                gps = (PA, PB)[half]
                ups = (PC, PD)[half]
                for kc in range(KC):
                    for nf in range(2):
                        p.op("t", lambda e, gps=gps, kc=kc, nf=nf, th=th,
                             half=half: e.matmul(
                                 gps[:, nf * 512:(nf + 1) * 512],
                                 wg_buf[:, half * 1024 + kc * 128:
                                       half * 1024 + kc * 128 + 128],
                                 q_sb[:, kc * SEQ + th * 1024 + nf * 512:
                                      kc * SEQ + th * 1024 + nf * 512 + 512],
                                 start=(kc == 0), stop=(kc == KC - 1)),
                             reads=[r_wg[half], r_h2], writes=[PS[id(gps)]])
                for kc in range(KC):
                    for nf in range(2):
                        p.op("t", lambda e, ups=ups, kc=kc, nf=nf, th=th,
                             half=half: e.matmul(
                                 ups[:, nf * 512:(nf + 1) * 512],
                                 wu_buf[:, half * 1024 + kc * 128:
                                       half * 1024 + kc * 128 + 128],
                                 q_sb[:, kc * SEQ + th * 1024 + nf * 512:
                                      kc * SEQ + th * 1024 + nf * 512 + 512],
                                 start=(kc == 0), stop=(kc == KC - 1)),
                             reads=[r_wu[half], r_h2], writes=[PS[id(ups)]])
                p.inc("t")
                p.op("s", lambda e, gps=gps, half=half: e.activation(
                    sg_buf[:, half * 1024:(half + 1) * 1024], gps[:],
                    AF.Silu, scale=DQS),
                    reads=[PS[id(gps)]], writes=[r_sg[half]])
                p.inc("s")
                if hc + 1 < HC:
                    stream_gu(th, hc + 1)
                p.op("v", lambda e, ups=ups, hc=hc, half=half: e.tensor_mul(
                    hf_sb[:, hc * 1024:(hc + 1) * 1024], ups[:],
                    sg_buf[:, half * 1024:(half + 1) * 1024]),
                    reads=[PS[id(ups)], r_sg[half]],
                    writes=[r_hf] + extra)
                p.inc("v")
            stream_wd(th, 0)
            for mc in range(KC):
                half = mc % 2
                dps = (PA, PB)[half]
                for hcc in range(HC):
                    for nf in range(2):
                        p.op("t", lambda e, dps=dps, hcc=hcc, nf=nf,
                             half=half: e.matmul(
                                 dps[:, nf * 512:(nf + 1) * 512],
                                 wd_buf[:, half * FH + hcc * 128:
                                       half * FH + hcc * 128 + 128],
                                 hf_sb[:, hcc * 1024 + nf * 512:
                                      hcc * 1024 + nf * 512 + 512],
                                 start=(hcc == 0), stop=(hcc == HC - 1)),
                             reads=[r_wd[half], r_hf], writes=[PS[id(dps)]])
                p.inc("t")
                if mc + 1 < KC:
                    stream_wd(th, mc + 1)
                xs = x_sb[:, mc * SEQ + th * 1024: mc * SEQ + th * 1024 + 1024]
                p.op("v", lambda e, dps=dps, xs=xs: e.scalar_tensor_tensor(
                    xs, dps[:], DQS * DQS, xs, ALU.mult, ALU.add),
                    reads=[PS[id(dps)], r_x], writes=[r_x])
                p.inc("v")

        for c in range(KC):
            p.dma("g", yT[c * 128:(c + 1) * 128, :],
                  x_sb[:, c * SEQ:(c + 1) * SEQ], reads=[r_x])
        for s2 in ("dg0", "dg1", "dq0", "dq1"):
            p._wait("g", (s2, p.count[s2]))

        with nc.Block() as block:
            @block.tensor
            def _(eng):
                for fn in p.stream["t"]:
                    fn(eng)

            @block.vector
            def _(eng):
                for fn in p.stream["v"]:
                    fn(eng)

            @block.scalar
            def _(eng):
                for fn in p.stream["s"]:
                    fn(eng)

            @block.sync
            def _(eng):
                for fn in p.stream["q"]:
                    fn(eng)

            @block.gpsimd
            def _(eng):
                for fn in p.stream["g"]:
                    fn(eng)

    return nc


def prepare_inputs(inputs):
    x = np.asarray(inputs["L0_attn_x"])
    wq = np.asarray(inputs["L0_attn_wq"])
    bq = np.asarray(inputs["L0_attn_bq"])
    wk = np.asarray(inputs["L0_attn_wk"])
    bk = np.asarray(inputs["L0_attn_bk"])
    wv = np.asarray(inputs["L0_attn_wv"])
    bv = np.asarray(inputs["L0_attn_bv"])
    wo = np.asarray(inputs["L0_attn_wo"])
    bo = np.asarray(inputs["L0_attn_bo"])
    wg = np.asarray(inputs["L0_ffn_w_gate"])
    wu = np.asarray(inputs["L0_ffn_w_up"])
    wd = np.asarray(inputs["L0_ffn_w_down"])

    wq_c = wq[PERM_F]
    bq_c = bq[PERM_F]
    perm_kv = np.concatenate([np.arange(64) + k * 64 for k in VT_KV_ORDER])
    wv_c = wv[perm_kv]
    bv_c = bv[perm_kv]
    wo_c = wo[:, PERM_F]

    wq_al = np.ascontiguousarray(
        wq_c.reshape(KC, 128, KC, 128).transpose(3, 2, 0, 1)).reshape(128, KC * DM)
    wk_al = np.ascontiguousarray(
        wk.reshape(2, 128, KC, 128).transpose(3, 2, 0, 1)).reshape(128, KC * DKV)
    wv_al = np.ascontiguousarray(
        wv_c.reshape(DKV, KC, 128).transpose(2, 1, 0)).reshape(128, KC * DKV)
    wo_al = np.ascontiguousarray(
        wo_c.reshape(KC, 128, KC, 128).transpose(3, 2, 0, 1)).reshape(128, KC * DM)
    wg_al = np.ascontiguousarray(
        wg.reshape(HC, 128, KC, 128).transpose(0, 3, 2, 1)).reshape(HC, 128, DM)
    wu_al = np.ascontiguousarray(
        wu.reshape(HC, 128, KC, 128).transpose(0, 3, 2, 1)).reshape(HC, 128, DM)
    wd_al = np.ascontiguousarray(
        wd.reshape(KC, 128, HC, 128).transpose(0, 3, 2, 1)).reshape(KC, 128, FH)

    shared = {
        "wq_a": wq_al, "wk_a": wk_al, "wv_a": wv_al, "wo_a": wo_al,
        "bq_r": np.ascontiguousarray(bq_c.reshape(1, DM)),
        "bk_r": np.ascontiguousarray(bk.reshape(1, DKV)),
        "bv_r": np.ascontiguousarray(bv_c.reshape(1, DKV)),
        "bo_r": np.ascontiguousarray(bo.reshape(1, DM)),
        "wg_a": wg_al, "wu_a": wu_al, "wd_a": wd_al,
    }
    in_maps = []
    for i in range(8):
        m = dict(shared)
        m["xT"] = np.ascontiguousarray(x[i].T)
        in_maps.append(m)
    return in_maps


def kernel(**inputs):
    in_maps = prepare_inputs(inputs)
    nc = build_program()
    res = run_bass_kernel_spmd(nc, in_maps, core_ids=list(range(8)))
    out = np.stack([np.ascontiguousarray(np.asarray(r["yT"]).T)
                    for r in res.results], axis=0)
    return out.astype(np.float32, copy=False)


# version 19
# speedup vs baseline: 1.3424x; 1.0046x over previous
import numpy as np
from contextlib import ExitStack

import concourse.bass as bass
import concourse.mybir as mybir
from concourse.ap import AP
from concourse.bass_utils import run_bass_kernel_spmd

F32 = mybir.dt.float32
BF16 = mybir.dt.bfloat16
I32 = mybir.dt.int32
AF = mybir.ActivationFunctionType
ALU = mybir.AluOpType

SEQ = 2048
DM = 1024
DKV = 256
NH, NKV, DH = 16, 4, 64
FH = 3072
KC = DM // 128
HC = FH // 128
NSC = SEQ // 128
EPS = 1e-5
ATT_SCALE = 0.125
DQS = 1.0 / 127.0

A_HEADS = [0, 1, 2, 3, 8, 9, 10, 11]
B_HEADS = [4, 5, 6, 7, 12, 13, 14, 15]
HEAD_POS = {}
for _c in range(8):
    HEAD_POS[A_HEADS[_c]] = (_c, 0)
    HEAD_POS[B_HEADS[_c]] = (_c, 1)
PERM_F = np.zeros(DM, np.int64)
for _h, (_c, _p) in HEAD_POS.items():
    PERM_F[_c * 128 + _p * 64: _c * 128 + _p * 64 + 64] = np.arange(64) + _h * 64

VT_W = 448
VT_STAT_OFF = {0: 128, 1: 0, 2: 320, 3: 192}
VT_KV_ORDER = [1, 0, 3, 2]


class Rsrc:
    __slots__ = ("name", "w", "r", "dirty_by")

    def __init__(self, name):
        self.name = name
        self.w = []
        self.r = []
        self.dirty_by = None


class Prog:

    serialize = False
    ser_phases = set()
    cur_phase = ""
    trace_log = None

    def __init__(self, nc, sems):
        self.nc = nc
        self.stream = {e: [] for e in ("t", "v", "s", "g", "q")}
        self.sems = sems
        self.count = {k: 0 for k in sems}
        self.floor = {e: {} for e in self.stream}
        self.pending = {e: ([], []) for e in self.stream}
        self.last_op_idx = {e: -1 for e in self.stream}
        self.evclock = {}
        self.eng_sem = {"t": "ts", "v": "vs", "s": "ss"}
        self.dma_sem = {"q": ["dq0", "dq1"], "g": ["dg0", "dg1"]}
        self.dma_rr = {"q": 0, "g": 0}

    def _wait(self, e, ev):
        sem, val = ev
        if self.floor[e].get(sem, 0) >= val:
            return
        assert val <= self.count[sem], (e, sem, val, self.count[sem])
        h = self.sems[sem]
        self.stream[e].append(lambda eng, h=h, v=val: eng.wait_ge(h, v))
        if Prog.trace_log is not None:
            Prog.trace_log.append(("W", e, sem, val, Prog.cur_phase))
        clk = self.evclock.get((sem, val), {})
        f = self.floor[e]
        for s2, v2 in clk.items():
            if f.get(s2, 0) < v2:
                f[s2] = v2
        if f.get(sem, 0) < val:
            f[sem] = val

    def _ser(self, e):
        if not (Prog.serialize or Prog.cur_phase in Prog.ser_phases):
            return
        for sem, cnt in self.count.items():
            if cnt:
                self._wait(e, (sem, cnt))

    def raw(self, e, fn):
        self.stream[e].append(fn)
        self.last_op_idx[e] = len(self.stream[e]) - 1

    def raw_inc(self, e):
        sem = self.eng_sem[e]
        self.count[sem] += 1
        val = self.count[sem]
        h = self.sems[sem]
        i = self.last_op_idx[e]
        assert i == len(self.stream[e]) - 1, (e, i, len(self.stream[e]))
        old_fn = self.stream[e][i]
        self.stream[e][i] = (
            lambda eng, f=old_fn, h=h: f(eng).then_inc(h, 1))
        if Prog.trace_log is not None:
            Prog.trace_log.append(("i", e, sem, val, Prog.cur_phase))
        f = dict(self.floor[e])
        f[sem] = val
        self.evclock[(sem, val)] = f
        return (sem, val)

    def op(self, e, fn, reads=(), writes=()):
        self._ser(e)
        for R in reads:
            assert R.dirty_by in (None, e), f"read of dirty {R.name} on {e}"
            for ev in R.w:
                self._wait(e, ev)
        for R in writes:
            assert R.dirty_by in (None, e), f"write of dirty {R.name} on {e}"
            for ev in R.w:
                self._wait(e, ev)
            for ev in R.r:
                self._wait(e, ev)
        self.stream[e].append(fn)
        self.last_op_idx[e] = len(self.stream[e]) - 1
        pr, pw = self.pending[e]
        pr.extend(reads)
        pw.extend(writes)
        for R in writes:
            R.dirty_by = e
        for R in reads:
            if R.dirty_by is None:
                R.dirty_by = e

    def inc(self, e, drain=None):
        if drain is None:
            drain = (e == "t")
        sem = self.eng_sem[e]
        self.count[sem] += 1
        val = self.count[sem]
        h = self.sems[sem]
        if drain:
            self.stream[e].append(
                lambda eng, h=h: eng.maybe_drain_then_inc((h, 1), fusable=True))
        else:
            i = self.last_op_idx[e]
            assert i == len(self.stream[e]) - 1, (e, i, len(self.stream[e]))
            old_fn = self.stream[e][i]
            self.stream[e][i] = (
                lambda eng, f=old_fn, h=h: f(eng).then_inc(h, 1))
        if Prog.trace_log is not None:
            Prog.trace_log.append(("I", e, sem, val, Prog.cur_phase))
        f = dict(self.floor[e])
        f[sem] = val
        self.evclock[(sem, val)] = f
        ev = (sem, val)
        pr, pw = self.pending[e]
        for R in pw:
            R.w = [ev]
            R.r = []
            R.dirty_by = None
        for R in pr:
            if R.dirty_by == e:
                R.dirty_by = None
            R.r.append(ev)
        self.pending[e] = ([], [])
        return ev

    def dma(self, e, out_ap, in_ap, reads=(), writes=()):
        self._ser(e)
        for R in reads:
            assert R.dirty_by in (None, e)
            for ev in R.w:
                self._wait(e, ev)
        for R in writes:
            assert R.dirty_by in (None, e)
            for ev in R.w + R.r:
                self._wait(e, ev)
        sem = self.dma_sem[e][self.dma_rr[e]]
        self.dma_rr[e] ^= 1
        h = self.sems[sem]
        self.count[sem] += 16
        val = self.count[sem]
        self.stream[e].append(
            lambda eng, h=h, o=out_ap, i=in_ap: eng.dma_start(o, i).then_inc(h, 16))
        if Prog.trace_log is not None:
            Prog.trace_log.append(("D", e, sem, val, Prog.cur_phase))
        f = dict(self.floor[e])
        f[sem] = val
        self.evclock[(sem, val)] = f
        ev = (sem, val)
        for R in writes:
            R.w = [ev]
            R.r = []
        for R in reads:
            R.r.append(ev)
        return ev


def build_program(taps=(), serialize=False, ser_phases=()):
    taps = set(taps)
    Prog.serialize = serialize
    Prog.ser_phases = set(ser_phases)
    Prog.cur_phase = "pre"
    nc = bass.Bass("TRN2", target_bir_lowering=False,
                   dynamic_dma_scratch_size=16384,
                   detect_race_conditions=False)

    def din(name, shape, dtype):
        return nc.dram_tensor(name, list(shape), dtype, kind="ExternalInput").ap()

    xT = din("xT", (DM, SEQ), F32)
    wq_a = din("wq_a", (128, KC * DM), F32)
    wk_a = din("wk_a", (128, KC * DKV), F32)
    wv_a = din("wv_a", (128, KC * DKV), F32)
    wo_a = din("wo_a", (128, KC * DM), F32)
    bq_r = din("bq_r", (1, DM), F32)
    bk_r = din("bk_r", (1, DKV), F32)
    bv_r = din("bv_r", (1, DKV), F32)
    bo_r = din("bo_r", (1, DM), F32)
    wg_a = din("wg_a", (HC, 128, DM), I32)
    wu_a = din("wu_a", (HC, 128, DM), I32)
    wd_a = din("wd_a", (KC, 128, FH), I32)
    yT = nc.dram_tensor("yT", [DM, SEQ], F32, kind="ExternalOutput").ap()

    tap_aps = {}
    for tname in taps:
        shp = {"h1": (DM, SEQ), "q": (DM, SEQ), "k": (DKV, SEQ),
               "vt": (128, NSC * VT_W), "ao": (DM, SEQ), "x1": (DM, SEQ),
               "h2": (DM, SEQ)}[tname]
        dt = F32 if tname == "x1" else BF16
        tap_aps[tname] = nc.dram_tensor("tap_" + tname, list(shp), dt,
                                        kind="ExternalOutput").ap()

    with ExitStack() as ctx:
        base = (nc.sbuf_base + 31) & ~31
        cur = [base]

        offs = {}

        def place(name, shape, dtype, at=None):
            if at is None:
                off = cur[0]
                per = int(np.prod(shape[1:])) * mybir.dt.size(dtype)
                cur[0] = (off + per + 31) & ~31
            else:
                off = at
            h = nc.alloc_sbuf_tensor_at(name, list(shape), dtype, offset=off)
            offs[name] = off
            assert cur[0] <= nc.sbuf_top, (name, cur[0], nc.sbuf_top)
            return h

        x_sb = place("x_sb", [128, KC * SEQ], F32)
        B0 = cur[0]
        h_sb = place("h_sb", [128, KC * SEQ], BF16)
        k_sb = place("k_sb", [128, 2 * SEQ], BF16)
        vt_sb = place("vt_sb", [128, NSC * VT_W], BF16)
        probs_sb = place("probs_sb", [128, 4 * 1024], BF16)
        hf_sb = place("hf_sb", [128, HC * 1024], BF16, at=B0)
        q_sb = place("q_sb", [128, KC * SEQ], BF16)
        W0 = cur[0]
        wq_bf = place("wq_bf", [128, KC * DM], BF16)
        wk_bf = place("wk_bf", [128, KC * DKV], BF16)
        wv_bf = place("wv_bf", [128, KC * DKV], BF16)
        wg_buf = place("wg_buf", [128, 2 * DM], BF16, at=W0)
        wu_buf = place("wu_buf", [128, 2 * DM], BF16, at=W0 + 4096)
        sg_buf = place("sg_buf", [128, 2 * 1024], BF16, at=W0 + 8192)
        wd_buf = place("wd_buf", [128, 2 * FH], BF16, at=W0 + 12288)
        stg32 = place("stg32", [128, 2048], F32)
        stgi = place("stgi", [128, 2048], I32, at=offs["stg32"])
        stgi2 = place("stgi2", [128, 2048], I32, at=W0 + 12288)
        ones_row = place("ones_row", [128, 128], BF16)
        ones_col = place("ones_col", [128, 1], BF16)
        ones_f32 = place("ones_f32", [1, 1], F32)
        eps_c = place("eps_c", [128, 1], F32)
        rowA = place("rowA", [1, 1024], F32)
        rowB = place("rowB", [1, 1024], F32)
        bv_row = place("bv_row", [1, DKV], F32)
        bv_rbf = place("bv_rbf", [1, DKV], BF16)
        bv_bs = place("bv_bs", [128, DKV], BF16)
        m_bf = place("m_bf", [1, SEQ], BF16, at=offs["k_sb"] + 4096)
        rstd_bf = place("rstd_bf", [1, SEQ], BF16, at=offs["vt_sb"] + 8192)
        m_bf2 = place("m_bf2", [128, 1024], BF16, at=offs["k_sb"] + 4096)
        rstd_bf2 = place("rstd_bf2", [128, 1024], BF16,
                         at=offs["vt_sb"] + 8192)
        av_f32 = place("av_f32", [128, 1024], F32, at=offs["rowA"])
        att_rbf = place("att_rbf", [128, 1024], BF16, at=offs["rowB"])
        rb_bf = place("rb_bf", [128, 1024], BF16, at=offs["rowB"] + 2048)
        bias_sb = place("bias_sb", [128, 18], F32)

        PA = ctx.enter_context(nc.psum_tensor("PA", [128, 1024], F32))
        PB = ctx.enter_context(nc.psum_tensor("PB", [128, 1024], F32))
        PC = ctx.enter_context(nc.psum_tensor("PC", [128, 1024], F32))
        PD = ctx.enter_context(nc.psum_tensor("PD", [128, 1024], F32))

        sems = {}
        for sname in ("ts", "vs", "ss", "dq0", "dq1", "dg0", "dg1"):
            sems[sname] = ctx.enter_context(nc.semaphore(sname))
        p = Prog(nc, sems)

        r_x = Rsrc("x")
        r_xc = [Rsrc(f"xc{i}") for i in range(KC)]
        r_h = Rsrc("h")
        r_ao = Rsrc("ao")
        r_q = Rsrc("q")
        r_h2 = Rsrc("h2")
        r_k = Rsrc("k")
        r_vt = Rsrc("vt")
        r_probs = [Rsrc(f"probs{i}") for i in range(4)]
        r_hf = Rsrc("hf")
        r_PA, r_PB = Rsrc("PA"), Rsrc("PB")
        r_PC, r_PD = Rsrc("PC"), Rsrc("PD")
        r_wq, r_wk, r_wv, r_wo = Rsrc("wq"), Rsrc("wk"), Rsrc("wv"), Rsrc("wo")
        r_wg = [Rsrc("wg0"), Rsrc("wg1")]
        r_wu = [Rsrc("wu0"), Rsrc("wu1")]
        r_sg = [Rsrc("sg0"), Rsrc("sg1")]
        r_wd = [Rsrc("wd0"), Rsrc("wd1")]
        r_stg = [Rsrc("stg0"), Rsrc("stg1")]
        r_stgi = r_stg
        r_stg4 = [r_stg[0], r_stg[1], Rsrc("stg2"), Rsrc("stg3")]
        r_rowA, r_rowB = Rsrc("rowA"), Rsrc("rowB")
        r_mrow, r_rrow = Rsrc("m_bf"), Rsrc("rstd_bf")
        r_bias, r_bv, r_bvbs, r_rb = (Rsrc("bias"), Rsrc("bv"),
                                      Rsrc("bv_bs"), Rsrc("rb"))
        r_const = Rsrc("const")
        r_xtmp, r_sqtmp, r_t1tmp = Rsrc("xtmp"), Rsrc("sqtmp"), Rsrc("t1tmp")
        PS = {id(PA): r_PA, id(PB): r_PB, id(PC): r_PC, id(PD): r_PD}
        RS_MBS = [r_probs[0], r_probs[1]]
        RS_RBS = [r_probs[2], r_probs[3]]
        m_bs = probs_sb[:, 0:2048]
        rs_bs = probs_sb[:, 2048:4096]

        p.op("v", lambda e: e.memset(ones_row[:], 1.0), writes=[r_const])
        p.op("v", lambda e: e.memset(ones_col[:], 1.0), writes=[r_const])
        p.op("v", lambda e: e.memset(ones_f32[:], 1.0), writes=[r_const])
        p.op("v", lambda e: e.memset(att_rbf[:], 0.0), writes=[r_rowB])
        p.op("v", lambda e: e.memset(eps_c[:], float(EPS)), writes=[r_const])
        p.inc("v")

        xevs = []
        for c in range(KC):
            xevs.append(p.dma("g", x_sb[:, c * SEQ:(c + 1) * SEQ],
                              xT[c * 128:(c + 1) * 128, :]))
            r_xc[c].w = [xevs[-1]]
        r_x.w = xevs

        def bias_block(row_src, n_chunks, col0):
            p.dma("q", rowA[0:1, 0:row_src.shape[1]], row_src, writes=[r_rowA])
            for j in range(n_chunks):
                p.op("t", lambda e, j=j, c0=col0: e.matmul(
                    PA[:, c0 + j:c0 + j + 1],
                    rowA[0:1, j * 128:(j + 1) * 128],
                    ones_f32[:], start=True, stop=True),
                    reads=[r_rowA, r_const], writes=[r_PA])
            p.inc("t")
        bias_block(bq_r, 8, 0)
        bias_block(bk_r, 2, 8)
        bias_block(bo_r, 8, 10)
        p.op("s", lambda e: e.copy(bias_sb[:], PA[:, 0:18]),
             reads=[r_PA], writes=[r_bias])
        p.inc("s")

        p.dma("q", bv_row[:], bv_r, writes=[r_bv])
        p.op("v", lambda e: e.tensor_copy(bv_rbf[:], bv_row[:]),
             reads=[r_bv], writes=[r_bv])
        p.inc("v")
        p.op("t", lambda e: e.matmul(PB[:, 0:DKV], ones_row[0:1, 0:128],
                                     bv_rbf[:], start=True, stop=True),
             reads=[r_bv, r_const], writes=[r_PB])
        p.inc("t")
        p.op("s", lambda e: e.copy(bv_bs[:], PB[:, 0:DKV]),
             reads=[r_PB], writes=[r_bvbs])
        p.inc("s")

        def stream32(src_ap, dst_ap, rsrcs, half):
            st = stg32[:, half * 1024:(half + 1) * 1024]
            p.dma("q", st, src_ap, writes=[r_stg[half]])
            p.op("v", lambda e, st=st, d=dst_ap: e.tensor_copy(d, st),
                 reads=[r_stg[half]], writes=list(rsrcs))
            p.inc("v")

        def streami(src_ap, dst_ap, rsrcs, slot, queue="q", extra_dma=()):
            t2 = (stgi, stgi, stgi2, stgi2)[slot]
            st = t2[:, (slot % 2) * 1024:(slot % 2 + 1) * 1024]
            p.dma(queue, st, src_ap,
                  writes=[r_stg4[slot]] + list(extra_dma))
            p.op("v", lambda e, st=st, d=dst_ap: e.tensor_copy(d, st),
                 reads=[r_stg4[slot]], writes=rsrcs)
            p.inc("v")

        for i in range(2):
            stream32(wk_a[:, i * 1024:(i + 1) * 1024],
                     wk_bf[:, i * 1024:(i + 1) * 1024], [r_wk], i % 2)
        for i in range(2):
            stream32(wv_a[:, i * 1024:(i + 1) * 1024],
                     wv_bf[:, i * 1024:(i + 1) * 1024], [r_wv], i % 2)
        for i in range(8):
            stream32(wq_a[:, i * 1024:(i + 1) * 1024],
                     wq_bf[:, i * 1024:(i + 1) * 1024], [r_wq], i % 2)

        def layernorm(src_res, dst_sb, dst_out_res, extra_scr=()):
            xb = k_sb[:, 0:2048]
            sq = vt_sb[:, 0:2048]
            t1 = vt_sb[:, 2048:4096]
            w_xb = [r_xtmp] + [r for r in extra_scr if r.name == "k"]
            w_sq = [r_sqtmp] + [r for r in extra_scr if r.name == "vt"]
            w_t1 = [r_t1tmp] + [r for r in extra_scr if r.name == "vt"]
            for c in range(KC):
                xs = x_sb[:, c * SEQ:(c + 1) * SEQ]
                p.op("v", lambda e, xs=xs: e.tensor_copy(xb, xs),
                     reads=[src_res], writes=w_xb)
                p.inc("v")
                p.op("s", lambda e, xs=xs: e.activation(sq, xs, AF.Square),
                     reads=[src_res], writes=w_sq)
                p.inc("s")
                for piece in range(4):
                    ps = (PA, PA, PB, PB)[piece]
                    off = (piece % 2) * 512
                    p.op("t", lambda e, ps=ps, off=off, piece=piece, c=c: e.matmul(
                        ps[0:1, off:off + 512], ones_col[:],
                        xb[:, piece * 512:(piece + 1) * 512],
                        start=(c == 0), stop=(c == KC - 1)),
                        reads=[r_xtmp, r_const], writes=[PS[id(ps)]])
                for piece in range(4):
                    ps = (PC, PC, PD, PD)[piece]
                    off = (piece % 2) * 512
                    p.op("t", lambda e, ps=ps, off=off, piece=piece, c=c: e.matmul(
                        ps[0:1, off:off + 512], ones_col[:],
                        sq[:, piece * 512:(piece + 1) * 512],
                        start=(c == 0), stop=(c == KC - 1)),
                        reads=[r_sqtmp, r_const], writes=[PS[id(ps)]])
                p.inc("t")
            for Hh in range(2):
                sum_ps = (PA, PB)[Hh]
                sq_ps = (PC, PD)[Hh]
                p.op("v", lambda e, ps=sum_ps: e.tensor_scalar(
                    rowA[:], ps[0:1, :], 1.0 / DM, None, ALU.mult),
                    reads=[PS[id(sum_ps)]], writes=[r_rowA])
                p.op("v", lambda e: e.tensor_mul(rowB[:], rowA[:], rowA[:]),
                     reads=[r_rowA], writes=[r_rowB])
                p.inc("v")
                p.op("s", lambda e, Hh=Hh: e.copy(
                    m_bf[0:1, Hh * 1024:(Hh + 1) * 1024], rowA[:]),
                    reads=[r_rowA], writes=[r_mrow, r_k])
                p.inc("s")
                p.op("v", lambda e, ps=sq_ps: e.scalar_tensor_tensor(
                    rowA[:], ps[0:1, :], 1.0 / DM, rowB[:],
                    ALU.mult, ALU.subtract),
                    reads=[PS[id(sq_ps)], r_rowB], writes=[r_rowA])
                p.inc("v")
                p.op("s", lambda e: e.activation(rowB[:], rowA[:], AF.Ln,
                                                 bias=eps_c[0:1, :]),
                     reads=[r_rowA, r_const], writes=[r_rowB])
                p.op("s", lambda e, Hh=Hh: e.activation(
                    rstd_bf[0:1, Hh * 1024:(Hh + 1) * 1024], rowB[:],
                    AF.Exp, scale=-0.5),
                    reads=[r_rowB], writes=[r_rrow, r_vt])
                p.inc("s")
            for piece in range(4):
                ps = (PA, PA, PB, PB)[piece]
                off = (piece % 2) * 512
                p.op("t", lambda e, ps=ps, off=off, piece=piece: e.matmul(
                    ps[:, off:off + 512], ones_row[0:1, 0:128],
                    m_bf[0:1, piece * 512:(piece + 1) * 512],
                    start=True, stop=True),
                    reads=[r_mrow, r_k, r_const], writes=[PS[id(ps)]])
                ps2 = (PC, PC, PD, PD)[piece]
                p.op("t", lambda e, ps=ps2, off=off, piece=piece: e.matmul(
                    ps[:, off:off + 512], ones_row[0:1, 0:128],
                    rstd_bf[0:1, piece * 512:(piece + 1) * 512],
                    start=True, stop=True),
                    reads=[r_rrow, r_vt, r_const], writes=[PS[id(ps2)]])
            p.inc("t")
            p.op("s", lambda e: e.copy(m_bs[:, 0:1024], PA[:]),
                 reads=[r_PA], writes=RS_MBS)
            p.op("s", lambda e: e.copy(m_bs[:, 1024:2048], PB[:]),
                 reads=[r_PB], writes=RS_MBS)
            p.op("s", lambda e: e.copy(rs_bs[:, 0:1024], PC[:]),
                 reads=[r_PC], writes=RS_RBS)
            p.op("s", lambda e: e.copy(rs_bs[:, 1024:2048], PD[:]),
                 reads=[r_PD], writes=RS_RBS)
            p.inc("s")
            for c in range(KC):
                xs = x_sb[:, c * SEQ:(c + 1) * SEQ]
                p.op("v", lambda e, xs=xs: e.tensor_sub(t1, xs, m_bs),
                     reads=[src_res] + RS_MBS, writes=w_t1)
                p.op("v", lambda e, c=c: e.tensor_mul(
                    dst_sb[:, c * SEQ:(c + 1) * SEQ], t1, rs_bs),
                    reads=[r_t1tmp] + RS_RBS, writes=dst_out_res)
                p.inc("v")

        Prog.cur_phase = "ln1"
        layernorm(r_x, h_sb, [r_h])
        Prog.cur_phase = "qproj"

        for mc in range(KC):
            ps_pair = (PA, PB) if mc % 2 == 0 else (PC, PD)
            rs_pair = [PS[id(ps_pair[0])], PS[id(ps_pair[1])]]
            for kc in range(KC):
                for nf in range(4):
                    ps = ps_pair[nf // 2]
                    off = (nf % 2) * 512
                    p.op("t", lambda e, ps=ps, off=off, kc=kc, mc=mc, nf=nf:
                         e.matmul(
                             ps[:, off:off + 512],
                             wq_bf[:, kc * 1024 + mc * 128:
                                   kc * 1024 + mc * 128 + 128],
                             h_sb[:, kc * SEQ + nf * 512:
                                  kc * SEQ + nf * 512 + 512],
                             start=(kc == 0), stop=(kc == KC - 1)),
                         reads=[r_wq, r_h], writes=rs_pair)
            p.inc("t")
            for half in range(2):
                ps = ps_pair[half]
                p.op("s", lambda e, ps=ps, mc=mc, half=half: e.activation(
                    q_sb[:, mc * SEQ + half * 1024:
                         mc * SEQ + half * 1024 + 1024],
                    ps[:], AF.Identity, bias=bias_sb[:, mc:mc + 1]),
                    reads=[rs_pair[half], r_bias], writes=[r_q])
            p.inc("s")

        Prog.cur_phase = "kproj"
        for mc2 in range(2):
            ps_pair = (PA, PB) if mc2 % 2 == 0 else (PC, PD)
            rs_pair = [PS[id(ps_pair[0])], PS[id(ps_pair[1])]]
            for kc in range(KC):
                for nf in range(4):
                    ps = ps_pair[nf // 2]
                    off = (nf % 2) * 512
                    p.op("t", lambda e, ps=ps, off=off, kc=kc, mc2=mc2, nf=nf:
                         e.matmul(
                             ps[:, off:off + 512],
                             wk_bf[:, kc * 256 + mc2 * 128:
                                   kc * 256 + mc2 * 128 + 128],
                             h_sb[:, kc * SEQ + nf * 512:
                                  kc * SEQ + nf * 512 + 512],
                             start=(kc == 0), stop=(kc == KC - 1)),
                         reads=[r_wk, r_h], writes=rs_pair)
            p.inc("t")
            for half in range(2):
                ps = ps_pair[half]
                p.op("s", lambda e, ps=ps, mc2=mc2, half=half: e.activation(
                    k_sb[:, mc2 * SEQ + half * 1024:
                         mc2 * SEQ + half * 1024 + 1024],
                    ps[:], AF.Identity, bias=bias_sb[:, 8 + mc2:9 + mc2]),
                    reads=[rs_pair[half], r_bias], writes=[r_k])
            p.inc("s")

        Prog.cur_phase = "vtok"
        def z_init(e):
            e.memset(AP(vt_sb, 0, [[NSC * VT_W, 128], [VT_W, NSC],
                                   [192, 3], [1, 64]]), 0.0)
        p.op("v", z_init, writes=[r_vt, r_sqtmp, r_t1tmp])
        p.op("v", lambda e: e.memset(
            AP(vt_sb, 0, [[NSC * VT_W, 128], [VT_W, NSC], [192, 3]]), 1.0),
            writes=[r_vt])
        p.inc("v")

        for sc in range(NSC):
            ps = (PA, PB)[sc % 2]
            rp = PS[id(ps)]
            for kc in range(KC):
                p.op("t", lambda e, ps=ps, sc=sc, kc=kc: e.matmul(
                    ps[:, 0:DKV],
                    h_sb[:, kc * SEQ + sc * 128: kc * SEQ + sc * 128 + 128],
                    wv_bf[:, kc * 256:(kc + 1) * 256],
                    start=(kc == 0), stop=(kc == KC - 1)),
                    reads=[r_h, r_wv], writes=[rp])
            p.inc("t")
            p.op("v", lambda e, ps=ps, sc=sc: e.tensor_add(
                AP(vt_sb, sc * VT_W + 64, [[NSC * VT_W, 128], [192, 2], [1, 128]]),
                ps[:, 0:DKV].rearrange("p (s c) -> p s c", s=2),
                bv_bs[:].rearrange("p (s c) -> p s c", s=2)),
                reads=[rp, r_bvbs], writes=[r_vt])
            p.inc("v")

        if "h1" in tap_aps:
            for c in range(KC):
                p.dma("g", tap_aps["h1"][c * 128:(c + 1) * 128, :],
                      h_sb[:, c * SEQ:(c + 1) * SEQ], reads=[r_h])
        if "q" in tap_aps:
            for c in range(KC):
                p.dma("g", tap_aps["q"][c * 128:(c + 1) * 128, :],
                      q_sb[:, c * SEQ:(c + 1) * SEQ], reads=[r_q])
        if "k" in tap_aps:
            for c in range(2):
                p.dma("g", tap_aps["k"][c * 128:(c + 1) * 128, :],
                      k_sb[:, c * SEQ:(c + 1) * SEQ], reads=[r_k])
        if "vt" in tap_aps:
            p.dma("g", tap_aps["vt"], vt_sb[:], reads=[r_vt])

        for i in range(8):
            stream32(wo_a[:, i * 1024:(i + 1) * 1024],
                     wq_bf[:, i * 1024:(i + 1) * 1024],
                     [r_wo, r_wq] if i < 2 else [r_wo], i % 2)

        Prog.cur_phase = "att"
        ev_exp_tail = None
        ev_qk = {}
        for R in (r_PA, r_PB, r_PC, r_PD):
            for ev in R.w + R.r:
                p._wait("t", ev)
        for R in (r_k, r_q, r_vt):
            for ev in R.w:
                p._wait("t", ev)
        for ev in (r_rowA.w + r_rowA.r + r_rowB.w + r_rowB.r +
                   r_ao.w + r_ao.r + r_h.r + r_h.w):
            p._wait("v", ev)

        def ring_of(si):
            return (PA, PB, PD) if si % 2 == 0 else (PA, PB, PC)

        def avbuf_of(si):
            return PC if si % 2 == 0 else PD

        def t_drain_inc():
            val = p.count["ts"] + 1
            p.count["ts"] = val
            p.stream["t"].append(
                lambda eng, h2=p.sems["ts"]: eng.maybe_drain_then_inc(
                    (h2, 1), fusable=True))
            fcl = dict(p.floor["t"])
            fcl["ts"] = val
            p.evclock[("ts", val)] = fcl
            return ("ts", val)

        slots = [(c, qb) for c in range(8) for qb in range(4)]

        def emit_qk(slot_idx, g):
            c, qb = slots[slot_idx]
            ps = ring_of(slot_idx)[g % 3]
            for side, hh in ((0, A_HEADS[c]), (1, B_HEADS[c])):
                kc2 = (hh // 4) // 2
                p.raw("t", lambda e, ps=ps, sc=g, side=side, c=c, qb=qb,
                      kc2=kc2: e.matmul(
                          ps[:, side * 512:(side + 1) * 512],
                          k_sb[side * 64: side * 64 + 64,
                               kc2 * SEQ + sc * 128: kc2 * SEQ + sc * 128 + 128],
                          q_sb[side * 64: side * 64 + 64,
                               c * SEQ + qb * 512: c * SEQ + qb * 512 + 512],
                          start=True, stop=True))
            ev_qk[(slot_idx, g)] = p.raw_inc("t")

        ev_ave_par = {0: None, 1: None}
        ev_norm_final = None
        for si, (c, qb) in enumerate(slots):
            AVB = avbuf_of(si)
            soA = VT_STAT_OFF[A_HEADS[c] // 4]
            soB = VT_STAT_OFF[B_HEADS[c] // 4]
            ev_exp = {}
            ev_av = {}
            if si == 0:
                for g0 in (0, 1, 2):
                    emit_qk(0, g0)
            for g in range(NSC):
                lag = ev_qk.get((si, g + 1)) or ev_av[g - 2]
                p._wait("s", lag)
                ps = ring_of(si)[g % 3]
                pb = probs_sb[:, (g % 4) * 1024:(g % 4 + 1) * 1024]
                p.raw("s", lambda e, ps=ps, pb=pb: e.activation(
                    pb, ps[:], AF.Exp, scale=ATT_SCALE))
                ev_exp[g] = p.raw_inc("s")
                if g + 3 < NSC:
                    p._wait("t", ev_exp[g])
                    emit_qk(si, g + 3)
                p._wait("t", ev_exp[g])
                if g == 0 and ev_ave_par[si % 2] is not None:
                    p._wait("t", ev_ave_par[si % 2])
                for side, so in ((0, soA), (1, soB)):
                    p.raw("t", lambda e, g=g, side=side, so=so, AVB=AVB:
                          e.matmul(
                              AVB[:, side * 512:(side + 1) * 512],
                              vt_sb[:, g * VT_W + so: g * VT_W + so + 128],
                              probs_sb[:, (g % 4) * 1024 + side * 512:
                                       (g % 4) * 1024 + side * 512 + 512],
                              start=(g == 0), stop=(g == NSC - 1)))
                ev_av[g] = p.raw_inc("t")
            if si + 1 < len(slots):
                p._wait("t", ev_exp[NSC - 1])
                emit_qk(si + 1, 0)
                emit_qk(si + 1, 1)
                ev_lag_tail = ev_qk[(si + 1, 1)]
            else:
                ev_lag_tail = t_drain_inc()
            p._wait("v", ev_lag_tail)
            p.raw("v", lambda e, AVB=AVB: e.tensor_copy(av_f32[:], AVB[:]))
            ev_ave = p.raw_inc("v")
            ev_ave_par[si % 2] = ev_ave
            if si + 1 < len(slots):
                p._wait("t", ev_ave)
                emit_qk(si + 1, 2)
            p._wait("v", ev_ave)
            for side, hh in ((0, A_HEADS[c]), (1, B_HEADS[c])):
                cs = side * 512
                srow = 64 * (1 - side)
                drow = side * 64
                p.raw("v", lambda e, sr=srow, cs=cs: e.reciprocal(
                    av_f32[sr:sr + 1, cs:cs + 512],
                    av_f32[sr:sr + 1, cs:cs + 512]))
                p.raw("v", lambda e, sr=srow, cs=cs: e.tensor_copy(
                    att_rbf[sr:sr + 1, cs:cs + 512],
                    av_f32[sr:sr + 1, cs:cs + 512]))
                sq = 64 * (1 - side)
                p.raw("v", lambda e, sq=sq, dr=drow, cs=cs: e.stream_shuffle(
                    att_rbf[dr:dr + 64, cs:cs + 512],
                    att_rbf[sq:sq + 64, cs:cs + 512], [0] * 32))
                p.raw("v", lambda e, dr=drow, cs=cs: e.stream_shuffle(
                    att_rbf[dr + 32:dr + 64, cs:cs + 512],
                    att_rbf[dr:dr + 32, cs:cs + 512], [0] * 32))
                p.raw("v", lambda e, dr=drow, cs=cs, c=c, qb=qb: e.tensor_mul(
                    h_sb[dr:dr + 64,
                         c * SEQ + qb * 512: c * SEQ + qb * 512 + 512],
                    av_f32[dr:dr + 64, cs:cs + 512],
                    att_rbf[dr:dr + 64, cs:cs + 512]))
            ev_norm_final = p.raw_inc("v")
            ev_exp_tail = ev_exp[NSC - 1]

        ev_fin = ev_norm_final
        p._wait("t", ev_fin)
        for R in (r_PA, r_PB, r_PC, r_PD, r_rowA, r_rowB, r_rb):
            R.w = [ev_fin]
            R.r = []
            R.dirty_by = None
        r_ao.w = [ev_fin]
        r_ao.r = []
        r_h.w = [ev_fin]
        r_h.r = []
        for R in (r_q, r_k, r_vt):
            R.r = [ev_fin]
        for R in r_probs:
            R.w = [ev_fin]
            R.r = []

        if "ao" in tap_aps:
            for c in range(KC):
                p.dma("g", tap_aps["ao"][c * 128:(c + 1) * 128, :],
                      h_sb[:, c * SEQ:(c + 1) * SEQ], reads=[r_ao])

        Prog.cur_phase = "wo"

        def r2(tensor, coloff, width=1024):
            return AP(tensor, coloff, [[1024, 33], [1, width]])

        def wo_mc2(mc):
            for Hh in range(2):
                ps = (PA, PB)[Hh]
                rp = PS[id(ps)]
                for kc in range(KC):
                    for nf in range(2):
                        p.op("t", lambda e, ps=ps, kc=kc, mc=mc, nf=nf,
                             Hh=Hh: e.matmul(
                                 ps[:, nf * 512:(nf + 1) * 512],
                                 wq_bf[:, kc * 1024 + mc * 128:
                                       kc * 1024 + mc * 128 + 128],
                                 h_sb[:, kc * SEQ + Hh * 1024 + nf * 512:
                                      kc * SEQ + Hh * 1024 + nf * 512 + 512],
                                 start=(kc == 0), stop=(kc == KC - 1)),
                             reads=[r_wo, r_ao], writes=[rp])
                p.inc("t")
                xs = x_sb[:, mc * SEQ + Hh * 1024: mc * SEQ + Hh * 1024 + 1024]
                p.op("v", lambda e, ps=ps, xs=xs, mc=mc: e.scalar_tensor_tensor(
                    xs, ps[:], bias_sb[:, 10 + mc:11 + mc], xs,
                    ALU.add, ALU.add),
                    reads=[rp, r_bias, r_x, r_xc[mc]], writes=[r_xc[mc]])
                p.inc("v")

        def ln2_stats_chunk(c):
            xb = k_sb[:, 0:2048]
            sq = vt_sb[:, 0:2048]
            xs = x_sb[:, c * SEQ:(c + 1) * SEQ]
            p.op("v", lambda e, xs=xs: e.tensor_copy(xb, xs),
                 reads=[r_xc[c]], writes=[r_xtmp, r_k])
            p.inc("v")
            p.op("s", lambda e, xs=xs: e.activation(sq, xs, AF.Square),
                 reads=[r_xc[c]], writes=[r_sqtmp, r_vt])
            p.inc("s")
            for tb in range(4):
                row = (tb // 2) * 32
                col = (tb % 2) * 512
                p.op("t", lambda e, row=row, col=col, tb=tb, c=c: e.matmul(
                    PC[row:row + 1, col:col + 512], ones_col[:],
                    xb[:, tb * 512:(tb + 1) * 512],
                    start=(c == 0), stop=(c == KC - 1)),
                    reads=[r_xtmp, r_const], writes=[r_PC])
            for tb in range(4):
                row = (tb // 2) * 32
                col = (tb % 2) * 512
                p.op("t", lambda e, row=row, col=col, tb=tb, c=c: e.matmul(
                    PD[row:row + 1, col:col + 512], ones_col[:],
                    sq[:, tb * 512:(tb + 1) * 512],
                    start=(c == 0), stop=(c == KC - 1)),
                    reads=[r_sqtmp, r_const], writes=[r_PD])
            p.inc("t")

        p.op("v", lambda e: e.memset(PC[0:32, :], 0.0),
             reads=(), writes=[r_PC])
        p.op("v", lambda e: e.memset(PD[0:32, :], 0.0),
             reads=(), writes=[r_PD])
        p.inc("v")
        for c in range(KC):
            wo_mc2(c)
            ln2_stats_chunk(c)

        Prog.cur_phase = "ln2"
        p.op("v", lambda e: e.tensor_scalar(
            r2(av_f32, 0), r2(PC, 0), 1.0 / DM, None, ALU.mult),
            reads=[r_PC], writes=[r_rowA])
        p.inc("v")
        p.op("s", lambda e: e.copy(r2(m_bf2, 0), r2(av_f32, 0)),
             reads=[r_rowA], writes=[r_mrow, r_k])
        p.inc("s")
        p.op("v", lambda e: e.tensor_mul(r2(av_f32, 0), r2(av_f32, 0),
                                         r2(av_f32, 0)),
             reads=[r_rowA], writes=[r_rowA])
        p.op("v", lambda e: e.scalar_tensor_tensor(
            r2(av_f32, 0), r2(PD, 0), 1.0 / DM, r2(av_f32, 0),
            ALU.mult, ALU.subtract),
            reads=[r_PD, r_rowA], writes=[r_rowA])
        p.inc("v")
        p.op("s", lambda e: e.activation(
            r2(av_f32, 0), r2(av_f32, 0), AF.Ln,
            bias=eps_c[0:33, :]),
            reads=[r_rowA, r_const], writes=[r_rowA])
        p.op("s", lambda e: e.activation(r2(rstd_bf2, 0), r2(av_f32, 0),
                                         AF.Exp, scale=-0.5),
             reads=[r_rowA], writes=[r_rrow, r_vt])
        p.inc("s")
        for tb in range(4):
            row = (tb // 2) * 32
            col = (tb % 2) * 512
            ps = PC if tb < 2 else PD
            p.op("t", lambda e, ps=ps, row=row, col=col: e.matmul(
                ps[:, col:col + 512], ones_row[row:row + 1, 0:128],
                AP(m_bf2, row * 1024 + col, [[1024, 1], [1, 512]]),
                start=True, stop=True),
                reads=[r_mrow, r_k, r_const],
                writes=[r_PC if tb < 2 else r_PD])
        p.inc("t")
        mb_s = probs_sb[:, 0:2048]
        rb_s = probs_sb[:, 2048:4096]
        p.op("s", lambda e: e.copy(mb_s[:, 0:1024], PC[:]),
             reads=[r_PC], writes=[r_probs[0], r_probs[1]])
        p.op("s", lambda e: e.copy(mb_s[:, 1024:2048], PD[:]),
             reads=[r_PD], writes=[r_probs[0], r_probs[1]])
        p.inc("s")
        for tb in range(4):
            row = (tb // 2) * 32
            col = (tb % 2) * 512
            ps = PC if tb < 2 else PD
            p.op("t", lambda e, ps=ps, row=row, col=col: e.matmul(
                ps[:, col:col + 512], ones_row[row:row + 1, 0:128],
                AP(rstd_bf2, row * 1024 + col, [[1024, 1], [1, 512]]),
                start=True, stop=True),
                reads=[r_rrow, r_vt, r_const],
                writes=[r_PC if tb < 2 else r_PD])
        p.inc("t")
        p.op("s", lambda e: e.copy(rb_s[:, 0:1024], PC[:]),
             reads=[r_PC], writes=[r_probs[2], r_probs[3]])
        p.op("s", lambda e: e.copy(rb_s[:, 1024:2048], PD[:]),
             reads=[r_PD], writes=[r_probs[2], r_probs[3]])
        p.inc("s")
        for c in range(KC):
            t1 = vt_sb[:, 2048:4096]
            xs = x_sb[:, c * SEQ:(c + 1) * SEQ]
            p.op("v", lambda e, xs=xs, t1=t1: e.tensor_sub(t1, xs, mb_s),
                 reads=[r_xc[c], r_probs[0], r_probs[1]],
                 writes=[r_t1tmp, r_vt])
            p.op("v", lambda e, c=c, t1=t1: e.tensor_mul(
                q_sb[:, c * SEQ:(c + 1) * SEQ], t1, rb_s),
                reads=[r_t1tmp, r_probs[2], r_probs[3]],
                writes=[r_h2, r_q])
            p.inc("v")
        Prog.cur_phase = "ffn"

        if "h2" in tap_aps:
            for c in range(KC):
                p.dma("g", tap_aps["h2"][c * 128:(c + 1) * 128, :],
                      q_sb[:, c * SEQ:(c + 1) * SEQ], reads=[r_h2])

        first_ffn = [r_ao, r_h, r_k, r_vt]

        def stream_gu(th, hc):
            half = hc % 2
            fw = [r_wo] if (th == 0 and hc < 2) else []
            fd = (([r_wo, r_wk, r_wv] if th == 0 else
                   [r_wd[0], r_wd[1]]) if hc < 4 else [])
            streami(wg_a[hc], wg_buf[:, half * 1024:(half + 1) * 1024],
                    [r_wg[half]] + fw, (0, 2)[hc % 2], queue="q",
                    extra_dma=fd)
            streami(wu_a[hc], wu_buf[:, half * 1024:(half + 1) * 1024],
                    [r_wu[half]] + fw, (1, 3)[hc % 2], queue="g",
                    extra_dma=fd)

        def stream_wd(th, mc):
            half = mc % 2
            fwd = [r_wo, r_wk, r_wv] if (th == 0 and mc < 2) else []
            for j in range(3):
                ovl = []
                byte0 = half * 6144 + j * 2048
                if byte0 < 4096:
                    ovl.append(r_stg4[2])
                if byte0 + 2048 > 4096 and byte0 < 8192:
                    ovl.append(r_stg4[3])
                streami(wd_a[mc][:, j * 1024:(j + 1) * 1024],
                        wd_buf[:, half * FH + j * 1024:
                              half * FH + j * 1024 + 1024],
                        [r_wd[half]] + fwd + ovl, j % 2,
                        queue="q" if j % 2 == 0 else "g")

        for th in range(2):
            stream_gu(th, 0)
            for hc in range(HC):
                half = hc % 2
                extra = first_ffn if (th == 0 and hc < 2) else []
                gps = (PA, PB)[half]
                ups = (PC, PD)[half]
                for kc in range(KC):
                    for nf in range(2):
                        p.op("t", lambda e, gps=gps, kc=kc, nf=nf, th=th,
                             half=half: e.matmul(
                                 gps[:, nf * 512:(nf + 1) * 512],
                                 wg_buf[:, half * 1024 + kc * 128:
                                       half * 1024 + kc * 128 + 128],
                                 q_sb[:, kc * SEQ + th * 1024 + nf * 512:
                                      kc * SEQ + th * 1024 + nf * 512 + 512],
                                 start=(kc == 0), stop=(kc == KC - 1)),
                             reads=[r_wg[half], r_h2], writes=[PS[id(gps)]])
                for kc in range(KC):
                    for nf in range(2):
                        p.op("t", lambda e, ups=ups, kc=kc, nf=nf, th=th,
                             half=half: e.matmul(
                                 ups[:, nf * 512:(nf + 1) * 512],
                                 wu_buf[:, half * 1024 + kc * 128:
                                       half * 1024 + kc * 128 + 128],
                                 q_sb[:, kc * SEQ + th * 1024 + nf * 512:
                                      kc * SEQ + th * 1024 + nf * 512 + 512],
                                 start=(kc == 0), stop=(kc == KC - 1)),
                             reads=[r_wu[half], r_h2], writes=[PS[id(ups)]])
                p.inc("t")
                p.op("s", lambda e, gps=gps, half=half: e.activation(
                    sg_buf[:, half * 1024:(half + 1) * 1024], gps[:],
                    AF.Silu, scale=DQS),
                    reads=[PS[id(gps)]], writes=[r_sg[half]])
                p.inc("s")
                if hc + 1 < HC:
                    stream_gu(th, hc + 1)
                p.op("v", lambda e, ups=ups, hc=hc, half=half: e.tensor_mul(
                    hf_sb[:, hc * 1024:(hc + 1) * 1024], ups[:],
                    sg_buf[:, half * 1024:(half + 1) * 1024]),
                    reads=[PS[id(ups)], r_sg[half]],
                    writes=[r_hf] + extra)
                p.inc("v")
            stream_wd(th, 0)
            for mc in range(KC):
                half = mc % 2
                dps = (PA, PB)[half]
                for hcc in range(HC):
                    for nf in range(2):
                        p.op("t", lambda e, dps=dps, hcc=hcc, nf=nf,
                             half=half: e.matmul(
                                 dps[:, nf * 512:(nf + 1) * 512],
                                 wd_buf[:, half * FH + hcc * 128:
                                       half * FH + hcc * 128 + 128],
                                 hf_sb[:, hcc * 1024 + nf * 512:
                                      hcc * 1024 + nf * 512 + 512],
                                 start=(hcc == 0), stop=(hcc == HC - 1)),
                             reads=[r_wd[half], r_hf], writes=[PS[id(dps)]])
                p.inc("t")
                if mc + 1 < KC:
                    stream_wd(th, mc + 1)
                xs = x_sb[:, mc * SEQ + th * 1024: mc * SEQ + th * 1024 + 1024]
                p.op("v", lambda e, dps=dps, xs=xs: e.scalar_tensor_tensor(
                    xs, dps[:], DQS * DQS, xs, ALU.mult, ALU.add),
                    reads=[PS[id(dps)], r_xc[mc]], writes=[r_xc[mc]])
                p.inc("v")

        for c in range(KC):
            p.dma("g", yT[c * 128:(c + 1) * 128, :],
                  x_sb[:, c * SEQ:(c + 1) * SEQ],
                  reads=[r_x, r_xc[c]])
        for s2 in ("dg0", "dg1", "dq0", "dq1"):
            p._wait("g", (s2, p.count[s2]))

        with nc.Block() as block:
            @block.tensor
            def _(eng):
                for fn in p.stream["t"]:
                    fn(eng)

            @block.vector
            def _(eng):
                for fn in p.stream["v"]:
                    fn(eng)

            @block.scalar
            def _(eng):
                for fn in p.stream["s"]:
                    fn(eng)

            @block.sync
            def _(eng):
                for fn in p.stream["q"]:
                    fn(eng)

            @block.gpsimd
            def _(eng):
                for fn in p.stream["g"]:
                    fn(eng)

    return nc


def prepare_inputs(inputs):
    x = np.asarray(inputs["L0_attn_x"])
    wq = np.asarray(inputs["L0_attn_wq"])
    bq = np.asarray(inputs["L0_attn_bq"])
    wk = np.asarray(inputs["L0_attn_wk"])
    bk = np.asarray(inputs["L0_attn_bk"])
    wv = np.asarray(inputs["L0_attn_wv"])
    bv = np.asarray(inputs["L0_attn_bv"])
    wo = np.asarray(inputs["L0_attn_wo"])
    bo = np.asarray(inputs["L0_attn_bo"])
    wg = np.asarray(inputs["L0_ffn_w_gate"])
    wu = np.asarray(inputs["L0_ffn_w_up"])
    wd = np.asarray(inputs["L0_ffn_w_down"])

    wq_c = wq[PERM_F]
    bq_c = bq[PERM_F]
    perm_kv = np.concatenate([np.arange(64) + k * 64 for k in VT_KV_ORDER])
    wv_c = wv[perm_kv]
    bv_c = bv[perm_kv]
    wo_c = wo[:, PERM_F]

    wq_al = np.ascontiguousarray(
        wq_c.reshape(KC, 128, KC, 128).transpose(3, 2, 0, 1)).reshape(128, KC * DM)
    wk_al = np.ascontiguousarray(
        wk.reshape(2, 128, KC, 128).transpose(3, 2, 0, 1)).reshape(128, KC * DKV)
    wv_al = np.ascontiguousarray(
        wv_c.reshape(DKV, KC, 128).transpose(2, 1, 0)).reshape(128, KC * DKV)
    wo_al = np.ascontiguousarray(
        wo_c.reshape(KC, 128, KC, 128).transpose(3, 2, 0, 1)).reshape(128, KC * DM)
    wg_al = np.ascontiguousarray(
        wg.reshape(HC, 128, KC, 128).transpose(0, 3, 2, 1)).reshape(HC, 128, DM)
    wu_al = np.ascontiguousarray(
        wu.reshape(HC, 128, KC, 128).transpose(0, 3, 2, 1)).reshape(HC, 128, DM)
    wd_al = np.ascontiguousarray(
        wd.reshape(KC, 128, HC, 128).transpose(0, 3, 2, 1)).reshape(KC, 128, FH)

    shared = {
        "wq_a": wq_al, "wk_a": wk_al, "wv_a": wv_al, "wo_a": wo_al,
        "bq_r": np.ascontiguousarray(bq_c.reshape(1, DM)),
        "bk_r": np.ascontiguousarray(bk.reshape(1, DKV)),
        "bv_r": np.ascontiguousarray(bv_c.reshape(1, DKV)),
        "bo_r": np.ascontiguousarray(bo.reshape(1, DM)),
        "wg_a": wg_al, "wu_a": wu_al, "wd_a": wd_al,
    }
    in_maps = []
    for i in range(8):
        m = dict(shared)
        m["xT"] = np.ascontiguousarray(x[i].T)
        in_maps.append(m)
    return in_maps


def kernel(**inputs):
    in_maps = prepare_inputs(inputs)
    nc = build_program()
    res = run_bass_kernel_spmd(nc, in_maps, core_ids=list(range(8)))
    out = np.stack([np.ascontiguousarray(np.asarray(r["yT"]).T)
                    for r in res.results], axis=0)
    return out.astype(np.float32, copy=False)


# version 20 (final)
# speedup vs baseline: 1.3619x; 1.0145x over previous
import numpy as np
from contextlib import ExitStack

import concourse.bass as bass
import concourse.mybir as mybir
from concourse.ap import AP
from concourse.bass_utils import run_bass_kernel_spmd

F32 = mybir.dt.float32
BF16 = mybir.dt.bfloat16
I32 = mybir.dt.int32
AF = mybir.ActivationFunctionType
ALU = mybir.AluOpType

SEQ = 2048
DM = 1024
DKV = 256
NH, NKV, DH = 16, 4, 64
FH = 3072
KC = DM // 128
HC = FH // 128
NSC = SEQ // 128
EPS = 1e-5
ATT_SCALE = 0.125
DQS = 1.0 / 127.0

A_HEADS = [0, 1, 2, 3, 8, 9, 10, 11]
B_HEADS = [4, 5, 6, 7, 12, 13, 14, 15]
HEAD_POS = {}
for _c in range(8):
    HEAD_POS[A_HEADS[_c]] = (_c, 0)
    HEAD_POS[B_HEADS[_c]] = (_c, 1)
PERM_F = np.zeros(DM, np.int64)
for _h, (_c, _p) in HEAD_POS.items():
    PERM_F[_c * 128 + _p * 64: _c * 128 + _p * 64 + 64] = np.arange(64) + _h * 64

VT_W = 448
VT_STAT_OFF = {0: 128, 1: 0, 2: 320, 3: 192}
VT_KV_ORDER = [1, 0, 3, 2]


class Rsrc:
    __slots__ = ("name", "w", "r", "dirty_by")

    def __init__(self, name):
        self.name = name
        self.w = []
        self.r = []
        self.dirty_by = None


class Prog:

    serialize = False
    ser_phases = set()
    cur_phase = ""
    trace_log = None

    def __init__(self, nc, sems):
        self.nc = nc
        self.stream = {e: [] for e in ("t", "v", "s", "g", "q")}
        self.sems = sems
        self.count = {k: 0 for k in sems}
        self.floor = {e: {} for e in self.stream}
        self.pending = {e: ([], []) for e in self.stream}
        self.last_op_idx = {e: -1 for e in self.stream}
        self.evclock = {}
        self.eng_sem = {"t": "ts", "v": "vs", "s": "ss"}
        self.dma_sem = {"q": ["dq0", "dq1"], "g": ["dg0", "dg1"]}
        self.dma_rr = {"q": 0, "g": 0}

    def _wait(self, e, ev):
        sem, val = ev
        if self.floor[e].get(sem, 0) >= val:
            return
        assert val <= self.count[sem], (e, sem, val, self.count[sem])
        h = self.sems[sem]
        self.stream[e].append(lambda eng, h=h, v=val: eng.wait_ge(h, v))
        if Prog.trace_log is not None:
            Prog.trace_log.append(("W", e, sem, val, Prog.cur_phase))
        clk = self.evclock.get((sem, val), {})
        f = self.floor[e]
        for s2, v2 in clk.items():
            if f.get(s2, 0) < v2:
                f[s2] = v2
        if f.get(sem, 0) < val:
            f[sem] = val

    def _ser(self, e):
        if not (Prog.serialize or Prog.cur_phase in Prog.ser_phases):
            return
        for sem, cnt in self.count.items():
            if cnt:
                self._wait(e, (sem, cnt))

    def raw(self, e, fn):
        self.stream[e].append(fn)
        self.last_op_idx[e] = len(self.stream[e]) - 1

    def raw_inc(self, e):
        sem = self.eng_sem[e]
        self.count[sem] += 1
        val = self.count[sem]
        h = self.sems[sem]
        i = self.last_op_idx[e]
        assert i == len(self.stream[e]) - 1, (e, i, len(self.stream[e]))
        old_fn = self.stream[e][i]
        self.stream[e][i] = (
            lambda eng, f=old_fn, h=h: f(eng).then_inc(h, 1))
        if Prog.trace_log is not None:
            Prog.trace_log.append(("i", e, sem, val, Prog.cur_phase))
        f = dict(self.floor[e])
        f[sem] = val
        self.evclock[(sem, val)] = f
        return (sem, val)

    def op(self, e, fn, reads=(), writes=()):
        self._ser(e)
        for R in reads:
            assert R.dirty_by in (None, e), f"read of dirty {R.name} on {e}"
            for ev in R.w:
                self._wait(e, ev)
        for R in writes:
            assert R.dirty_by in (None, e), f"write of dirty {R.name} on {e}"
            for ev in R.w:
                self._wait(e, ev)
            for ev in R.r:
                self._wait(e, ev)
        self.stream[e].append(fn)
        self.last_op_idx[e] = len(self.stream[e]) - 1
        pr, pw = self.pending[e]
        pr.extend(reads)
        pw.extend(writes)
        for R in writes:
            R.dirty_by = e
        for R in reads:
            if R.dirty_by is None:
                R.dirty_by = e

    def inc(self, e, drain=None):
        if drain is None:
            drain = (e == "t")
        sem = self.eng_sem[e]
        self.count[sem] += 1
        val = self.count[sem]
        h = self.sems[sem]
        if drain:
            self.stream[e].append(
                lambda eng, h=h: eng.maybe_drain_then_inc((h, 1), fusable=True))
        else:
            i = self.last_op_idx[e]
            assert i == len(self.stream[e]) - 1, (e, i, len(self.stream[e]))
            old_fn = self.stream[e][i]
            self.stream[e][i] = (
                lambda eng, f=old_fn, h=h: f(eng).then_inc(h, 1))
        if Prog.trace_log is not None:
            Prog.trace_log.append(("I", e, sem, val, Prog.cur_phase))
        f = dict(self.floor[e])
        f[sem] = val
        self.evclock[(sem, val)] = f
        ev = (sem, val)
        pr, pw = self.pending[e]
        for R in pw:
            R.w = [ev]
            R.r = []
            R.dirty_by = None
        for R in pr:
            if R.dirty_by == e:
                R.dirty_by = None
            R.r.append(ev)
        self.pending[e] = ([], [])
        return ev

    def dma(self, e, out_ap, in_ap, reads=(), writes=()):
        self._ser(e)
        for R in reads:
            assert R.dirty_by in (None, e)
            for ev in R.w:
                self._wait(e, ev)
        for R in writes:
            assert R.dirty_by in (None, e)
            for ev in R.w + R.r:
                self._wait(e, ev)
        sem = self.dma_sem[e][self.dma_rr[e]]
        self.dma_rr[e] ^= 1
        h = self.sems[sem]
        self.count[sem] += 16
        val = self.count[sem]
        self.stream[e].append(
            lambda eng, h=h, o=out_ap, i=in_ap: eng.dma_start(o, i).then_inc(h, 16))
        if Prog.trace_log is not None:
            Prog.trace_log.append(("D", e, sem, val, Prog.cur_phase))
        f = dict(self.floor[e])
        f[sem] = val
        self.evclock[(sem, val)] = f
        ev = (sem, val)
        for R in writes:
            R.w = [ev]
            R.r = []
        for R in reads:
            R.r.append(ev)
        return ev


def build_program(taps=(), serialize=False, ser_phases=()):
    taps = set(taps)
    Prog.serialize = serialize
    Prog.ser_phases = set(ser_phases)
    Prog.cur_phase = "pre"
    nc = bass.Bass("TRN2", target_bir_lowering=False,
                   dynamic_dma_scratch_size=16384,
                   detect_race_conditions=False)

    def din(name, shape, dtype):
        return nc.dram_tensor(name, list(shape), dtype, kind="ExternalInput").ap()

    xT = din("xT", (DM, SEQ), F32)
    wq_a = din("wq_a", (128, KC * DM), F32)
    wk_a = din("wk_a", (128, KC * DKV), F32)
    wv_a = din("wv_a", (128, KC * DKV), F32)
    wo_a = din("wo_a", (128, KC * DM), F32)
    bq_r = din("bq_r", (1, DM), F32)
    bk_r = din("bk_r", (1, DKV), F32)
    bv_r = din("bv_r", (1, DKV), F32)
    bo_r = din("bo_r", (1, DM), F32)
    wg_a = din("wg_a", (HC, 128, DM), I32)
    wu_a = din("wu_a", (HC, 128, DM), I32)
    wd_a = din("wd_a", (KC, 128, FH), I32)
    yT = nc.dram_tensor("yT", [DM, SEQ], F32, kind="ExternalOutput").ap()

    tap_aps = {}
    for tname in taps:
        shp = {"h1": (DM, SEQ), "q": (DM, SEQ), "k": (DKV, SEQ),
               "vt": (128, NSC * VT_W), "ao": (DM, SEQ), "x1": (DM, SEQ),
               "h2": (DM, SEQ)}[tname]
        dt = F32 if tname == "x1" else BF16
        tap_aps[tname] = nc.dram_tensor("tap_" + tname, list(shp), dt,
                                        kind="ExternalOutput").ap()

    with ExitStack() as ctx:
        base = (nc.sbuf_base + 31) & ~31
        cur = [base]

        offs = {}

        def place(name, shape, dtype, at=None):
            if at is None:
                off = cur[0]
                per = int(np.prod(shape[1:])) * mybir.dt.size(dtype)
                cur[0] = (off + per + 31) & ~31
            else:
                off = at
            h = nc.alloc_sbuf_tensor_at(name, list(shape), dtype, offset=off)
            offs[name] = off
            assert cur[0] <= nc.sbuf_top, (name, cur[0], nc.sbuf_top)
            return h

        x_sb = place("x_sb", [128, KC * SEQ], F32)
        B0 = cur[0]
        h_sb = place("h_sb", [128, KC * SEQ], BF16)
        k_sb = place("k_sb", [128, 2 * SEQ], BF16)
        vt_sb = place("vt_sb", [128, NSC * VT_W], BF16)
        probs_sb = place("probs_sb", [128, 4 * 1024], BF16)
        hf_sb = place("hf_sb", [128, HC * 1024], BF16, at=B0)
        q_sb = place("q_sb", [128, KC * SEQ], BF16)
        W0 = cur[0]
        wq_bf = place("wq_bf", [128, KC * DM], BF16)
        wk_bf = place("wk_bf", [128, KC * DKV], BF16)
        wv_bf = place("wv_bf", [128, KC * DKV], BF16)
        wg_buf = place("wg_buf", [128, 2 * DM], BF16, at=W0)
        wu_buf = place("wu_buf", [128, 2 * DM], BF16, at=W0 + 4096)
        sg_buf = place("sg_buf", [128, 2 * 1024], BF16, at=W0 + 8192)
        wd_buf = place("wd_buf", [128, 2 * FH], BF16, at=W0 + 12288)
        stg32 = place("stg32", [128, 2048], F32)
        stgi = place("stgi", [128, 2048], I32, at=offs["stg32"])
        stgi2 = place("stgi2", [128, 2048], I32, at=W0 + 12288)
        ones_row = place("ones_row", [128, 128], BF16)
        ones_col = place("ones_col", [128, 1], BF16)
        ones_f32 = place("ones_f32", [1, 1], F32)
        eps_c = place("eps_c", [128, 1], F32)
        rowA = place("rowA", [1, 1024], F32)
        rowB = place("rowB", [1, 1024], F32)
        bv_row = place("bv_row", [1, DKV], F32)
        bv_rbf = place("bv_rbf", [1, DKV], BF16)
        bv_bs = place("bv_bs", [128, DKV], BF16)
        m_bf = place("m_bf", [1, SEQ], BF16, at=offs["k_sb"] + 4096)
        rstd_bf = place("rstd_bf", [1, SEQ], BF16, at=offs["vt_sb"] + 8192)
        m_bf2 = place("m_bf2", [128, 1024], BF16, at=offs["k_sb"] + 4096)
        rstd_bf2 = place("rstd_bf2", [128, 1024], BF16,
                         at=offs["vt_sb"] + 8192)
        av_f32 = place("av_f32", [128, 1024], F32, at=offs["rowA"])
        att_rbf = place("att_rbf", [128, 1024], BF16, at=offs["rowB"])
        rb_bf = place("rb_bf", [128, 1024], BF16, at=offs["rowB"] + 2048)
        bias_sb = place("bias_sb", [128, 18], F32)

        PA = ctx.enter_context(nc.psum_tensor("PA", [128, 1024], F32))
        PB = ctx.enter_context(nc.psum_tensor("PB", [128, 1024], F32))
        PC = ctx.enter_context(nc.psum_tensor("PC", [128, 1024], F32))
        PD = ctx.enter_context(nc.psum_tensor("PD", [128, 1024], F32))

        sems = {}
        for sname in ("ts", "vs", "ss", "dq0", "dq1", "dg0", "dg1"):
            sems[sname] = ctx.enter_context(nc.semaphore(sname))
        p = Prog(nc, sems)

        r_x = Rsrc("x")
        r_xc = [Rsrc(f"xc{i}") for i in range(KC)]
        r_h = Rsrc("h")
        r_ao = Rsrc("ao")
        r_q = Rsrc("q")
        r_h2 = Rsrc("h2")
        r_k = Rsrc("k")
        r_vt = Rsrc("vt")
        r_probs = [Rsrc(f"probs{i}") for i in range(4)]
        r_hf = Rsrc("hf")
        r_PA, r_PB = Rsrc("PA"), Rsrc("PB")
        r_PC, r_PD = Rsrc("PC"), Rsrc("PD")
        r_wq, r_wk, r_wv, r_wo = Rsrc("wq"), Rsrc("wk"), Rsrc("wv"), Rsrc("wo")
        r_wg = [Rsrc("wg0"), Rsrc("wg1")]
        r_wu = [Rsrc("wu0"), Rsrc("wu1")]
        r_sg = [Rsrc("sg0"), Rsrc("sg1")]
        r_wd = [Rsrc("wd0"), Rsrc("wd1")]
        r_stg = [Rsrc("stg0"), Rsrc("stg1")]
        r_stgi = r_stg
        r_stg4 = [r_stg[0], r_stg[1], Rsrc("stg2"), Rsrc("stg3")]
        r_rowA, r_rowB = Rsrc("rowA"), Rsrc("rowB")
        r_mrow, r_rrow = Rsrc("m_bf"), Rsrc("rstd_bf")
        r_bias, r_bv, r_bvbs, r_rb = (Rsrc("bias"), Rsrc("bv"),
                                      Rsrc("bv_bs"), Rsrc("rb"))
        r_const = Rsrc("const")
        r_xtmp, r_sqtmp, r_t1tmp = Rsrc("xtmp"), Rsrc("sqtmp"), Rsrc("t1tmp")
        PS = {id(PA): r_PA, id(PB): r_PB, id(PC): r_PC, id(PD): r_PD}
        RS_MBS = [r_probs[0], r_probs[1]]
        RS_RBS = [r_probs[2], r_probs[3]]
        m_bs = probs_sb[:, 0:2048]
        rs_bs = probs_sb[:, 2048:4096]

        p.op("v", lambda e: e.memset(ones_row[:], 1.0), writes=[r_const])
        p.op("v", lambda e: e.memset(ones_col[:], 1.0), writes=[r_const])
        p.op("v", lambda e: e.memset(ones_f32[:], 1.0), writes=[r_const])
        p.op("v", lambda e: e.memset(att_rbf[:], 0.0), writes=[r_rowB])
        p.op("v", lambda e: e.memset(eps_c[:], float(EPS)), writes=[r_const])
        p.inc("v")

        xevs = []
        for c in range(KC):
            xevs.append(p.dma("g", x_sb[:, c * SEQ:(c + 1) * SEQ],
                              xT[c * 128:(c + 1) * 128, :]))
            r_xc[c].w = [xevs[-1]]
        r_x.w = xevs

        def bias_block(row_src, n_chunks, col0):
            p.dma("q", rowA[0:1, 0:row_src.shape[1]], row_src, writes=[r_rowA])
            for j in range(n_chunks):
                p.op("t", lambda e, j=j, c0=col0: e.matmul(
                    PA[:, c0 + j:c0 + j + 1],
                    rowA[0:1, j * 128:(j + 1) * 128],
                    ones_f32[:], start=True, stop=True),
                    reads=[r_rowA, r_const], writes=[r_PA])
            p.inc("t")
        bias_block(bq_r, 8, 0)
        bias_block(bk_r, 2, 8)
        bias_block(bo_r, 8, 10)
        p.op("s", lambda e: e.copy(bias_sb[:], PA[:, 0:18]),
             reads=[r_PA], writes=[r_bias])
        p.inc("s")

        p.dma("q", bv_row[:], bv_r, writes=[r_bv])
        p.op("v", lambda e: e.tensor_copy(bv_rbf[:], bv_row[:]),
             reads=[r_bv], writes=[r_bv])
        p.inc("v")
        p.op("t", lambda e: e.matmul(PB[:, 0:DKV], ones_row[0:1, 0:128],
                                     bv_rbf[:], start=True, stop=True),
             reads=[r_bv, r_const], writes=[r_PB])
        p.inc("t")
        p.op("s", lambda e: e.copy(bv_bs[:], PB[:, 0:DKV]),
             reads=[r_PB], writes=[r_bvbs])
        p.inc("s")

        def stream32(src_ap, dst_ap, rsrcs, half):
            st = stg32[:, half * 1024:(half + 1) * 1024]
            p.dma("q", st, src_ap, writes=[r_stg[half]])
            p.op("v", lambda e, st=st, d=dst_ap: e.tensor_copy(d, st),
                 reads=[r_stg[half]], writes=list(rsrcs))
            p.inc("v")

        def streami(src_ap, dst_ap, rsrcs, slot, queue="q", extra_dma=()):
            t2 = (stgi, stgi, stgi2, stgi2)[slot]
            st = t2[:, (slot % 2) * 1024:(slot % 2 + 1) * 1024]
            p.dma(queue, st, src_ap,
                  writes=[r_stg4[slot]] + list(extra_dma))
            p.op("v", lambda e, st=st, d=dst_ap: e.tensor_copy(d, st),
                 reads=[r_stg4[slot]], writes=rsrcs)
            p.inc("v")

        for i in range(2):
            stream32(wk_a[:, i * 1024:(i + 1) * 1024],
                     wk_bf[:, i * 1024:(i + 1) * 1024], [r_wk], i % 2)
        for i in range(2):
            stream32(wv_a[:, i * 1024:(i + 1) * 1024],
                     wv_bf[:, i * 1024:(i + 1) * 1024], [r_wv], i % 2)
        for i in range(8):
            stream32(wq_a[:, i * 1024:(i + 1) * 1024],
                     wq_bf[:, i * 1024:(i + 1) * 1024], [r_wq], i % 2)

        def layernorm(src_res, dst_sb, dst_out_res, extra_scr=()):
            xb = k_sb[:, 0:2048]
            sq = vt_sb[:, 0:2048]
            t1 = vt_sb[:, 2048:4096]
            w_xb = [r_xtmp] + [r for r in extra_scr if r.name == "k"]
            w_sq = [r_sqtmp] + [r for r in extra_scr if r.name == "vt"]
            w_t1 = [r_t1tmp] + [r for r in extra_scr if r.name == "vt"]
            for c in range(KC):
                xs = x_sb[:, c * SEQ:(c + 1) * SEQ]
                p.op("v", lambda e, xs=xs: e.tensor_copy(xb, xs),
                     reads=[src_res], writes=w_xb)
                p.inc("v")
                p.op("s", lambda e, xs=xs: e.activation(sq, xs, AF.Square),
                     reads=[src_res], writes=w_sq)
                p.inc("s")
                for piece in range(4):
                    ps = (PA, PA, PB, PB)[piece]
                    off = (piece % 2) * 512
                    p.op("t", lambda e, ps=ps, off=off, piece=piece, c=c: e.matmul(
                        ps[0:1, off:off + 512], ones_col[:],
                        xb[:, piece * 512:(piece + 1) * 512],
                        start=(c == 0), stop=(c == KC - 1)),
                        reads=[r_xtmp, r_const], writes=[PS[id(ps)]])
                for piece in range(4):
                    ps = (PC, PC, PD, PD)[piece]
                    off = (piece % 2) * 512
                    p.op("t", lambda e, ps=ps, off=off, piece=piece, c=c: e.matmul(
                        ps[0:1, off:off + 512], ones_col[:],
                        sq[:, piece * 512:(piece + 1) * 512],
                        start=(c == 0), stop=(c == KC - 1)),
                        reads=[r_sqtmp, r_const], writes=[PS[id(ps)]])
                p.inc("t")
            for Hh in range(2):
                sum_ps = (PA, PB)[Hh]
                sq_ps = (PC, PD)[Hh]
                p.op("v", lambda e, ps=sum_ps: e.tensor_scalar(
                    rowA[:], ps[0:1, :], 1.0 / DM, None, ALU.mult),
                    reads=[PS[id(sum_ps)]], writes=[r_rowA])
                p.op("v", lambda e: e.tensor_mul(rowB[:], rowA[:], rowA[:]),
                     reads=[r_rowA], writes=[r_rowB])
                p.inc("v")
                p.op("s", lambda e, Hh=Hh: e.copy(
                    m_bf[0:1, Hh * 1024:(Hh + 1) * 1024], rowA[:]),
                    reads=[r_rowA], writes=[r_mrow, r_k])
                p.inc("s")
                p.op("v", lambda e, ps=sq_ps: e.scalar_tensor_tensor(
                    rowA[:], ps[0:1, :], 1.0 / DM, rowB[:],
                    ALU.mult, ALU.subtract),
                    reads=[PS[id(sq_ps)], r_rowB], writes=[r_rowA])
                p.inc("v")
                p.op("s", lambda e: e.activation(rowB[:], rowA[:], AF.Ln,
                                                 bias=eps_c[0:1, :]),
                     reads=[r_rowA, r_const], writes=[r_rowB])
                p.op("s", lambda e, Hh=Hh: e.activation(
                    rstd_bf[0:1, Hh * 1024:(Hh + 1) * 1024], rowB[:],
                    AF.Exp, scale=-0.5),
                    reads=[r_rowB], writes=[r_rrow, r_vt])
                p.inc("s")
            for piece in range(4):
                ps = (PA, PA, PB, PB)[piece]
                off = (piece % 2) * 512
                p.op("t", lambda e, ps=ps, off=off, piece=piece: e.matmul(
                    ps[:, off:off + 512], ones_row[0:1, 0:128],
                    m_bf[0:1, piece * 512:(piece + 1) * 512],
                    start=True, stop=True),
                    reads=[r_mrow, r_k, r_const], writes=[PS[id(ps)]])
                ps2 = (PC, PC, PD, PD)[piece]
                p.op("t", lambda e, ps=ps2, off=off, piece=piece: e.matmul(
                    ps[:, off:off + 512], ones_row[0:1, 0:128],
                    rstd_bf[0:1, piece * 512:(piece + 1) * 512],
                    start=True, stop=True),
                    reads=[r_rrow, r_vt, r_const], writes=[PS[id(ps2)]])
            p.inc("t")
            p.op("s", lambda e: e.copy(m_bs[:, 0:1024], PA[:]),
                 reads=[r_PA], writes=RS_MBS)
            p.op("s", lambda e: e.copy(m_bs[:, 1024:2048], PB[:]),
                 reads=[r_PB], writes=RS_MBS)
            p.op("s", lambda e: e.copy(rs_bs[:, 0:1024], PC[:]),
                 reads=[r_PC], writes=RS_RBS)
            p.op("s", lambda e: e.copy(rs_bs[:, 1024:2048], PD[:]),
                 reads=[r_PD], writes=RS_RBS)
            p.inc("s")
            for c in range(KC):
                xs = x_sb[:, c * SEQ:(c + 1) * SEQ]
                p.op("v", lambda e, xs=xs: e.tensor_sub(t1, xs, m_bs),
                     reads=[src_res] + RS_MBS, writes=w_t1)
                p.op("v", lambda e, c=c: e.tensor_mul(
                    dst_sb[:, c * SEQ:(c + 1) * SEQ], t1, rs_bs),
                    reads=[r_t1tmp] + RS_RBS, writes=dst_out_res)
                p.inc("v")

        Prog.cur_phase = "ln1"
        layernorm(r_x, h_sb, [r_h])
        Prog.cur_phase = "qproj"

        for mc in range(KC):
            ps_pair = (PA, PB) if mc % 2 == 0 else (PC, PD)
            rs_pair = [PS[id(ps_pair[0])], PS[id(ps_pair[1])]]
            for kc in range(KC):
                for nf in range(4):
                    ps = ps_pair[nf // 2]
                    off = (nf % 2) * 512
                    p.op("t", lambda e, ps=ps, off=off, kc=kc, mc=mc, nf=nf:
                         e.matmul(
                             ps[:, off:off + 512],
                             wq_bf[:, kc * 1024 + mc * 128:
                                   kc * 1024 + mc * 128 + 128],
                             h_sb[:, kc * SEQ + nf * 512:
                                  kc * SEQ + nf * 512 + 512],
                             start=(kc == 0), stop=(kc == KC - 1)),
                         reads=[r_wq, r_h], writes=rs_pair)
            p.inc("t")
            for half in range(2):
                ps = ps_pair[half]
                p.op("s", lambda e, ps=ps, mc=mc, half=half: e.activation(
                    q_sb[:, mc * SEQ + half * 1024:
                         mc * SEQ + half * 1024 + 1024],
                    ps[:], AF.Identity, bias=bias_sb[:, mc:mc + 1]),
                    reads=[rs_pair[half], r_bias], writes=[r_q])
            p.inc("s")

        Prog.cur_phase = "kproj"
        for mc2 in range(2):
            ps_pair = (PA, PB) if mc2 % 2 == 0 else (PC, PD)
            rs_pair = [PS[id(ps_pair[0])], PS[id(ps_pair[1])]]
            for kc in range(KC):
                for nf in range(4):
                    ps = ps_pair[nf // 2]
                    off = (nf % 2) * 512
                    p.op("t", lambda e, ps=ps, off=off, kc=kc, mc2=mc2, nf=nf:
                         e.matmul(
                             ps[:, off:off + 512],
                             wk_bf[:, kc * 256 + mc2 * 128:
                                   kc * 256 + mc2 * 128 + 128],
                             h_sb[:, kc * SEQ + nf * 512:
                                  kc * SEQ + nf * 512 + 512],
                             start=(kc == 0), stop=(kc == KC - 1)),
                         reads=[r_wk, r_h], writes=rs_pair)
            p.inc("t")
            for half in range(2):
                ps = ps_pair[half]
                p.op("s", lambda e, ps=ps, mc2=mc2, half=half: e.activation(
                    k_sb[:, mc2 * SEQ + half * 1024:
                         mc2 * SEQ + half * 1024 + 1024],
                    ps[:], AF.Identity, bias=bias_sb[:, 8 + mc2:9 + mc2]),
                    reads=[rs_pair[half], r_bias], writes=[r_k])
            p.inc("s")

        Prog.cur_phase = "vtok"
        def z_init(e):
            e.memset(AP(vt_sb, 0, [[NSC * VT_W, 128], [VT_W, NSC],
                                   [192, 3], [1, 64]]), 0.0)
        p.op("v", z_init, writes=[r_vt, r_sqtmp, r_t1tmp])
        p.op("v", lambda e: e.memset(
            AP(vt_sb, 0, [[NSC * VT_W, 128], [VT_W, NSC], [192, 3]]), 1.0),
            writes=[r_vt])
        p.inc("v")

        for sc in range(NSC):
            ps = (PA, PB)[sc % 2]
            rp = PS[id(ps)]
            for kc in range(KC):
                p.op("t", lambda e, ps=ps, sc=sc, kc=kc: e.matmul(
                    ps[:, 0:DKV],
                    h_sb[:, kc * SEQ + sc * 128: kc * SEQ + sc * 128 + 128],
                    wv_bf[:, kc * 256:(kc + 1) * 256],
                    start=(kc == 0), stop=(kc == KC - 1)),
                    reads=[r_h, r_wv], writes=[rp])
            p.inc("t")
            p.op("v", lambda e, ps=ps, sc=sc: e.tensor_add(
                AP(vt_sb, sc * VT_W + 64, [[NSC * VT_W, 128], [192, 2], [1, 128]]),
                ps[:, 0:DKV].rearrange("p (s c) -> p s c", s=2),
                bv_bs[:].rearrange("p (s c) -> p s c", s=2)),
                reads=[rp, r_bvbs], writes=[r_vt])
            p.inc("v")

        if "h1" in tap_aps:
            for c in range(KC):
                p.dma("g", tap_aps["h1"][c * 128:(c + 1) * 128, :],
                      h_sb[:, c * SEQ:(c + 1) * SEQ], reads=[r_h])
        if "q" in tap_aps:
            for c in range(KC):
                p.dma("g", tap_aps["q"][c * 128:(c + 1) * 128, :],
                      q_sb[:, c * SEQ:(c + 1) * SEQ], reads=[r_q])
        if "k" in tap_aps:
            for c in range(2):
                p.dma("g", tap_aps["k"][c * 128:(c + 1) * 128, :],
                      k_sb[:, c * SEQ:(c + 1) * SEQ], reads=[r_k])
        if "vt" in tap_aps:
            p.dma("g", tap_aps["vt"], vt_sb[:], reads=[r_vt])

        for i in range(8):
            stream32(wo_a[:, i * 1024:(i + 1) * 1024],
                     wq_bf[:, i * 1024:(i + 1) * 1024],
                     [r_wo, r_wq] if i < 2 else [r_wo], i % 2)

        Prog.cur_phase = "att"
        ev_exp_tail = None
        ev_qk = {}
        for R in (r_PA, r_PB, r_PC, r_PD):
            for ev in R.w + R.r:
                p._wait("t", ev)
        for R in (r_k, r_q, r_vt):
            for ev in R.w:
                p._wait("t", ev)
        for ev in (r_rowA.w + r_rowA.r + r_rowB.w + r_rowB.r +
                   r_ao.w + r_ao.r + r_h.r + r_h.w):
            p._wait("v", ev)

        def ring_of(si):
            return (PA, PB, PD) if si % 2 == 0 else (PA, PB, PC)

        def avbuf_of(si):
            return PC if si % 2 == 0 else PD

        def t_drain_inc():
            val = p.count["ts"] + 1
            p.count["ts"] = val
            p.stream["t"].append(
                lambda eng, h2=p.sems["ts"]: eng.maybe_drain_then_inc(
                    (h2, 1), fusable=True))
            fcl = dict(p.floor["t"])
            fcl["ts"] = val
            p.evclock[("ts", val)] = fcl
            return ("ts", val)

        slots = [(c, qb) for c in range(8) for qb in range(4)]

        def emit_qk(slot_idx, g):
            c, qb = slots[slot_idx]
            ps = ring_of(slot_idx)[g % 3]
            for side, hh in ((0, A_HEADS[c]), (1, B_HEADS[c])):
                kc2 = (hh // 4) // 2
                p.raw("t", lambda e, ps=ps, sc=g, side=side, c=c, qb=qb,
                      kc2=kc2: e.matmul(
                          ps[:, side * 512:(side + 1) * 512],
                          k_sb[side * 64: side * 64 + 64,
                               kc2 * SEQ + sc * 128: kc2 * SEQ + sc * 128 + 128],
                          q_sb[side * 64: side * 64 + 64,
                               c * SEQ + qb * 512: c * SEQ + qb * 512 + 512],
                          start=True, stop=True))
            ev_qk[(slot_idx, g)] = p.raw_inc("t")

        ev_ave_par = {0: None, 1: None}
        ev_norm_final = None
        for si, (c, qb) in enumerate(slots):
            AVB = avbuf_of(si)
            soA = VT_STAT_OFF[A_HEADS[c] // 4]
            soB = VT_STAT_OFF[B_HEADS[c] // 4]
            ev_exp = {}
            ev_av = {}
            if si == 0:
                for g0 in (0, 1, 2):
                    emit_qk(0, g0)
            for g in range(NSC):
                lag = ev_qk.get((si, g + 1)) or ev_av[g - 2]
                p._wait("s", lag)
                ps = ring_of(si)[g % 3]
                pb = probs_sb[:, (g % 4) * 1024:(g % 4 + 1) * 1024]
                p.raw("s", lambda e, ps=ps, pb=pb: e.activation(
                    pb, ps[:], AF.Exp, scale=ATT_SCALE))
                ev_exp[g] = p.raw_inc("s")
                if g + 3 < NSC:
                    p._wait("t", ev_exp[g])
                    emit_qk(si, g + 3)
                p._wait("t", ev_exp[g])
                if g == 0 and ev_ave_par[si % 2] is not None:
                    p._wait("t", ev_ave_par[si % 2])
                for side, so in ((0, soA), (1, soB)):
                    p.raw("t", lambda e, g=g, side=side, so=so, AVB=AVB:
                          e.matmul(
                              AVB[:, side * 512:(side + 1) * 512],
                              vt_sb[:, g * VT_W + so: g * VT_W + so + 128],
                              probs_sb[:, (g % 4) * 1024 + side * 512:
                                       (g % 4) * 1024 + side * 512 + 512],
                              start=(g == 0), stop=(g == NSC - 1)))
                ev_av[g] = p.raw_inc("t")
            if si + 1 < len(slots):
                p._wait("t", ev_exp[NSC - 1])
                emit_qk(si + 1, 0)
                emit_qk(si + 1, 1)
                ev_lag_tail = ev_qk[(si + 1, 1)]
            else:
                ev_lag_tail = t_drain_inc()
            p._wait("v", ev_lag_tail)
            p.raw("v", lambda e, AVB=AVB: e.tensor_copy(av_f32[:], AVB[:]))
            ev_ave = p.raw_inc("v")
            ev_ave_par[si % 2] = ev_ave
            if si + 1 < len(slots):
                p._wait("t", ev_ave)
                emit_qk(si + 1, 2)
            p._wait("v", ev_ave)
            for side, hh in ((0, A_HEADS[c]), (1, B_HEADS[c])):
                cs = side * 512
                srow = 64 * (1 - side)
                drow = side * 64
                p.raw("v", lambda e, sr=srow, cs=cs: e.reciprocal(
                    av_f32[sr:sr + 1, cs:cs + 512],
                    av_f32[sr:sr + 1, cs:cs + 512]))
                p.raw("v", lambda e, sr=srow, cs=cs: e.tensor_copy(
                    att_rbf[sr:sr + 1, cs:cs + 512],
                    av_f32[sr:sr + 1, cs:cs + 512]))
                sq = 64 * (1 - side)
                p.raw("v", lambda e, sq=sq, dr=drow, cs=cs: e.stream_shuffle(
                    att_rbf[dr:dr + 64, cs:cs + 512],
                    att_rbf[sq:sq + 64, cs:cs + 512], [0] * 32))
                p.raw("v", lambda e, dr=drow, cs=cs: e.stream_shuffle(
                    att_rbf[dr + 32:dr + 64, cs:cs + 512],
                    att_rbf[dr:dr + 32, cs:cs + 512], [0] * 32))
                p.raw("v", lambda e, dr=drow, cs=cs, c=c, qb=qb: e.tensor_mul(
                    h_sb[dr:dr + 64,
                         c * SEQ + qb * 512: c * SEQ + qb * 512 + 512],
                    av_f32[dr:dr + 64, cs:cs + 512],
                    att_rbf[dr:dr + 64, cs:cs + 512]))
            ev_norm_final = p.raw_inc("v")
            ev_exp_tail = ev_exp[NSC - 1]

        ev_fin = ev_norm_final
        p._wait("t", ev_fin)
        for R in (r_PA, r_PB, r_PC, r_PD, r_rowA, r_rowB, r_rb):
            R.w = [ev_fin]
            R.r = []
            R.dirty_by = None
        r_ao.w = [ev_fin]
        r_ao.r = []
        r_h.w = [ev_fin]
        r_h.r = []
        for R in (r_q, r_k, r_vt):
            R.r = [ev_fin]
        for R in r_probs:
            R.w = [ev_fin]
            R.r = []

        if "ao" in tap_aps:
            for c in range(KC):
                p.dma("g", tap_aps["ao"][c * 128:(c + 1) * 128, :],
                      h_sb[:, c * SEQ:(c + 1) * SEQ], reads=[r_ao])

        Prog.cur_phase = "wo"

        def r2(tensor, coloff, width=1024):
            return AP(tensor, coloff, [[1024, 33], [1, width]])

        def wo_mc2(mc):
            for Hh in range(2):
                ps = (PA, PB)[Hh]
                rp = PS[id(ps)]
                for kc in range(KC):
                    for nf in range(2):
                        p.op("t", lambda e, ps=ps, kc=kc, mc=mc, nf=nf,
                             Hh=Hh: e.matmul(
                                 ps[:, nf * 512:(nf + 1) * 512],
                                 wq_bf[:, kc * 1024 + mc * 128:
                                       kc * 1024 + mc * 128 + 128],
                                 h_sb[:, kc * SEQ + Hh * 1024 + nf * 512:
                                      kc * SEQ + Hh * 1024 + nf * 512 + 512],
                                 start=(kc == 0), stop=(kc == KC - 1)),
                             reads=[r_wo, r_ao], writes=[rp])
                p.inc("t")
                xs = x_sb[:, mc * SEQ + Hh * 1024: mc * SEQ + Hh * 1024 + 1024]
                p.op("v", lambda e, ps=ps, xs=xs, mc=mc: e.scalar_tensor_tensor(
                    xs, ps[:], bias_sb[:, 10 + mc:11 + mc], xs,
                    ALU.add, ALU.add),
                    reads=[rp, r_bias, r_x, r_xc[mc]], writes=[r_xc[mc]])
                p.inc("v")

        def ln2_stats_chunk(c):
            xb = k_sb[:, 0:2048]
            sq = vt_sb[:, 0:2048]
            xs = x_sb[:, c * SEQ:(c + 1) * SEQ]
            p.op("v", lambda e, xs=xs: e.tensor_copy(xb, xs),
                 reads=[r_xc[c]], writes=[r_xtmp, r_k])
            p.inc("v")
            p.op("s", lambda e, xs=xs: e.activation(sq, xs, AF.Square),
                 reads=[r_xc[c]], writes=[r_sqtmp, r_vt])
            p.inc("s")
            for tb in range(4):
                row = (tb // 2) * 32
                col = (tb % 2) * 512
                p.op("t", lambda e, row=row, col=col, tb=tb, c=c: e.matmul(
                    PC[row:row + 1, col:col + 512], ones_col[:],
                    xb[:, tb * 512:(tb + 1) * 512],
                    start=(c == 0), stop=(c == KC - 1)),
                    reads=[r_xtmp, r_const], writes=[r_PC])
            for tb in range(4):
                row = (tb // 2) * 32
                col = (tb % 2) * 512
                p.op("t", lambda e, row=row, col=col, tb=tb, c=c: e.matmul(
                    PD[row:row + 1, col:col + 512], ones_col[:],
                    sq[:, tb * 512:(tb + 1) * 512],
                    start=(c == 0), stop=(c == KC - 1)),
                    reads=[r_sqtmp, r_const], writes=[r_PD])
            p.inc("t")

        p.op("v", lambda e: e.memset(PC[0:32, :], 0.0),
             reads=(), writes=[r_PC])
        p.op("v", lambda e: e.memset(PD[0:32, :], 0.0),
             reads=(), writes=[r_PD])
        p.inc("v")
        for c in range(KC):
            wo_mc2(c)
            if c >= 1:
                ln2_stats_chunk(c - 1)
        ln2_stats_chunk(KC - 1)

        Prog.cur_phase = "ln2"
        p.op("v", lambda e: e.tensor_scalar(
            r2(av_f32, 0), r2(PC, 0), 1.0 / DM, None, ALU.mult),
            reads=[r_PC], writes=[r_rowA])
        p.inc("v")
        p.op("s", lambda e: e.copy(r2(m_bf2, 0), r2(av_f32, 0)),
             reads=[r_rowA], writes=[r_mrow, r_k])
        p.inc("s")
        p.op("v", lambda e: e.tensor_mul(r2(av_f32, 0), r2(av_f32, 0),
                                         r2(av_f32, 0)),
             reads=[r_rowA], writes=[r_rowA])
        p.op("v", lambda e: e.scalar_tensor_tensor(
            r2(av_f32, 0), r2(PD, 0), 1.0 / DM, r2(av_f32, 0),
            ALU.mult, ALU.subtract),
            reads=[r_PD, r_rowA], writes=[r_rowA])
        p.inc("v")
        p.op("s", lambda e: e.activation(
            r2(av_f32, 0), r2(av_f32, 0), AF.Ln,
            bias=eps_c[0:33, :]),
            reads=[r_rowA, r_const], writes=[r_rowA])
        p.op("s", lambda e: e.activation(r2(rstd_bf2, 0), r2(av_f32, 0),
                                         AF.Exp, scale=-0.5),
             reads=[r_rowA], writes=[r_rrow, r_vt])
        p.inc("s")
        for tb in range(4):
            row = (tb // 2) * 32
            col = (tb % 2) * 512
            ps = PC if tb < 2 else PD
            p.op("t", lambda e, ps=ps, row=row, col=col: e.matmul(
                ps[:, col:col + 512], ones_row[row:row + 1, 0:128],
                AP(m_bf2, row * 1024 + col, [[1024, 1], [1, 512]]),
                start=True, stop=True),
                reads=[r_mrow, r_k, r_const],
                writes=[r_PC if tb < 2 else r_PD])
        p.inc("t")
        mb_s = probs_sb[:, 0:2048]
        rb_s = probs_sb[:, 2048:4096]
        p.op("s", lambda e: e.copy(mb_s[:, 0:1024], PC[:]),
             reads=[r_PC], writes=[r_probs[0], r_probs[1]])
        p.op("s", lambda e: e.copy(mb_s[:, 1024:2048], PD[:]),
             reads=[r_PD], writes=[r_probs[0], r_probs[1]])
        p.inc("s")
        for tb in range(4):
            row = (tb // 2) * 32
            col = (tb % 2) * 512
            ps = PC if tb < 2 else PD
            p.op("t", lambda e, ps=ps, row=row, col=col: e.matmul(
                ps[:, col:col + 512], ones_row[row:row + 1, 0:128],
                AP(rstd_bf2, row * 1024 + col, [[1024, 1], [1, 512]]),
                start=True, stop=True),
                reads=[r_rrow, r_vt, r_const],
                writes=[r_PC if tb < 2 else r_PD])
        p.inc("t")
        p.op("s", lambda e: e.copy(rb_s[:, 0:1024], PC[:]),
             reads=[r_PC], writes=[r_probs[2], r_probs[3]])
        p.op("s", lambda e: e.copy(rb_s[:, 1024:2048], PD[:]),
             reads=[r_PD], writes=[r_probs[2], r_probs[3]])
        p.inc("s")
        for c in range(KC):
            t1 = vt_sb[:, 2048:4096]
            xs = x_sb[:, c * SEQ:(c + 1) * SEQ]
            p.op("v", lambda e, xs=xs, t1=t1: e.tensor_sub(t1, xs, mb_s),
                 reads=[r_xc[c], r_probs[0], r_probs[1]],
                 writes=[r_t1tmp, r_vt])
            p.op("v", lambda e, c=c, t1=t1: e.tensor_mul(
                q_sb[:, c * SEQ:(c + 1) * SEQ], t1, rb_s),
                reads=[r_t1tmp, r_probs[2], r_probs[3]],
                writes=[r_h2, r_q])
            p.inc("v")
        Prog.cur_phase = "ffn"

        if "h2" in tap_aps:
            for c in range(KC):
                p.dma("g", tap_aps["h2"][c * 128:(c + 1) * 128, :],
                      q_sb[:, c * SEQ:(c + 1) * SEQ], reads=[r_h2])

        first_ffn = [r_ao, r_h, r_k, r_vt]

        def stream_gu(th, hc):
            half = hc % 2
            fw = [r_wo] if (th == 0 and hc < 2) else []
            fd = (([r_wo, r_wk, r_wv] if th == 0 else
                   [r_wd[0], r_wd[1]]) if hc < 4 else [])
            streami(wg_a[hc], wg_buf[:, half * 1024:(half + 1) * 1024],
                    [r_wg[half]] + fw, (0, 2)[hc % 2], queue="q",
                    extra_dma=fd)
            streami(wu_a[hc], wu_buf[:, half * 1024:(half + 1) * 1024],
                    [r_wu[half]] + fw, (1, 3)[hc % 2], queue="g",
                    extra_dma=fd)

        def stream_wd(th, mc):
            half = mc % 2
            fwd = [r_wo, r_wk, r_wv] if (th == 0 and mc < 2) else []
            for j in range(3):
                ovl = []
                byte0 = half * 6144 + j * 2048
                if byte0 < 4096:
                    ovl.append(r_stg4[2])
                if byte0 + 2048 > 4096 and byte0 < 8192:
                    ovl.append(r_stg4[3])
                streami(wd_a[mc][:, j * 1024:(j + 1) * 1024],
                        wd_buf[:, half * FH + j * 1024:
                              half * FH + j * 1024 + 1024],
                        [r_wd[half]] + fwd + ovl, j % 2,
                        queue="q" if j % 2 == 0 else "g")

        for th in range(2):
            stream_gu(th, 0)
            for hc in range(HC):
                half = hc % 2
                extra = first_ffn if (th == 0 and hc < 2) else []
                gps = (PA, PB)[half]
                ups = (PC, PD)[half]
                for kc in range(KC):
                    for nf in range(2):
                        p.op("t", lambda e, gps=gps, kc=kc, nf=nf, th=th,
                             half=half: e.matmul(
                                 gps[:, nf * 512:(nf + 1) * 512],
                                 wg_buf[:, half * 1024 + kc * 128:
                                       half * 1024 + kc * 128 + 128],
                                 q_sb[:, kc * SEQ + th * 1024 + nf * 512:
                                      kc * SEQ + th * 1024 + nf * 512 + 512],
                                 start=(kc == 0), stop=(kc == KC - 1)),
                             reads=[r_wg[half], r_h2], writes=[PS[id(gps)]])
                for kc in range(KC):
                    for nf in range(2):
                        p.op("t", lambda e, ups=ups, kc=kc, nf=nf, th=th,
                             half=half: e.matmul(
                                 ups[:, nf * 512:(nf + 1) * 512],
                                 wu_buf[:, half * 1024 + kc * 128:
                                       half * 1024 + kc * 128 + 128],
                                 q_sb[:, kc * SEQ + th * 1024 + nf * 512:
                                      kc * SEQ + th * 1024 + nf * 512 + 512],
                                 start=(kc == 0), stop=(kc == KC - 1)),
                             reads=[r_wu[half], r_h2], writes=[PS[id(ups)]])
                p.inc("t")
                p.op("s", lambda e, gps=gps, half=half: e.activation(
                    sg_buf[:, half * 1024:(half + 1) * 1024], gps[:],
                    AF.Silu, scale=DQS),
                    reads=[PS[id(gps)]], writes=[r_sg[half]])
                p.inc("s")
                if hc + 1 < HC:
                    stream_gu(th, hc + 1)
                p.op("v", lambda e, ups=ups, hc=hc, half=half: e.tensor_mul(
                    hf_sb[:, hc * 1024:(hc + 1) * 1024], ups[:],
                    sg_buf[:, half * 1024:(half + 1) * 1024]),
                    reads=[PS[id(ups)], r_sg[half]],
                    writes=[r_hf] + extra)
                p.inc("v")
            stream_wd(th, 0)
            for mc in range(KC):
                half = mc % 2
                dps = (PA, PB)[half]
                for hcc in range(HC):
                    for nf in range(2):
                        p.op("t", lambda e, dps=dps, hcc=hcc, nf=nf,
                             half=half: e.matmul(
                                 dps[:, nf * 512:(nf + 1) * 512],
                                 wd_buf[:, half * FH + hcc * 128:
                                       half * FH + hcc * 128 + 128],
                                 hf_sb[:, hcc * 1024 + nf * 512:
                                      hcc * 1024 + nf * 512 + 512],
                                 start=(hcc == 0), stop=(hcc == HC - 1)),
                             reads=[r_wd[half], r_hf], writes=[PS[id(dps)]])
                p.inc("t")
                if mc + 1 < KC:
                    stream_wd(th, mc + 1)
                xs = x_sb[:, mc * SEQ + th * 1024: mc * SEQ + th * 1024 + 1024]
                p.op("v", lambda e, dps=dps, xs=xs: e.scalar_tensor_tensor(
                    xs, dps[:], DQS * DQS, xs, ALU.mult, ALU.add),
                    reads=[PS[id(dps)], r_xc[mc]], writes=[r_xc[mc]])
                p.inc("v")

        for c in range(KC):
            p.dma("g", yT[c * 128:(c + 1) * 128, :],
                  x_sb[:, c * SEQ:(c + 1) * SEQ],
                  reads=[r_x, r_xc[c]])
        for s2 in ("dg0", "dg1", "dq0", "dq1"):
            p._wait("g", (s2, p.count[s2]))

        with nc.Block() as block:
            @block.tensor
            def _(eng):
                for fn in p.stream["t"]:
                    fn(eng)

            @block.vector
            def _(eng):
                for fn in p.stream["v"]:
                    fn(eng)

            @block.scalar
            def _(eng):
                for fn in p.stream["s"]:
                    fn(eng)

            @block.sync
            def _(eng):
                for fn in p.stream["q"]:
                    fn(eng)

            @block.gpsimd
            def _(eng):
                for fn in p.stream["g"]:
                    fn(eng)

    return nc


def prepare_inputs(inputs):
    x = np.asarray(inputs["L0_attn_x"])
    wq = np.asarray(inputs["L0_attn_wq"])
    bq = np.asarray(inputs["L0_attn_bq"])
    wk = np.asarray(inputs["L0_attn_wk"])
    bk = np.asarray(inputs["L0_attn_bk"])
    wv = np.asarray(inputs["L0_attn_wv"])
    bv = np.asarray(inputs["L0_attn_bv"])
    wo = np.asarray(inputs["L0_attn_wo"])
    bo = np.asarray(inputs["L0_attn_bo"])
    wg = np.asarray(inputs["L0_ffn_w_gate"])
    wu = np.asarray(inputs["L0_ffn_w_up"])
    wd = np.asarray(inputs["L0_ffn_w_down"])

    wq_c = wq[PERM_F]
    bq_c = bq[PERM_F]
    perm_kv = np.concatenate([np.arange(64) + k * 64 for k in VT_KV_ORDER])
    wv_c = wv[perm_kv]
    bv_c = bv[perm_kv]
    wo_c = wo[:, PERM_F]

    wq_al = np.ascontiguousarray(
        wq_c.reshape(KC, 128, KC, 128).transpose(3, 2, 0, 1)).reshape(128, KC * DM)
    wk_al = np.ascontiguousarray(
        wk.reshape(2, 128, KC, 128).transpose(3, 2, 0, 1)).reshape(128, KC * DKV)
    wv_al = np.ascontiguousarray(
        wv_c.reshape(DKV, KC, 128).transpose(2, 1, 0)).reshape(128, KC * DKV)
    wo_al = np.ascontiguousarray(
        wo_c.reshape(KC, 128, KC, 128).transpose(3, 2, 0, 1)).reshape(128, KC * DM)
    wg_al = np.ascontiguousarray(
        wg.reshape(HC, 128, KC, 128).transpose(0, 3, 2, 1)).reshape(HC, 128, DM)
    wu_al = np.ascontiguousarray(
        wu.reshape(HC, 128, KC, 128).transpose(0, 3, 2, 1)).reshape(HC, 128, DM)
    wd_al = np.ascontiguousarray(
        wd.reshape(KC, 128, HC, 128).transpose(0, 3, 2, 1)).reshape(KC, 128, FH)

    shared = {
        "wq_a": wq_al, "wk_a": wk_al, "wv_a": wv_al, "wo_a": wo_al,
        "bq_r": np.ascontiguousarray(bq_c.reshape(1, DM)),
        "bk_r": np.ascontiguousarray(bk.reshape(1, DKV)),
        "bv_r": np.ascontiguousarray(bv_c.reshape(1, DKV)),
        "bo_r": np.ascontiguousarray(bo.reshape(1, DM)),
        "wg_a": wg_al, "wu_a": wu_al, "wd_a": wd_al,
    }
    in_maps = []
    for i in range(8):
        m = dict(shared)
        m["xT"] = np.ascontiguousarray(x[i].T)
        in_maps.append(m)
    return in_maps


def kernel(**inputs):
    in_maps = prepare_inputs(inputs)
    nc = build_program()
    res = run_bass_kernel_spmd(nc, in_maps, core_ids=list(range(8)))
    out = np.stack([np.ascontiguousarray(np.asarray(r["yT"]).T)
                    for r in res.results], axis=0)
    return out.astype(np.float32, copy=False)
